# version 04
# baseline (speedup 1.0000x reference)
import numpy as np
import concourse.bass as bass
import concourse.mybir as mybir
from concourse.bass_utils import run_bass_kernel_spmd

F32 = mybir.dt.float32
BF16 = mybir.dt.bfloat16
U32 = mybir.dt.uint32
AF = mybir.ActivationFunctionType
ALU = mybir.AluOpType
AX = mybir.AxisListType.X

NB = 2
S = 2048
T = NB * S
D = 2048
NCH = D // 128
NHEAD = 16
HDIM = 128
NMEM = 4
NCHK = S // 128
NEXP = 8
HID = 1792
NJ = HID // 128
NT128 = T // 128
NT512 = T // 512
EPS_ATT = 1e-6
EPS_LN = 1e-5
ATT_SCALE = HDIM ** -0.5

ENGINES = ("PE", "DVE", "ACT", "POOL", "SP")


class Op:
    __slots__ = ("eng", "fn", "deps", "is_dma", "chan", "idx", "tok_val", "need_tok", "dbg")

    def __init__(self, eng, fn, deps, is_dma, chan, idx, dbg):
        self.eng = eng
        self.fn = fn
        self.deps = deps
        self.is_dma = is_dma
        self.chan = chan
        self.idx = idx
        self.tok_val = None
        self.need_tok = False
        self.dbg = dbg


class Prog:

    def __init__(self):
        self.ops = {e: [] for e in ENGINES}
        self.chan_count = {}

    def add(self, eng, fn, deps=(), dma=False, chan=None, dbg=""):
        deps = tuple(d for d in deps if d is not None)
        for d in deps:
            assert isinstance(d, Op)
        op = Op(eng, fn, deps, dma, chan, len(self.ops[eng]), dbg)
        self.ops[eng].append(op)
        if dma:
            assert chan is not None, f"dma needs a channel: {dbg}"
            self.chan_count[chan] = self.chan_count.get(chan, 0) + 1
            op.tok_val = self.chan_count[chan] * 16
        return op

    def pe(self, fn, deps=(), dbg=""):
        return self.add("PE", fn, deps, dbg=dbg)

    def v(self, fn, deps=(), dbg=""):
        return self.add("DVE", fn, deps, dbg=dbg)

    def a(self, fn, deps=(), dbg=""):
        return self.add("ACT", fn, deps, dbg=dbg)

    def pool(self, fn, deps=(), dbg=""):
        return self.add("POOL", fn, deps, dbg=dbg)

    def sp(self, fn, deps=(), dbg=""):
        return self.add("SP", fn, deps, dbg=dbg)

    def dma(self, queue, fn, deps=(), chan=None, dbg=""):
        return self.add(queue, fn, deps, dma=True, chan=chan or dbg, dbg=dbg)

    def finalize(self, nc, stack):
        for e in ENGINES:
            for op in self.ops[e]:
                for d in op.deps:
                    if not d.is_dma:
                        d.need_tok = True
        for e in ENGINES:
            cnt = 0
            for op in self.ops[e]:
                if not op.is_dma and op.need_tok:
                    cnt += 1
                    op.tok_val = cnt
        tok_sem = {e: stack.enter_context(nc.semaphore(f"tok_{e}")) for e in ENGINES}
        dma_sem = {c: stack.enter_context(nc.semaphore(f"dc_{c}"))
                   for c in self.chan_count}

        def emit_engine(eng_name, eng_obj):
            waited = {}
            for op in self.ops[eng_name]:
                waits = {}
                for d in op.deps:
                    sem = dma_sem[d.chan] if d.is_dma else tok_sem[d.eng]
                    key = id(sem)
                    if d.tok_val is None:
                        raise AssertionError(f"dep without token: {d.eng}#{d.idx} {d.dbg}")
                    if waits.get(key, (None, -1))[1] < d.tok_val:
                        waits[key] = (sem, d.tok_val)
                for key, (sem, val) in sorted(waits.items(), key=lambda kv: kv[1][1]):
                    if waited.get(key, -1) < val:
                        eng_obj.wait_ge(sem, val)
                        waited[key] = val
                ins = op.fn(eng_obj)
                if op.is_dma:
                    ins.then_inc(dma_sem[op.chan], 16)
                elif op.need_tok:
                    ins.then_inc(tok_sem[eng_name], 1)
        return emit_engine


def build_nc(debug=False, phases=(1, 2, 3, 4), p2_heads=NHEAD, p2_stage=9):
    nc = bass.Bass("TRN2", target_bir_lowering=False, debug=False)

    x_in = nc.dram_tensor("x", [T, D], F32, kind="ExternalInput").ap()
    w_q = nc.dram_tensor("w_q", [D, D], F32, kind="ExternalInput").ap()
    w_k = nc.dram_tensor("w_k", [D, D], F32, kind="ExternalInput").ap()
    w_v = nc.dram_tensor("w_v", [D, D], F32, kind="ExternalInput").ap()
    w_o = nc.dram_tensor("w_o", [D, D], F32, kind="ExternalInput").ap()
    mem_in = nc.dram_tensor("memories", [NMEM, NHEAD, HDIM, HDIM], F32, kind="ExternalInput").ap()
    mn_in = nc.dram_tensor("memory_norms", [NMEM, NHEAD, HDIM], F32, kind="ExternalInput").ap()
    gate_in = nc.dram_tensor("gate", [1, NHEAD], F32, kind="ExternalInput").ap()
    gw_in = nc.dram_tensor("gate_w", [D, NEXP], F32, kind="ExternalInput").ap()
    bias_in = nc.dram_tensor("expert_bias", [1, NEXP], F32, kind="ExternalInput").ap()
    w1_in = nc.dram_tensor("w1", [NEXP, D, HID], F32, kind="ExternalInput").ap()
    w2_in = nc.dram_tensor("w2", [NEXP, HID, D], F32, kind="ExternalInput").ap()
    w3_in = nc.dram_tensor("w3", [NEXP, D, HID], F32, kind="ExternalInput").ap()
    sw1_in = nc.dram_tensor("sw1", [D, HID], F32, kind="ExternalInput").ap()
    sw2_in = nc.dram_tensor("sw2", [HID, D], F32, kind="ExternalInput").ap()
    sw3_in = nc.dram_tensor("sw3", [D, HID], F32, kind="ExternalInput").ap()

    y_out = nc.dram_tensor("y", [T, D], F32, kind="ExternalOutput").ap()

    scratch_kind = "ExternalOutput" if debug else "Internal"
    xn_tm = nc.dram_tensor("xn_tm", [T, D], BF16, kind=scratch_kind).ap()
    xn_fm = nc.dram_tensor("xn_fm", [NCH, 128, T], BF16, kind=scratch_kind).ap()
    attn_tm = nc.dram_tensor("attn_tm", [T, D], BF16, kind=scratch_kind).ap()
    gT_dram = nc.dram_tensor("gT_dram", [NEXP, T], F32, kind=scratch_kind).ap()
    wq_bf = nc.dram_tensor("wq_bf", [D, D], BF16, kind="Internal").ap()
    wk_bf = nc.dram_tensor("wk_bf", [D, D], BF16, kind="Internal").ap()
    wv_bf = nc.dram_tensor("wv_bf", [D, D], BF16, kind="Internal").ap()
    wo_bf = nc.dram_tensor("wo_bf", [D, D], BF16, kind="Internal").ap()
    w2bf = [nc.dram_tensor(f"w2bf{i}", [HID, D], BF16, kind="Internal").ap() for i in range(2)]

    SBUF = lambda name, shape, dt=F32: nc.alloc_sbuf_tensor(name, shape, dt).ap()
    ident_bf = SBUF("ident_bf", [128, 128], BF16)
    ident_f = SBUF("ident_f", [128, 128], F32)
    tri_bf = SBUF("tri_bf", [128, 128], BF16)
    ones_bf = SBUF("ones_bf", [1, 128], BF16)
    ones_f = SBUF("ones_f", [1, 128], F32)
    g_res = SBUF("g_res", [128, NT128 * NEXP], F32)
    iota8 = SBUF("iota8", [128, NEXP], F32)
    bias_b = SBUF("bias_b", [128, NEXP], F32)
    gwf_sb = SBUF("gwf_sb", [128, NCH * NEXP], F32)
    ncsb = SBUF("ncsb", [128, NEXP], F32)
    xsum_bf = SBUF("xsum_bf", [128, NB * NCH], BF16)
    xsum_f = SBUF("xsum_f", [128, NB * NCH], F32)

    x_tile = [SBUF(f"x_tile{i}", [128, D]) for i in range(2)]
    sq_scr = SBUF("sq_scr", [128, D])
    xn_bf = [SBUF(f"xn_bf{i}", [128, D], BF16) for i in range(2)]
    stats = {k: [SBUF(f"st_{k}{i}", [128, 1]) for i in range(2)]
             for k in ("rsum", "ss", "mean", "ex2", "vneg", "va", "sq", "rq",
                       "nt", "s1", "rr", "racc", "nmr")}

    rt = {k: [SBUF(f"rt_{k}{i}", [128, NEXP], U32 if k == "ix" else F32) for i in range(2)]
          for k in ("ex", "sc", "b", "mx", "ix", "ixf", "m1", "m2",
                    "lgc", "u", "kf", "f", "pp", "q")}
    rt_ki = [SBUF(f"rt_ki{i}", [128, NEXP], mybir.dt.int32) for i in range(2)]
    rt_nmax = [SBUF(f"rt_nmax{i}", [128, 1]) for i in range(2)]
    rt_rsum = [SBUF(f"rt_rsum{i}", [128, 1]) for i in range(2)]
    rt_rden = [SBUF(f"rt_rden{i}", [128, 1]) for i in range(2)]
    gt_stage = [SBUF(f"gt_stage{i}", [NEXP, 128], F32) for i in range(2)]

    xn_sb = SBUF("xn_sb", [128, NCH * T], BF16)

    def OVERLAY(name, shape, dt, base_ap, byte_off=0):
        addr = nc.lookup_mloc(base_ap.tensor).addr + byte_off
        return nc.alloc_sbuf_tensor_at(name, shape, dt, offset=addr).ap()

    wT = {m: [SBUF(f"w{m}T{i}", [128, D], BF16) for i in range(2)] for m in "qkv"}
    allones_f = OVERLAY("allones_f", [128, 128], F32, wT["q"][0], 0)
    xt_sb = [OVERLAY(f"xt_sb{i}", [128, 128], F32, wT["q"][0], 512 + i * 512)
             for i in range(2)]
    mem_rhs = [SBUF(f"mem_rhs{i}", [128, NMEM * HDIM], BF16) for i in range(2)]
    mn_raw = [SBUF(f"mn_raw{i}", [NMEM, HDIM], BF16) for i in range(2)]
    mn_t = [SBUF(f"mn_t{i}", [128, NMEM], BF16) for i in range(2)]
    qp_sb = OVERLAY("qp_sb", [128, S], BF16, x_tile[0])
    kp_sb = OVERLAY("kp_sb", [128, S], BF16, x_tile[1])
    kpT_sb = OVERLAY("kpT_sb", [128, S], BF16, sq_scr)
    vfm_sb = OVERLAY("vfm_sb", [128, S], BF16, sq_scr, 4096)
    vc_sb = SBUF("vc_sb", [128, NCHK * (HDIM + 1)], BF16)
    elu_t1 = SBUF("elu_t1", [128, 512], F32)
    elu_t2 = SBUF("elu_t2", [128, 512], F32)
    elu_t3 = SBUF("elu_t3", [128, 512], F32)
    R_f = [SBUF(f"R_f{i}", [128, HDIM + 1], F32) for i in range(2)]
    R_bf = [SBUF(f"R_bf{i}", [128, HDIM + 1], BF16) for i in range(2)]
    sT_sb = SBUF("sT_sb", [128, 128], BF16)
    hb_sb = [OVERLAY(f"hb_sb{i}", [128, S], BF16, xn_bf[i]) for i in range(2)]
    mn_rows = OVERLAY("mn_rows", [NMEM, NHEAD * HDIM], BF16, qp_sb)
    macc = [SBUF(f"macc{i}", [128, 128], F32) for i in range(2)]
    cm64 = SBUF("cm64", [128, NCHK * NMEM], F32)
    dclip = SBUF("dclip", [128, 1])
    rden_sb = SBUF("rden_sb", [128, 1])
    rdeno = SBUF("rdeno", [128, 1])
    ho_f = SBUF("ho_f", [128, 128], F32)
    wgb_sb = SBUF("wgb_sb", [128, NMEM])
    omgb_sb = SBUF("omgb_sb", [128, 1])
    qbarT_bf = SBUF("qbarT_bf", [128, 1], BF16)
    mnsum = SBUF("mnsum", [NMEM, 1], F32)
    mnsum_t = SBUF("mnsum_t", [1, NMEM], F32)
    memmask = SBUF("memmask", [1, NMEM], F32)
    gate_t = SBUF("gate_t", [1, NHEAD], F32)
    gate_e = SBUF("gate_e", [1, NHEAD], F32)
    gate_g = SBUF("gate_g", [1, NHEAD], F32)
    gate_omg = SBUF("gate_omg", [1, NHEAD], F32)
    rel_s = SBUF("rel_s", [1, NMEM], F32)
    rel_nm = SBUF("rel_nm", [1, 1], F32)
    rel_e = SBUF("rel_e", [1, NMEM], F32)
    rel_den = SBUF("rel_den", [1, 1], F32)
    rel_rd = SBUF("rel_rd", [1, 1], F32)
    rel_w = SBUF("rel_w", [1, NMEM], F32)
    wg_row = SBUF("wg_row", [1, NMEM], F32)
    wg_bf = SBUF("wg_bf", [1, NMEM], BF16)
    omg_bf = SBUF("omg_bf", [1, 1], BF16)

    PS = [nc.alloc_psum_tensor(f"psum{i}", [128, 512], F32).ap() for i in range(8)]
    PS5B = PS[5].bitcast(BF16)

    P = Prog()

    t0 = P.pool(lambda g: g.memset(tri_bf[:], 1.0), dbg="tri0")
    tok_tri = P.pool(lambda g: g.affine_select(
        out=tri_bf[:], in_=tri_bf[:], compare_op=ALU.is_ge, fill=0.0,
        base=0, pattern=[[1, 128]], channel_multiplier=-1), deps=(t0,), dbg="tri")
    t0 = P.pool(lambda g: g.memset(ident_bf[:], 1.0), dbg="idb0")
    tok_id_bf = P.pool(lambda g: g.affine_select(
        out=ident_bf[:], in_=ident_bf[:], compare_op=ALU.is_equal, fill=0.0,
        base=0, pattern=[[1, 128]], channel_multiplier=-1), deps=(t0,), dbg="idb")
    t0 = P.pool(lambda g: g.memset(ident_f[:], 1.0), dbg="idf0")
    tok_id = P.pool(lambda g: g.affine_select(
        out=ident_f[:], in_=ident_f[:], compare_op=ALU.is_equal, fill=0.0,
        base=0, pattern=[[1, 128]], channel_multiplier=-1), deps=(t0,), dbg="idf")
    tok_ones_bf = P.pool(lambda g: g.memset(ones_bf[:], 1.0), dbg="ones_bf")
    tok_ones_f = P.pool(lambda g: g.memset(ones_f[:], 1.0), dbg="ones_f")
    tok_misc = P.pool(lambda g: g.iota(iota8[:], pattern=[[1, NEXP]], base=0,
                                       channel_multiplier=0,
                                       allow_small_or_imprecise_dtypes=True), dbg="iota")

    tok_bias = P.dma("SP", lambda s: s.dma_start(
        out=bias_b[:], in_=bias_in.to_broadcast([128, NEXP])), chan="bias", dbg="bias")

    tok_gwf = P.dma("POOL", lambda g: g.dma_start(
        out=gwf_sb.rearrange("p (c e) -> p c e", e=NEXP),
        in_=gw_in.rearrange("(c p) e -> p c e", p=128)), chan="gw", dbg="gwf")
    tok_ones128 = P.pool(lambda g: g.memset(allones_f[:], 1.0), dbg="ones128")

    def mk_csum():
        def f(t):
            ins = None
            for c in range(NCH):
                ins = t.matmul(PS[7][:, 0:NEXP], allones_f[:, :],
                               gwf_sb[:, c * NEXP:(c + 1) * NEXP],
                               start=(c == 0), stop=(c == NCH - 1))
            return ins
        return f
    tok_csmm = P.pe(mk_csum(), deps=(tok_gwf, tok_ones128), dbg="csmm")
    tok_ncsb = P.v(lambda v: v.tensor_scalar_mul(ncsb[:], PS[7][:, 0:NEXP], -1.0),
                   deps=(tok_csmm,), dbg="ncsb")

    LOG2E = 1.4426950408889634
    MAGIC = 12582912.0
    LN2_HI = 0.693359375
    LN2_LO = -2.12194440054690581e-4
    EXPC = [1.0 / 720, 1.0 / 120, 1.0 / 24, 1.0 / 6, 0.5, 1.0, 1.0]

    tok_xn_store = [None] * NT128
    tok_xnbf = [None] * NT128
    tok_ztr = [None] * NT128
    tok_lgs = [None] * NT128
    tok_g = [None] * NT128
    tok_g_store = [None] * NT128
    rt_last_read = [None, tok_ncsb]
    gt_last_evac = [None, None]
    zbank_free = [None, None]
    xt_free = [None, None]
    for i in range(NT128):
        p = i % 2
        ld_deps = (tok_xnbf[i - 2], tok_ztr[i - 2], tok_lgs[i - 2]) if i >= 2 else ()
        t_ld = P.dma("SP", (lambda i=i, p=p: (lambda s: s.dma_start(
            out=x_tile[p][:], in_=x_in[i * 128:(i + 1) * 128, :])))(),
            deps=ld_deps, chan=f"xl{p}", dbg=f"xld{i}")
        t_rsum = P.v((lambda p=p: (lambda v: v.reduce_sum(stats["rsum"][p][:], x_tile[p][:], axis=AX)))(),
                     deps=(t_ld,), dbg=f"rsum{i}")
        t_sq = P.a((lambda p=p: (lambda a: a.activation(sq_scr[:], x_tile[p][:], AF.Square,
                                                        accum_out=stats["ss"][p][:])))(),
                   deps=(t_ld,), dbg=f"sq{i}")
        t_mean = P.a((lambda p=p: (lambda a: a.activation(stats["mean"][p][:], stats["rsum"][p][:],
                                                          AF.Copy, scale=1.0 / D)))(),
                     deps=(t_rsum,), dbg=f"mean{i}")
        t_ex2 = P.a((lambda p=p: (lambda a: a.activation(stats["ex2"][p][:], stats["ss"][p][:],
                                                         AF.Copy, scale=1.0 / D)))(),
                    deps=(t_sq,), dbg=f"ex2{i}")
        t_vneg = P.v((lambda p=p: (lambda v: v.scalar_tensor_tensor(
            out=stats["vneg"][p][:], in0=stats["mean"][p][:], scalar=stats["mean"][p][:, 0:1],
            in1=stats["ex2"][p][:], op0=ALU.mult, op1=ALU.subtract)))(),
            deps=(t_mean, t_ex2), dbg=f"vneg{i}")
        t_va = P.v((lambda p=p: (lambda v: v.tensor_scalar(
            out=stats["va"][p][:], in0=stats["vneg"][p][:], scalar1=-1.0,
            scalar2=EPS_LN, op0=ALU.mult, op1=ALU.add)))(),
            deps=(t_vneg,), dbg=f"va{i}")
        t_sqn = P.a((lambda p=p: (lambda a: a.activation(stats["sq"][p][:],
            stats["va"][p][:], AF.Sqrt)))(), deps=(t_va,), dbg=f"sqn{i}")
        t_rq = P.v((lambda p=p: (lambda v: v.reciprocal(stats["rq"][p][:],
            stats["sq"][p][:])))(), deps=(t_sqn,), dbg=f"rq{i}")
        t_nt = P.v((lambda p=p: (lambda v: v.tensor_mul(stats["nt"][p][:],
            stats["va"][p][:], stats["rq"][p][:])))(), deps=(t_rq,), dbg=f"nt{i}")
        t_s1 = P.v((lambda p=p: (lambda v: v.tensor_add(stats["s1"][p][:],
            stats["sq"][p][:], stats["nt"][p][:])))(), deps=(t_nt,), dbg=f"s1{i}")
        t_rr = P.v((lambda p=p: (lambda v: v.reciprocal(stats["rr"][p][:],
            stats["s1"][p][:])))(), deps=(t_s1,), dbg=f"rr{i}")
        t_racc = P.v((lambda p=p: (lambda v: v.tensor_scalar_mul(stats["racc"][p][:],
            stats["rr"][p][:], 2.0)))(), deps=(t_rr,), dbg=f"racc{i}")
        t_nmr = P.v((lambda p=p: (lambda v: v.scalar_tensor_tensor(
            out=stats["nmr"][p][:], in0=stats["mean"][p][:], scalar=-1.0,
            in1=stats["racc"][p][:], op0=ALU.mult, op1=ALU.mult)))(),
            deps=(t_racc,), dbg=f"nmr{i}")
        t_xn = P.a((lambda p=p: (lambda a: a.activation(xn_bf[p][:], x_tile[p][:], AF.Identity,
                                                        scale=stats["racc"][p][:, 0:1],
                                                        bias=stats["nmr"][p][:, 0:1])))(),
                   deps=(t_nmr,) + ((tok_xn_store[i - 2],) if i >= 2 else ()), dbg=f"xn{i}")
        tok_xnbf[i] = t_xn
        tok_xn_store[i] = P.dma("ACT", (lambda i=i, p=p: (lambda a: a.dma_start(
            out=xn_tm[i * 128:(i + 1) * 128, :], in_=xn_bf[p][:])))(),
            deps=(t_xn,), chan=f"xnst{p}", dbg=f"xnst{i}")

        lg = PS[6 + p]
        tT = [None] * NCH
        tE = [None] * NCH
        tM = [None] * NCH

        def rec_T(c, i=i, p=p, t_ld=t_ld):
            b = c % 2
            deps = (t_ld, tok_id) + ((zbank_free[b],) if zbank_free[b] else ())
            tT[c] = P.pe((lambda c=c, p=p, b=b: (lambda t: t.transpose(
                PS[b][:, 0:128], x_tile[p][:, c * 128:(c + 1) * 128], ident_f[:])))(),
                deps=deps, dbg=f"zT{i}_{c}")

        def rec_E(c, i=i):
            b = c % 2
            deps = (tT[c],) + ((xt_free[b],) if xt_free[b] else ())
            tE[c] = P.v((lambda b=b: (lambda v: v.tensor_copy(
                xt_sb[b][:], PS[b][:, 0:128])))(), deps=deps, dbg=f"zE{i}_{c}")
            zbank_free[b] = tE[c]

        def rec_M(c, i=i, p=p, lg=lg):
            b = c % 2
            deps = (tE[c], tok_gwf)
            if c == 0 and rt_last_read[p] is not None:
                deps = deps + (rt_last_read[p],)
            tM[c] = P.pe((lambda c=c, b=b, lg=lg: (lambda t: t.matmul(
                lg[:, 0:NEXP], xt_sb[b][:], gwf_sb[:, c * NEXP:(c + 1) * NEXP],
                start=(c == 0), stop=(c == NCH - 1))))(), deps=deps, dbg=f"zM{i}_{c}")
            xt_free[b] = tM[c]

        rec_T(0); rec_E(0); rec_T(1); rec_E(1)
        for c in range(NCH):
            rec_M(c)
            if c + 2 < NCH:
                rec_T(c + 2)
                rec_E(c + 2)
        tok_ztr[i] = tT[NCH - 1]

        t_lgc = P.v((lambda p=p, lg=lg: (lambda v: v.scalar_tensor_tensor(
            out=rt["lgc"][p][:], in0=ncsb[:], scalar=stats["mean"][p][:, 0:1],
            in1=lg[:, 0:NEXP], op0=ALU.mult, op1=ALU.add)))(),
            deps=(tM[NCH - 1], t_mean, tok_ncsb), dbg=f"lgc{i}")
        t_lgs = P.v((lambda p=p: (lambda v: v.tensor_single_scalar(
            out=rt["lgc"][p][:], in_=rt["lgc"][p][:],
            scalar=stats["racc"][p][:, 0:1], op=ALU.mult)))(),
            deps=(t_lgc, t_racc), dbg=f"lgs{i}")
        tok_lgs[i] = t_lgs
        rt_last_read[p] = t_lgc

        t_nmax = P.v((lambda p=p: (lambda v: v.reduce_max(
            rt_nmax[p][:], rt["lgc"][p][:], axis=AX, negate=True)))(),
            deps=(t_lgs,), dbg=f"rtnmax{i}")
        t_u = P.v((lambda p=p: (lambda v: v.tensor_single_scalar(
            out=rt["u"][p][:], in_=rt["lgc"][p][:], scalar=rt_nmax[p][:, 0:1],
            op=ALU.add)))(), deps=(t_nmax,), dbg=f"rtu{i}")
        t_e1 = P.v((lambda p=p: (lambda v: v.tensor_scalar_mul(
            rt["kf"][p][:], rt["u"][p][:], LOG2E)))(), deps=(t_u,), dbg=f"e1_{i}")
        t_e2 = P.v((lambda p=p: (lambda v: v.tensor_scalar(
            out=rt["kf"][p][:], in0=rt["kf"][p][:], scalar1=MAGIC, scalar2=-MAGIC,
            op0=ALU.add, op1=ALU.add)))(), deps=(t_e1,), dbg=f"e2_{i}")
        t_e3 = P.v((lambda p=p: (lambda v: v.scalar_tensor_tensor(
            out=rt["f"][p][:], in0=rt["kf"][p][:], scalar=-LN2_HI,
            in1=rt["u"][p][:], op0=ALU.mult, op1=ALU.add)))(), deps=(t_e2,), dbg=f"e3_{i}")
        t_e4 = P.v((lambda p=p: (lambda v: v.scalar_tensor_tensor(
            out=rt["f"][p][:], in0=rt["kf"][p][:], scalar=-LN2_LO,
            in1=rt["f"][p][:], op0=ALU.mult, op1=ALU.add)))(), deps=(t_e3,), dbg=f"e4_{i}")
        t_prev = P.v((lambda p=p: (lambda v: v.tensor_scalar(
            out=rt["pp"][p][:], in0=rt["f"][p][:], scalar1=EXPC[0], scalar2=EXPC[1],
            op0=ALU.mult, op1=ALU.add)))(), deps=(t_e4,), dbg=f"ep0_{i}")
        for hi, cc in enumerate(EXPC[2:]):
            t_qm = P.v((lambda p=p: (lambda v: v.tensor_mul(
                rt["q"][p][:], rt["pp"][p][:], rt["f"][p][:])))(),
                deps=(t_prev,), dbg=f"eq{hi}_{i}")
            t_prev = P.v((lambda p=p, cc=cc: (lambda v: v.tensor_scalar_add(
                rt["pp"][p][:], rt["q"][p][:], cc)))(),
                deps=(t_qm,), dbg=f"ep{hi + 1}_{i}")
        t_ki = P.v((lambda p=p: (lambda v: v.tensor_copy(rt_ki[p][:], rt["kf"][p][:])))(),
                   deps=(t_e4,), dbg=f"eki{i}")
        t_k2 = P.v((lambda p=p: (lambda v: v.tensor_scalar_add(rt_ki[p][:], rt_ki[p][:], 127)))(),
                   deps=(t_ki,), dbg=f"ek2_{i}")
        t_k3 = P.v((lambda p=p: (lambda v: v.tensor_single_scalar(
            out=rt_ki[p][:], in_=rt_ki[p][:], scalar=23, op=ALU.logical_shift_left)))(),
            deps=(t_k2,), dbg=f"ek3_{i}")
        t_ex = P.v((lambda p=p: (lambda v: v.tensor_mul(
            rt["ex"][p][:], rt["pp"][p][:], rt_ki[p].bitcast(F32)[:])))(),
            deps=(t_prev, t_k3), dbg=f"rtex{i}")
        t_rs = P.v((lambda p=p: (lambda v: v.reduce_sum(rt_rsum[p][:], rt["ex"][p][:], axis=AX)))(),
                   deps=(t_ex,), dbg=f"rtrs{i}")
        t_rd = P.v((lambda p=p: (lambda v: v.reciprocal(rt_rden[p][:], rt_rsum[p][:])))(),
                   deps=(t_rs,), dbg=f"rtrd{i}")
        t_sc = P.v((lambda p=p: (lambda v: v.tensor_single_scalar(
            out=rt["sc"][p][:], in_=rt["ex"][p][:], scalar=rt_rden[p][:, 0:1], op=ALU.mult)))(),
            deps=(t_rd,), dbg=f"rtsc{i}")
        t_b = P.v((lambda p=p: (lambda v: v.tensor_add(rt["b"][p][:], rt["sc"][p][:], bias_b[:])))(),
                  deps=(t_sc, tok_bias), dbg=f"rtb{i}")
        t_mx = P.v((lambda p=p: (lambda v: v.max(out=rt["mx"][p][:], in_=rt["b"][p][:])))(),
                   deps=(t_b,), dbg=f"rtmx{i}")
        t_ix = P.v((lambda p=p: (lambda v: v.max_index(
            out=rt["ix"][p][:], in_max=rt["mx"][p][:], in_values=rt["b"][p][:])))(),
            deps=(t_mx,), dbg=f"rtix{i}")
        t_ixf = P.v((lambda p=p: (lambda v: v.tensor_copy(rt["ixf"][p][:], rt["ix"][p][:])))(),
                    deps=(t_ix,), dbg=f"rtixf{i}")
        t_m1 = P.v((lambda p=p: (lambda v: v.tensor_single_scalar(
            out=rt["m1"][p][:], in_=iota8[:], scalar=rt["ixf"][p][:, 0:1], op=ALU.is_equal)))(),
            deps=(t_ixf, tok_misc), dbg=f"rtm1_{i}")
        t_m2 = P.v((lambda p=p: (lambda v: v.tensor_single_scalar(
            out=rt["m2"][p][:], in_=iota8[:], scalar=rt["ixf"][p][:, 1:2], op=ALU.is_equal)))(),
            deps=(t_ixf,), dbg=f"rtm2_{i}")
        t_m12 = P.v((lambda p=p: (lambda v: v.tensor_add(rt["m1"][p][:], rt["m1"][p][:], rt["m2"][p][:])))(),
                    deps=(t_m1, t_m2), dbg=f"rtm12_{i}")
        tok_g[i] = P.v((lambda i=i, p=p: (lambda v: v.tensor_mul(
            g_res[:, i * NEXP:(i + 1) * NEXP], rt["m1"][p][:], rt["sc"][p][:])))(),
            deps=(t_m12,), dbg=f"rtg{i}")
        gt_ps = PS[6 + p][0:NEXP, 8:136]
        t_gt = P.pe((lambda i=i, gt_ps=gt_ps: (lambda t: t.transpose(
            gt_ps, g_res[:, i * NEXP:(i + 1) * NEXP], ident_f[:])))(),
            deps=(tok_g[i], tok_id) + ((gt_last_evac[p],) if gt_last_evac[p] is not None else ()),
            dbg=f"gtT{i}")
        t_gtev = P.v((lambda p=p, gt_ps=gt_ps: (lambda v: v.tensor_copy(gt_stage[p][:], gt_ps)))(),
                     deps=(t_gt,) + ((tok_g_store[i - 2],) if i >= 2 else ()), dbg=f"gtev{i}")
        gt_last_evac[p] = t_gtev
        rt_last_read[p] = t_gtev
        tok_g_store[i] = P.dma("ACT", (lambda i=i, p=p: (lambda a: a.dma_start(
            out=gT_dram[:, i * 128:(i + 1) * 128], in_=gt_stage[p][:])))(),
            deps=(t_gtev,), chan=f"gtst{p}", dbg=f"gtst{i}")

    tok_tr_b = [[None] * NCH for _ in range(NB)]
    tok_fmst = [None] * NCH
    for b in range(NB):
        half_dep = tuple(tok_xn_store[b * (NT128 // NB):(b + 1) * (NT128 // NB)])
        for c in range(NCH):
            tok_tr_b[b][c] = P.dma("SP", (lambda b=b, c=c: (lambda s: s.dma_start(
                out=xn_sb[:, c * T + b * S: c * T + (b + 1) * S],
                in_=xn_tm[b * S:(b + 1) * S, c * 128:(c + 1) * 128],
                transpose=True)))(),
                deps=half_dep, chan=f"tr{b}", dbg=f"tr{b}_{c}")
    tr_half = [tuple(tok_tr_b[b]) for b in range(NB)]
    tr_all = tr_half[0] + tr_half[1]
    for c in range(NCH):
        tok_fmst[c] = P.dma("ACT", (lambda c=c: (lambda a: a.dma_start(
            out=xn_fm[c, :, :], in_=xn_sb[:, c * T:(c + 1) * T])))(),
            deps=tr_all, chan="fmst", dbg=f"fmst{c}")

    tok_xsum = []
    for b in range(NB):
        for c in range(NCH):
            tok_xsum.append(P.v((lambda b=b, c=c: (lambda v: v.reduce_sum(
                xsum_f[:, (b * NCH + c):(b * NCH + c + 1)],
                xn_sb[:, c * T + b * S: c * T + (b + 1) * S], axis=AX)))(),
                deps=tr_half[b], dbg=f"xsum{b}_{c}"))
    tok_xsum_b = [None] * NB
    for b in range(NB):
        tok_xsum_b[b] = P.v((lambda b=b: (lambda v: v.tensor_scalar_mul(
            xsum_bf[:, b * NCH:(b + 1) * NCH], xsum_f[:, b * NCH:(b + 1) * NCH],
            1.0 / S)))(),
            deps=tuple(tok_xsum[b * NCH:(b + 1) * NCH]), dbg=f"xsum_bf{b}")
    tok_xsum_bf = tok_xsum_b[-1]

    tok_hb_store = []
    if 2 in phases:
        tok_wcast = {}
        for nm, src, dst in (("q", w_q, wq_bf), ("k", w_k, wk_bf), ("v", w_v, wv_bf),
                             ("o", w_o, wo_bf)):
            tok_wcast[nm] = P.dma("POOL", (lambda src=src, dst=dst: (
                lambda g: g.dma_start(out=dst[:, :], in_=src[:, :])))(),
                chan=f"wc{nm}", dbg=f"wcast{nm}")
        t_mnr = P.dma("POOL", lambda g: g.dma_start(
            out=mn_rows[:], in_=mn_in.rearrange("m h d -> m (h d)")),
            deps=(tok_ztr[NT128 - 2], tok_xnbf[NT128 - 2], tok_lgs[NT128 - 2]),
            chan="mnr", dbg="mnrows")
        t_mns = P.v(lambda v: v.reduce_sum(mnsum[:], mn_rows[:], axis=AX),
                    deps=(t_mnr,), dbg="mnsum")
        t_mnT = P.pe(lambda t: t.transpose(PS[7][0:1, 8:12], mnsum[:], ident_f[0:NMEM, 0:NMEM]),
                     deps=(t_mns, tok_id, rt_last_read[1]), dbg="mnT")
        t_mnev = P.v(lambda v: v.tensor_copy(mnsum_t[:], PS[7][0:1, 8:12]),
                     deps=(t_mnT,), dbg="mnev")
        t_mask = P.v(lambda v: v.tensor_single_scalar(out=memmask[:], in_=mnsum_t[:],
                                                      scalar=1e-6, op=ALU.is_ge),
                     deps=(t_mnev,), dbg="memmask")
        t_gld = P.dma("POOL", lambda g: g.dma_start(out=gate_t[:], in_=gate_in[:, :]),
                      chan="gld", dbg="gateld")
        t_ge = P.a(lambda a: a.activation(gate_e[:], gate_t[:], AF.Exp, scale=-1.0),
                   deps=(t_gld,), dbg="gate_e")
        t_g1 = P.v(lambda v: v.tensor_scalar_add(gate_e[:], gate_e[:], 1.0),
                   deps=(t_ge,), dbg="gate_1p")
        t_gg = P.v(lambda v: v.reciprocal(gate_g[:], gate_e[:]), deps=(t_g1,), dbg="gate_g")
        t_gomg = P.v(lambda v: v.tensor_scalar(out=gate_omg[:], in0=gate_g[:], scalar1=-1.0,
                                               scalar2=1.0, op0=ALU.mult, op1=ALU.add),
                     deps=(t_gg,), dbg="gate_omg")

        wT_free = [None, None]
        mem_free = [None, None]
        hb_free = [None, None]
        tok_mem = [None, None]
        tok_mnt = [None, None]
        tok_wT = [None, None]
        prep_chain = None
        bank67_free_init_mnev = True
        prev_pe_last = None
        bank01_free = [zbank_free[0], zbank_free[1]]
        bank23_free = [None, None]
        bank4_free = None
        bank5_free = None
        bank67_free = [rt_last_read[0], t_mnev]
        amv_sl = [elu_t1, elu_t2]
        amv_slot_free = [None, None]
        amv_free = None
        last_eluadd = None
        tok_vones = P.v(lambda v: v.memset(
            vc_sb.rearrange("p (n e) -> p n e", e=HDIM + 1)[:, :, HDIM:HDIM + 1], 1.0),
            dbg="vones")

        for h in range(p2_heads):
            hp = h % 2
            wdep = (wT_free[hp],)
            if h == 0:
                wdep = wdep + (xt_free[0], xt_free[1], tok_csmm)
            toks = []
            for nm in "qkv":
                for c in range(NCH):
                    toks.append(P.dma("SP", (lambda nm=nm, c=c, h=h, hp=hp: (lambda s: s.dma_start(
                        out=wT[nm][hp][:, c * 128:(c + 1) * 128],
                        in_=(wq_bf if nm == "q" else wk_bf if nm == "k" else wv_bf)[
                            h * 128:(h + 1) * 128, c * 128:(c + 1) * 128],
                        transpose=True)))(),
                        deps=wdep + (tok_wcast[nm],), chan=f"wT{hp}", dbg=f"wT{nm}{h}_{c}"))
            tok_wT[hp] = toks[-1]
            tok_mem[hp] = P.dma("POOL", (lambda h=h, hp=hp: (lambda g: g.dma_start(
                out=mem_rhs[hp].rearrange("p (m e) -> p m e", m=NMEM),
                in_=mem_in[:, h, :, :].rearrange("m d e -> d m e"))))(),
                deps=(mem_free[hp],), chan=f"mem{hp}", dbg=f"memld{h}")
            t_mnld = P.dma("POOL", (lambda h=h, hp=hp: (lambda g: g.dma_start(
                out=mn_raw[hp][:, :], in_=mn_in[:, h, :])))(),
                deps=(mem_free[hp],), chan=f"mnld{hp}", dbg=f"mnld{h}")
            stage0 = PS[0].bitcast(BF16)
            t_mntT = P.pe((lambda hp=hp, stage0=stage0: (lambda t: t.transpose(
                stage0[:, 896:896 + NMEM],
                mn_raw[hp][:, :], ident_bf[0:NMEM, 0:NMEM])))(),
                deps=(t_mnld, tok_id_bf, bank01_free[0]), dbg=f"mnT{h}")
            tok_mnt[hp] = P.a((lambda hp=hp, stage0=stage0: (lambda a: a.copy(
                mn_t[hp][:], stage0[:, 896:896 + NMEM])))(),
                deps=(t_mntT,), dbg=f"mnTev{h}")
            bank01_free[0] = tok_mnt[hp]

            if p2_stage < 2:
                continue
            for b in range(NB):
                hbp = (h * NB + b) % 2
                t_qbar = P.pe((lambda hp=hp, b=b: (lambda t: [
                    t.matmul(PS[5][:, 0:1], wT["q"][hp][:, c * 128:(c + 1) * 128],
                             xsum_bf[:, (b * NCH + c):(b * NCH + c + 1)],
                             start=(c == 0), stop=(c == NCH - 1))
                    for c in range(NCH)][-1]))(),
                    deps=tuple(d for d in (tok_wT[hp], tok_xsum_b[b], bank5_free) if d),
                    dbg=f"qbar{h}_{b}")
                t_qbev = P.v(lambda v: v.tensor_copy(qbarT_bf[:], PS[5][:, 0:1]),
                             deps=(t_qbar,), dbg=f"qbev{h}_{b}")
                t_rel = P.pe((lambda hp=hp: (lambda t: t.matmul(
                    PS[5][0:1, 1:5], qbarT_bf[:], mn_t[hp][:], start=True, stop=True)))(),
                    deps=(t_qbev, tok_mnt[hp]), dbg=f"rel{h}_{b}")
                t_rsc = P.v(lambda v: v.tensor_scalar_mul(rel_s[:], PS[5][0:1, 1:5], ATT_SCALE),
                            deps=(t_rel,), dbg=f"relsc{h}_{b}")
                t_rnm = P.v(lambda v: v.reduce_max(rel_nm[:], rel_s[:], axis=AX, negate=True),
                            deps=(t_rsc,), dbg=f"relnm{h}_{b}")
                t_rex = P.a(lambda a: a.activation(rel_e[:], rel_s[:], AF.Exp,
                                                   bias=rel_nm[0:1, 0:1]),
                            deps=(t_rnm,), dbg=f"relex{h}_{b}")
                t_rem = P.v(lambda v: v.tensor_mul(rel_e[:], rel_e[:], memmask[:]),
                            deps=(t_rex, t_mask), dbg=f"relem{h}_{b}")
                t_rds = P.v(lambda v: v.reduce_sum(rel_den[:], rel_e[:], axis=AX),
                            deps=(t_rem,), dbg=f"relds{h}_{b}")
                t_rdc = P.v(lambda v: v.tensor_scalar_max(rel_den[:], rel_den[:], 1e-30),
                            deps=(t_rds,), dbg=f"reldc{h}_{b}")
                t_rrd = P.v(lambda v: v.reciprocal(rel_rd[:], rel_den[:]),
                            deps=(t_rdc,), dbg=f"relrd{h}_{b}")
                t_rw = P.v(lambda v: v.tensor_single_scalar(out=rel_w[:], in_=rel_e[:],
                                                            scalar=rel_rd[0:1, 0:1], op=ALU.mult),
                           deps=(t_rrd,), dbg=f"relw{h}_{b}")
                t_wg = P.v((lambda h=h: (lambda v: v.tensor_single_scalar(
                    out=wg_row[:], in_=rel_w[:], scalar=gate_g[0:1, h:h + 1], op=ALU.mult)))(),
                    deps=(t_rw, t_gg), dbg=f"wgrow{h}_{b}")
                t_wgbf = P.a(lambda a: a.copy(wg_bf[:], wg_row[:]),
                             deps=(t_wg, prep_chain), dbg=f"wgbf{h}_{b}")
                t_ogbf = P.a((lambda h=h: (lambda a: a.copy(omg_bf[:], gate_omg[0:1, h:h + 1])))(),
                             deps=(t_gomg, prep_chain), dbg=f"ogbf{h}_{b}")
                t_wgmm = P.pe(lambda t: t.matmul(PS[5][:, 5:9], ones_bf[:], wg_bf[:],
                                                 start=True, stop=True),
                              deps=(t_wgbf, tok_ones_bf, t_rsc), dbg=f"wgmm{h}_{b}")
                t_wgbev = P.v(lambda v: v.tensor_copy(wgb_sb[:], PS[5][:, 5:9]),
                              deps=(t_wgmm,), dbg=f"wgbev{h}_{b}")
                t_ogmm = P.pe(lambda t: t.matmul(PS[5][:, 9:10], ones_bf[:], omg_bf[:],
                                                 start=True, stop=True),
                              deps=(t_ogbf, t_wgbev), dbg=f"ogmm{h}_{b}")
                t_ogbev = P.v(lambda v: v.tensor_copy(omgb_sb[:], PS[5][:, 9:10]),
                              deps=(t_ogmm,), dbg=f"ogbev{h}_{b}")
                bank5_free = t_ogbev
                if p2_stage < 3:
                    prep_chain = t_ogbev
                    continue

                tok_q = [None] * 4
                tok_k = [None] * 4
                tok_v = [None] * 4
                gi = 0
                for nm in "qkv":
                    for tt in range(4):
                        bank = PS[gi % 2]
                        deps_p = [tok_wT[hp], prev_pe_last, bank01_free[gi % 2]] + list(tr_half[b])
                        if h == 0 and b == 0:
                            deps_p.append(t_mns)
                        t_mm = P.pe((lambda nm=nm, hp=hp, b=b, tt=tt, bank=bank: (lambda t: [
                            t.matmul(bank[:, :],
                                     wT[nm][hp][:, c * 128:(c + 1) * 128],
                                     xn_sb[:, c * T + b * S + tt * 512: c * T + b * S + (tt + 1) * 512],
                                     start=(c == 0), stop=(c == NCH - 1))
                            for c in range(NCH)][-1]))(),
                            deps=tuple(d for d in deps_p if d is not None),
                            dbg=f"proj{nm}{h}_{b}_{tt}")
                        xn_last_reader = t_mm
                        if nm in "qk":
                            dst = qp_sb if nm == "q" else kp_sb
                            t_1 = P.a((lambda bank=bank: (lambda a: a.activation(
                                elu_t1[:], bank[:, :], AF.Relu, scale=-1.0)))(),
                                deps=(t_mm,), dbg=f"elu1{nm}{h}_{b}_{tt}")
                            t_2 = P.a(lambda a: a.activation(elu_t2[:], elu_t1[:], AF.Exp, scale=-1.0),
                                      deps=(t_1, last_eluadd), dbg=f"elu2{nm}{h}_{b}_{tt}")
                            t_3 = P.a((lambda bank=bank: (lambda a: a.activation(
                                elu_t3[:], bank[:, :], AF.Relu)))(),
                                deps=(t_2,), dbg=f"elu3{nm}{h}_{b}_{tt}")
                            t_add = P.v((lambda dst=dst, tt=tt: (lambda v: v.tensor_add(
                                dst[:, tt * 512:(tt + 1) * 512], elu_t2[:], elu_t3[:])))(),
                                deps=(t_3,), dbg=f"eluadd{nm}{h}_{b}_{tt}")
                            bank01_free[gi % 2] = t_3
                            last_eluadd = t_add
                            (tok_q if nm == "q" else tok_k)[tt] = t_add
                        else:
                            t_cp = P.a((lambda bank=bank, tt=tt: (lambda a: a.copy(
                                vfm_sb[:, tt * 512:(tt + 1) * 512], bank[:, :])))(),
                                deps=(t_mm,), dbg=f"vcp{h}_{b}_{tt}")
                            bank01_free[gi % 2] = t_cp
                            tok_v[tt] = t_cp
                        gi += 1

                if p2_stage < 4:
                    prev_pe_last = t_mm
                    wT_free[hp] = t_mm
                    continue
                tok_kpT = [None] * NCHK
                tok_vc = [None] * NCHK
                for n in range(NCHK):
                    sl = n % 2
                    st_ps = PS[sl].bitcast(BF16)[:, 896:1024]
                    t_T = P.pe((lambda n=n, st_ps=st_ps: (lambda t: t.transpose(
                        st_ps, kp_sb[:, n * 128:(n + 1) * 128], ident_bf[:])))(),
                        deps=(tok_k[n // 4], bank01_free[sl], tok_id_bf), dbg=f"kpT{h}_{b}_{n}")
                    t_ev = P.a((lambda n=n, st_ps=st_ps: (lambda a: a.copy(
                        kpT_sb[:, n * 128:(n + 1) * 128], st_ps)))(),
                        deps=(t_T, prev_pe_last), dbg=f"kpTev{h}_{b}_{n}")
                    bank01_free[sl] = t_ev
                    tok_kpT[n] = t_ev
                for n in range(NCHK):
                    sl = n % 2
                    st_ps = PS[sl].bitcast(BF16)[:, 896:1024]
                    t_T = P.pe((lambda n=n, st_ps=st_ps: (lambda t: t.transpose(
                        st_ps, vfm_sb[:, n * 128:(n + 1) * 128], ident_bf[:])))(),
                        deps=(tok_v[n // 4], bank01_free[sl], tok_id_bf), dbg=f"vT{h}_{b}_{n}")
                    t_ev = P.a((lambda n=n, st_ps=st_ps: (lambda a: a.copy(
                        vc_sb[:, n * 129:n * 129 + 128], st_ps)))(),
                        deps=(t_T, prev_pe_last), dbg=f"vTev{h}_{b}_{n}")
                    bank01_free[sl] = t_ev
                    tok_vc[n] = t_ev

                if p2_stage < 5:
                    prev_pe_last = t_T
                    wT_free[hp] = t_T
                    continue
                nm_toks = []
                for n in range(NCHK):
                    nm_toks.append(P.pe((lambda n=n, hp=hp: (lambda t: t.matmul(
                        PS[3][:, n * NMEM:(n + 1) * NMEM],
                        qp_sb[:, n * 128:(n + 1) * 128], mn_t[hp][:],
                        start=True, stop=True)))(),
                        deps=(tok_q[n // 4], tok_mnt[hp], bank23_free[1]),
                        dbg=f"nmb{h}_{b}_{n}"))
                t_ncl = P.v(lambda v: v.tensor_scalar_max(
                    cm64[:], PS[3][:, 0:NCHK * NMEM], EPS_ATT),
                    deps=tuple(nm_toks), dbg=f"nclipb{h}_{b}")
                bank23_free[1] = t_ncl
                t_cm = P.v(lambda v: v.reciprocal(cm64[:], cm64[:]),
                           deps=(t_ncl,), dbg=f"rnormb{h}_{b}")
                cm64v = cm64.rearrange("p (n m) -> p n m", m=NMEM)
                for m in range(NMEM):
                    t_cm = P.v((lambda m=m: (lambda v: v.tensor_single_scalar(
                        out=cm64v[:, :, m:m + 1], in_=cm64v[:, :, m:m + 1],
                        scalar=wgb_sb[:, m:m + 1], op=ALU.mult)))(),
                        deps=(t_cm, t_wgbev), dbg=f"cmb{h}_{b}_{m}")
                tok_cm64 = t_cm

                t_Rz = P.v(lambda v: v.memset(R_f[0][:], 0.0),
                           deps=(prev_pe_last,) if prev_pe_last else (), dbg=f"Rz{h}_{b}")
                t_Rbf = P.a(lambda a: a.copy(R_bf[0][:], R_f[0][:]),
                            deps=(t_Rz,) + ((prev_pe_last,) if prev_pe_last else ()),
                            dbg=f"Rbf{h}_{b}")
                tok_Rf = t_Rz
                tok_Rbf = t_Rbf

                last_ho_cast = None
                tok_ho = [None] * NCHK
                tail = [None] * NCHK
                tail_prev = None

                def rec_tail(n, h=h, b=b, hbp=hbp):
                    nonlocal last_ho_cast, tail_prev
                    t_nd, t_am, t_Ru = tail[n]
                    ndb = PS[6 + n % 2]
                    amb = PS[2 + n % 2]
                    t_dc = P.v((lambda ndb=ndb: (lambda v: v.tensor_scalar_max(
                        dclip[:], ndb[:, HDIM:HDIM + 1], EPS_ATT)))(),
                        deps=(t_nd, tail_prev), dbg=f"dclip{h}_{b}_{n}")
                    t_rd2 = P.v(lambda v: v.reciprocal(rden_sb[:], dclip[:]), deps=(t_dc,),
                                dbg=f"rden{h}_{b}_{n}")
                    t_rdo = P.v(lambda v: v.tensor_mul(rdeno[:], rden_sb[:], omgb_sb[:]),
                                deps=(t_rd2, t_ogbev), dbg=f"rdeno{h}_{b}_{n}")
                    t_mc = P.v((lambda n=n, amb=amb: (lambda v: v.tensor_single_scalar(
                        out=macc[0][:], in_=amb[:, 0:128],
                        scalar=cm64[:, n * NMEM:n * NMEM + 1], op=ALU.mult)))(),
                        deps=(t_am, tok_cm64, tail_prev), dbg=f"mc0_{h}_{b}_{n}")
                    for m in range(1, NMEM):
                        t_mc = P.v((lambda m=m, n=n, amb=amb: (lambda v: v.scalar_tensor_tensor(
                            out=macc[m % 2][:], in0=amb[:, m * 128:(m + 1) * 128],
                            scalar=cm64[:, n * NMEM + m:n * NMEM + m + 1], in1=macc[(m - 1) % 2][:],
                            op0=ALU.mult, op1=ALU.add)))(),
                            deps=(t_mc,), dbg=f"mc{m}_{h}_{b}_{n}")
                    bank23_free[n % 2] = t_mc
                    t_ho = P.v((lambda ndb=ndb: (lambda v: v.scalar_tensor_tensor(
                        out=ho_f[:], in0=ndb[:, 0:HDIM], scalar=rdeno[:, 0:1],
                        in1=macc[(NMEM - 1) % 2][:], op0=ALU.mult, op1=ALU.add)))(),
                        deps=(t_mc, t_rdo, last_ho_cast), dbg=f"ho{h}_{b}_{n}")
                    bank67_free[n % 2] = t_ho
                    t_hc = P.a((lambda n=n, hbp=hbp: (lambda a: a.copy(
                        hb_sb[hbp][:, n * 128:(n + 1) * 128], ho_f[:])))(),
                        deps=(t_ho, hb_free[hbp], last_ho_cast), dbg=f"hocast{h}_{b}_{n}")
                    last_ho_cast = t_hc
                    tail_prev = t_ho
                    tok_ho[n] = t_hc

                for n in range(NCHK):
                    rp = n % 2
                    t_kv = P.pe((lambda n=n: (lambda t: t.matmul(
                        PS[4][:, 0:HDIM + 1], kpT_sb[:, n * 128:(n + 1) * 128],
                        vc_sb[:, n * 129:(n + 1) * 129], start=True, stop=True)))(),
                        deps=(tok_kpT[n], tok_vc[n], bank4_free, tok_vones),
                        dbg=f"kv{h}_{b}_{n}")
                    t_sc = P.pe((lambda n=n: (lambda t: t.matmul(
                        PS[5][:, 0:128], kp_sb[:, n * 128:(n + 1) * 128],
                        qp_sb[:, n * 128:(n + 1) * 128], start=True, stop=True)))(),
                        deps=(tok_k[n // 4], tok_q[n // 4], bank5_free), dbg=f"sc{h}_{b}_{n}")
                    t_sT = P.v((lambda: (lambda v: v.tensor_mul(sT_sb[:], PS[5][:, 0:128], tri_bf[:])))(),
                               deps=(t_sc, tok_tri), dbg=f"sT{h}_{b}_{n}")
                    bank5_free = t_sT
                    t_Ru = P.v((lambda rp=rp: (lambda v: v.tensor_add(
                        R_f[1 - rp][:], R_f[rp][:], PS[4][:, 0:HDIM + 1])))(),
                        deps=(t_kv, tok_Rf), dbg=f"Ru{h}_{b}_{n}")
                    bank4_free = t_Ru
                    tok_Rf = t_Ru
                    t_nd = P.pe((lambda n=n, rp=rp, tok_Rbf=tok_Rbf: (lambda t: (
                        t.matmul(PS[6 + n % 2][:, 0:HDIM + 1], sT_sb[:],
                                 vc_sb[:, n * 129:(n + 1) * 129], start=True, stop=False),
                        t.matmul(PS[6 + n % 2][:, 0:HDIM + 1], qp_sb[:, n * 128:(n + 1) * 128],
                                 R_bf[rp][:], start=False, stop=True))[-1]))(),
                        deps=(t_sT, tok_Rbf, bank67_free[n % 2]), dbg=f"nd{h}_{b}_{n}")
                    t_am = P.pe((lambda n=n, hp=hp: (lambda t: t.matmul(
                        PS[2 + n % 2][:, :], qp_sb[:, n * 128:(n + 1) * 128], mem_rhs[hp][:],
                        start=True, stop=True)))(),
                        deps=(tok_q[n // 4], tok_mem[hp], bank23_free[n % 2]), dbg=f"am{h}_{b}_{n}")
                    pe_chunk_last = t_am
                    if n < NCHK - 1:
                        tok_Rbf = P.a((lambda rp=rp: (lambda a: a.copy(
                            R_bf[1 - rp][:], R_f[1 - rp][:])))(),
                            deps=(t_Ru, t_nd), dbg=f"Rbfc{h}_{b}_{n}")
                    tail[n] = (t_nd, t_am, t_Ru)
                    if n >= 1:
                        rec_tail(n - 1)
                rec_tail(NCHK - 1)
                prep_chain = tail_prev
                prev_pe_last = pe_chunk_last
                wT_free[hp] = pe_chunk_last
                mem_free[hp] = pe_chunk_last
                hb_free[hbp] = P.dma("POOL", (lambda h=h, b=b, hbp=hbp: (lambda g: g.dma_start(
                    out=attn_tm[b * S:(b + 1) * S, h * 128:(h + 1) * 128].rearrange(
                        "(n c) e -> c n e", c=128),
                    in_=hb_sb[hbp].rearrange("c (n e) -> c n e", e=128))))(),
                    deps=tuple(tok_ho), chan=f"hbst{hbp}", dbg=f"hbst{h}_{b}")
                tok_hb_store.append(hb_free[hbp])

    tok_y_store = []
    if 3 in phases:
        woT_sb = OVERLAY("woT_sb", [128, NCH * D], BF16, xn_sb)
        afm_sb = [OVERLAY(f"afm_sb{i}", [128, NCH * 512], BF16, xn_sb, 65536 + i * 16384)
                  for i in range(2)]
        x_res = [OVERLAY(f"x_res{i}", [128, D], F32, xn_sb, 98304 + i * 8192)
                 for i in range(2)]
        y_stage = [OVERLAY(f"y_stage{i}", [128, 512], F32, xn_sb, 114688 + i * 2048)
                   for i in range(2)]
        p3dep = (xn_last_reader,)

        tok_woT = None
        for c in range(NCH):
            for eb in range(NCH):
                tok_woT = P.dma("SP", (lambda c=c, eb=eb: (lambda sy: sy.dma_start(
                    out=woT_sb[:, c * D + eb * 128: c * D + (eb + 1) * 128],
                    in_=wo_bf[eb * 128:(eb + 1) * 128, c * 128:(c + 1) * 128],
                    transpose=True)))(),
                    deps=p3dep + (tok_wcast["o"],), chan="woT", dbg=f"woT{c}_{eb}")

        afm_free = [None, None]
        xres_free = [None, None]
        yps_free = [bank23_free[0], bank23_free[1]]
        yst_free = [None, None]
        tok_afm = [None, None]
        tok_xres = [None, None]
        for i5 in range(NT512):
            par = i5 % 2
            b = (i5 * 512) // S
            tok_afm[par] = None
            afm_toks = []
            for c in range(NCH):
                afm_toks.append(P.dma("SP", (lambda i5=i5, c=c, par=par: (lambda sy: sy.dma_start(
                    out=afm_sb[par][:, c * 512:(c + 1) * 512],
                    in_=attn_tm[i5 * 512:(i5 + 1) * 512, c * 128:(c + 1) * 128],
                    transpose=True)))(),
                    deps=(tok_hb_store[c * NB + b], afm_free[par]), chan=f"afm{par}",
                    dbg=f"afm{i5}_{c}"))
            tok_afm[par] = afm_toks[-1]
            for it in range(4):
                i = i5 * 4 + it
                xp = i % 2
                tok_xres[xp] = P.dma("ACT", (lambda i=i, xp=xp: (lambda a: a.dma_start(
                    out=x_res[xp][:], in_=x_in[i * 128:(i + 1) * 128, :])))(),
                    deps=(xres_free[xp],) + p3dep, chan=f"xres{xp}", dbg=f"xres{i}")
                for eb in range(4):
                    u = (i * 4 + eb) % 2
                    bank = PS[2 + u]
                    t_mm = P.pe((lambda it=it, eb=eb, par=par, bank=bank: (lambda t: [
                        t.matmul(bank[:, :],
                                 afm_sb[par][:, c * 512 + it * 128: c * 512 + it * 128 + 128],
                                 woT_sb[:, c * D + eb * 512:(c) * D + (eb + 1) * 512],
                                 start=(c == 0), stop=(c == NCH - 1))
                        for c in range(NCH)][-1]))(),
                        deps=(tok_afm[par], tok_woT, yps_free[u]), dbg=f"ymm{i}_{eb}")
                    afm_free[par] = t_mm
                    sp2 = (i * 4 + eb) % 2
                    t_ev = P.v((lambda eb=eb, xp=xp, sp2=sp2, bank=bank: (lambda v: v.tensor_add(
                        y_stage[sp2][:], bank[:, :], x_res[xp][:, eb * 512:(eb + 1) * 512])))(),
                        deps=(t_mm, tok_xres[xp], yst_free[sp2]), dbg=f"yev{i}_{eb}")
                    yps_free[u] = t_ev
                    xres_free[xp] = t_ev
                    yst_free[sp2] = P.dma("ACT", (lambda i=i, eb=eb, sp2=sp2: (lambda a: a.dma_start(
                        out=y_out[i * 128:(i + 1) * 128, eb * 512:(eb + 1) * 512],
                        in_=y_stage[sp2][:])))(),
                        deps=(t_ev,), chan=f"yst{sp2}", dbg=f"yst{i}_{eb}")
                    tok_y_store.append(yst_free[sp2])

    tok_y_final = []
    if 4 in phases:
        w1b = OVERLAY("w1b", [128, NCH * HID], BF16, xn_sb, 0)
        w3b = OVERLAY("w3b", [128, NCH * HID], BF16, xn_sb, NCH * HID * 2)
        xq_sb = OVERLAY("xq_sb", [128, NCH * 512], BF16, xn_sb, 2 * NCH * HID * 2)
        H_sb = OVERLAY("H_sb", [128, NJ * 512], BF16, wT["q"][0], 0)
        w2str = [OVERLAY(f"w2str{i}", [128, NJ * 512], BF16, x_tile[0], i * NJ * 512 * 2)
                 for i in range(2)]
        gb_sb = OVERLAY("gb_sb", [128, 512], F32, elu_t1, 0)
        a_sb = OVERLAY("a_sb", [128, 512], F32, elu_t2, 0)
        s_sb = OVERLAY("s_sb", [128, 512], F32, elu_t3, 0)
        yin_sb = [OVERLAY(f"yin_sb{i}", [128, 512], F32, vc_sb, i * 2048) for i in range(2)] + \
                 [OVERLAY(f"yin_sb{2 + i}", [128, 512], F32, wT["v"][0], i * 2048) for i in range(2)]
        yst4_sb = [OVERLAY(f"yst4_sb{i}", [128, 512], F32, hb_sb[1], i * 2048) for i in range(2)] + \
                  [OVERLAY(f"yst4_sb{2 + i}", [128, 512], F32, wT["v"][1], i * 2048) for i in range(2)]
        grow_sb = OVERLAY("grow_sb", [1, 512], F32, mem_rhs[0], 0)

        p4gate = tuple(d for d in (
            prev_pe_last, last_eluadd, tok_hb_store[-1], tok_hb_store[-2],
            afm_free[0], afm_free[1], xres_free[0], xres_free[1],
            yst_free[0], yst_free[1], yps_free[0], yps_free[1]) if d is not None)

        ytile = [[tok_y_store[i * 4 + eb] for eb in range(4)] for i in range(NT128)]

        u1free = [bank01_free[0], bank01_free[1]]
        u3free = [yps_free[0], yps_free[1]]
        ybfree = [bank4_free, bank5_free, yps_free[0], yps_free[1]]
        gb_free = bank67_free[0]
        yin_free = [None, None, None, None]
        yst4_free = [None, None, None, None]
        YB4 = [PS[4], PS[5], PS[2], PS[3]]
        w2s_free = [None, None]
        w2c_free = [None, None]
        u_last = None
        w2_last = None
        gbmm_last = None
        prev_s = None
        prev_a = None
        prev_H = None

        for e in range(NEXP + 1):
            routed = e < NEXP
            w1src = w1_in[e] if routed else sw1_in
            w3src = w3_in[e] if routed else sw3_in
            w2src = w2_in[e] if routed else sw2_in
            par = e % 2
            cast_dep = (u_last,) if u_last is not None else p4gate
            t_w1c = P.dma("POOL", (lambda w1src=w1src: (lambda g: g.dma_start(
                out=w1b.rearrange("p (c j) -> p c j", j=HID),
                in_=w1src.rearrange("(c p) j -> p c j", p=128))))(),
                deps=cast_dep + ((tok_wcast["o"],) if e == 0 else ()), chan="w1b", dbg=f"w1c{e}")
            t_w3c = P.dma("POOL", (lambda w3src=w3src: (lambda g: g.dma_start(
                out=w3b.rearrange("p (c j) -> p c j", j=HID),
                in_=w3src.rearrange("(c p) j -> p c j", p=128))))(),
                deps=cast_dep, chan="w3b", dbg=f"w3c{e}")
            t_w2c = P.dma("POOL", (lambda w2src=w2src, par=par: (lambda g: g.dma_start(
                out=w2bf[par][:, :], in_=w2src[:, :])))(),
                deps=(w2c_free[par],) if w2c_free[par] is not None else p4gate,
                chan=f"w2c{par}", dbg=f"w2c{e}")
            for t5 in range(NT512):
                t_xq = P.dma("SP", (lambda t5=t5: (lambda sy: sy.dma_start(
                    out=xq_sb.rearrange("p (c w) -> p c w", w=512),
                    in_=xn_fm[:, :, t5 * 512:(t5 + 1) * 512].rearrange("c p w -> p c w"))))(),
                    deps=((u_last,) if u_last is not None else p4gate) + (tok_fmst[-1],),
                    chan="xq", dbg=f"xq{e}_{t5}")
                if routed:
                    t_gr = P.dma("SP", (lambda e=e, t5=t5: (lambda sy: sy.dma_start(
                        out=grow_sb[:], in_=gT_dram[e:e + 1, t5 * 512:(t5 + 1) * 512])))(),
                        deps=((gbmm_last,) if gbmm_last is not None else p4gate) +
                             (tok_g_store[-1], tok_g_store[-2]),
                        chan="grow", dbg=f"grow{e}_{t5}")
                    t_gbmm = P.pe((lambda: (lambda t: t.matmul(
                        PS[6][:, :], ones_f[:], grow_sb[:], start=True, stop=True)))(),
                        deps=(t_gr, gb_free, tok_ones_f), dbg=f"gbmm{e}_{t5}")
                    gbmm_last = t_gbmm
                    t_gb = P.v(lambda v: v.tensor_copy(gb_sb[:], PS[6][:, :]),
                               deps=(t_gbmm, prev_a), dbg=f"gbev{e}_{t5}")
                    gb_free = t_gb
                else:
                    t_gb = None
                u_toks = []
                for jc in range(NJ):
                    b1 = PS[jc % 2]
                    b3 = PS[2 + jc % 2]
                    t_u1 = P.pe((lambda jc=jc, b1=b1: (lambda t: [
                        t.matmul(b1[:, :],
                                 w1b[:, c * HID + jc * 128: c * HID + (jc + 1) * 128],
                                 xq_sb[:, c * 512:(c + 1) * 512],
                                 start=(c == 0), stop=(c == NCH - 1))
                        for c in range(NCH)][-1]))(),
                        deps=(t_w1c, t_xq, u1free[jc % 2]), dbg=f"u1_{e}_{t5}_{jc}")
                    t_u3 = P.pe((lambda jc=jc, b3=b3: (lambda t: [
                        t.matmul(b3[:, :],
                                 w3b[:, c * HID + jc * 128: c * HID + (jc + 1) * 128],
                                 xq_sb[:, c * 512:(c + 1) * 512],
                                 start=(c == 0), stop=(c == NCH - 1))
                        for c in range(NCH)][-1]))(),
                        deps=(t_w3c, t_xq, u3free[jc % 2]), dbg=f"u3_{e}_{t5}_{jc}")
                    u_last = t_u3
                    if routed:
                        t_a = P.v((lambda b1=b1: (lambda v: v.tensor_mul(a_sb[:], b1[:, :], gb_sb[:])))(),
                                  deps=(t_u1, t_gb, prev_s), dbg=f"a{e}_{t5}_{jc}")
                        prev_a = t_a
                        t_s = P.a(lambda a: a.activation(s_sb[:], a_sb[:], AF.Silu),
                                  deps=(t_a, prev_H), dbg=f"s{e}_{t5}_{jc}")
                        u1free[jc % 2] = t_a
                    else:
                        t_s = P.a((lambda b1=b1: (lambda a: a.activation(s_sb[:], b1[:, :], AF.Silu)))(),
                                  deps=(t_u1, prev_H), dbg=f"s{e}_{t5}_{jc}")
                        u1free[jc % 2] = t_s
                    prev_s = t_s
                    t_H = P.v((lambda jc=jc, b3=b3: (lambda v: v.tensor_mul(
                        H_sb[:, jc * 512:(jc + 1) * 512], s_sb[:], b3[:, :])))(),
                        deps=(t_s, t_u3) + ((w2_last,) if jc == 0 and w2_last is not None else ()),
                        dbg=f"H{e}_{t5}_{jc}")
                    prev_H = t_H
                    u3free[jc % 2] = t_H
                    u_toks.append(t_H)
                for eb in range(4):
                    sp = eb % 2
                    t_w2s = P.dma("SP", (lambda par=par, eb=eb, sp=sp: (lambda sy: sy.dma_start(
                        out=w2str[sp].rearrange("p (c w) -> p c w", w=512),
                        in_=w2bf[par][:, eb * 512:(eb + 1) * 512].rearrange("(c p) w -> p c w", p=128))))(),
                        deps=(t_w2c,) + ((w2s_free[sp],) if w2s_free[sp] is not None else p4gate),
                        chan=f"w2s{sp}", dbg=f"w2s{e}_{t5}_{eb}")
                    for ts in range(4):
                        i = t5 * 4 + ts
                        u = (eb * 4 + ts) % 4
                        ybank = YB4[u]
                        t_ymm = P.pe((lambda ts=ts, sp=sp, ybank=ybank: (lambda t: [
                            t.matmul(ybank[:, :],
                                     H_sb[:, jc2 * 512 + ts * 128: jc2 * 512 + ts * 128 + 128],
                                     w2str[sp][:, jc2 * 512:(jc2 + 1) * 512],
                                     start=(jc2 == 0), stop=(jc2 == NJ - 1))
                            for jc2 in range(NJ)][-1]))(),
                            deps=(u_toks[-1], u_toks[-2], t_w2s, ybfree[u]),
                            dbg=f"ymm{e}_{t5}_{eb}_{ts}")
                        w2_last = t_ymm
                        w2s_free[sp] = t_ymm
                        yp = (eb * 4 + ts) % 4
                        t_yin = P.dma("SP", (lambda i=i, eb=eb, yp=yp: (lambda a: a.dma_start(
                            out=yin_sb[yp][:], in_=y_out[i * 128:(i + 1) * 128,
                                                        eb * 512:(eb + 1) * 512])))(),
                            deps=(ytile[i][eb],) + ((yin_free[yp],) if yin_free[yp] is not None else ()),
                            chan=f"yin{yp}", dbg=f"yin{e}_{i}_{eb}")
                        sc_ap = (lambda i=i, e=e: g_res[:, i * NEXP + e:i * NEXP + e + 1])() \
                            if routed else 1.0
                        t_yev = P.v((lambda ybank=ybank, yp=yp, sc_ap=sc_ap: (lambda v: v.scalar_tensor_tensor(
                            out=yst4_sb[yp][:], in0=ybank[:, :], scalar=sc_ap,
                            in1=yin_sb[yp][:], op0=ALU.mult, op1=ALU.add)))(),
                            deps=(t_ymm, t_yin) +
                                 ((yst4_free[yp],) if yst4_free[yp] is not None else ()) +
                                 ((tok_g[i],) if routed else ()),
                            dbg=f"yev{e}_{i}_{eb}")
                        ybfree[u] = t_yev
                        yin_free[yp] = t_yev
                        t_yst = P.dma("SP", (lambda i=i, eb=eb, yp=yp: (lambda a: a.dma_start(
                            out=y_out[i * 128:(i + 1) * 128, eb * 512:(eb + 1) * 512],
                            in_=yst4_sb[yp][:])))(),
                            deps=(t_yev,), chan=f"yo{yp}", dbg=f"yst{e}_{i}_{eb}")
                        yst4_free[yp] = t_yst
                        ytile[i][eb] = t_yst
                u3free = [ybfree[2], ybfree[3]]
            w2c_free[par] = w2_last
        for row in ytile:
            tok_y_final.extend(row)

    final_deps = [tok_fmst[c] for c in range(NCH)] + [tok_g_store[i] for i in range(NT128)]
    final_deps += list(tok_xsum_b) + [tok_tri] + tok_hb_store + tok_y_store + tok_y_final
    P.sp(lambda s: s.nop(), deps=tuple(final_deps), dbg="final")

    from contextlib import ExitStack
    with ExitStack() as stack:
        emit = P.finalize(nc, stack)
        with nc.Block() as block:
            @block.tensor
            def _(t):
                emit("PE", t)

            @block.vector
            def _(v):
                emit("DVE", v)

            @block.scalar
            def _(a):
                emit("ACT", a)

            @block.gpsimd
            def _(g):
                emit("POOL", g)

            @block.sync
            def _(s):
                emit("SP", s)
    return nc



def kernel(**inputs):
    ncores = 8
    x = np.ascontiguousarray(inputs["L0_attn_hidden_states"])
    nc = build_nc()
    in_maps = []
    for i in range(ncores):
        shard = np.ascontiguousarray(x[2 * i:2 * i + 2].reshape(T, D))
        in_maps.append({
            "x": shard,
            "w_q": inputs["L0_attn_w_q"], "w_k": inputs["L0_attn_w_k"],
            "w_v": inputs["L0_attn_w_v"], "w_o": inputs["L0_attn_w_o"],
            "memories": inputs["L0_attn_memories"],
            "memory_norms": inputs["L0_attn_memory_norms"],
            "gate": inputs["L0_attn_gate"].reshape(1, NHEAD),
            "gate_w": inputs["L0_ffn_gate_w"],
            "expert_bias": inputs["L0_ffn_expert_bias"].reshape(1, NEXP),
            "w1": inputs["L0_ffn_w1"], "w2": inputs["L0_ffn_w2"], "w3": inputs["L0_ffn_w3"],
            "sw1": inputs["L0_ffn_sw1"], "sw2": inputs["L0_ffn_sw2"], "sw3": inputs["L0_ffn_sw3"],
        })
    res = run_bass_kernel_spmd(nc, in_maps, core_ids=list(range(ncores)))
    out = np.concatenate([r["y"].reshape(2, S, D) for r in res.results], axis=0)
    return out


# version 05
# speedup vs baseline: 1.2542x; 1.2542x over previous
import numpy as np
import concourse.bass as bass
import concourse.mybir as mybir
from concourse.bass_utils import run_bass_kernel_spmd

F32 = mybir.dt.float32
BF16 = mybir.dt.bfloat16
U32 = mybir.dt.uint32
AF = mybir.ActivationFunctionType
ALU = mybir.AluOpType
AX = mybir.AxisListType.X

NB = 2
S = 2048
T = NB * S
D = 2048
NCH = D // 128
NHEAD = 16
HDIM = 128
NMEM = 4
NCHK = S // 128
NEXP = 8
HID = 1792
NJ = HID // 128
NT128 = T // 128
NT512 = T // 512
EPS_ATT = 1e-6
EPS_LN = 1e-5
ATT_SCALE = HDIM ** -0.5

ENGINES = ("PE", "DVE", "ACT", "POOL", "SP")


class Op:
    __slots__ = ("eng", "fn", "deps", "is_dma", "chan", "idx", "tok_val", "need_tok", "dbg")

    def __init__(self, eng, fn, deps, is_dma, chan, idx, dbg):
        self.eng = eng
        self.fn = fn
        self.deps = deps
        self.is_dma = is_dma
        self.chan = chan
        self.idx = idx
        self.tok_val = None
        self.need_tok = False
        self.dbg = dbg


class Prog:

    def __init__(self):
        self.ops = {e: [] for e in ENGINES}
        self.chan_count = {}

    def add(self, eng, fn, deps=(), dma=False, chan=None, dbg=""):
        deps = tuple(d for d in deps if d is not None)
        for d in deps:
            assert isinstance(d, Op)
        op = Op(eng, fn, deps, dma, chan, len(self.ops[eng]), dbg)
        self.ops[eng].append(op)
        if dma:
            assert chan is not None, f"dma needs a channel: {dbg}"
            self.chan_count[chan] = self.chan_count.get(chan, 0) + 1
            op.tok_val = self.chan_count[chan] * 16
        return op

    def pe(self, fn, deps=(), dbg=""):
        return self.add("PE", fn, deps, dbg=dbg)

    def v(self, fn, deps=(), dbg=""):
        return self.add("DVE", fn, deps, dbg=dbg)

    def a(self, fn, deps=(), dbg=""):
        return self.add("ACT", fn, deps, dbg=dbg)

    def pool(self, fn, deps=(), dbg=""):
        return self.add("POOL", fn, deps, dbg=dbg)

    def sp(self, fn, deps=(), dbg=""):
        return self.add("SP", fn, deps, dbg=dbg)

    def dma(self, queue, fn, deps=(), chan=None, dbg=""):
        return self.add(queue, fn, deps, dma=True, chan=chan or dbg, dbg=dbg)

    def finalize(self, nc, stack):
        for e in ENGINES:
            for op in self.ops[e]:
                for d in op.deps:
                    if not d.is_dma:
                        d.need_tok = True
        for e in ENGINES:
            cnt = 0
            for op in self.ops[e]:
                if not op.is_dma and op.need_tok:
                    cnt += 1
                    op.tok_val = cnt
        tok_sem = {e: stack.enter_context(nc.semaphore(f"tok_{e}")) for e in ENGINES}
        dma_sem = {c: stack.enter_context(nc.semaphore(f"dc_{c}"))
                   for c in self.chan_count}

        def emit_engine(eng_name, eng_obj):
            waited = {}
            for op in self.ops[eng_name]:
                waits = {}
                for d in op.deps:
                    sem = dma_sem[d.chan] if d.is_dma else tok_sem[d.eng]
                    key = id(sem)
                    if d.tok_val is None:
                        raise AssertionError(f"dep without token: {d.eng}#{d.idx} {d.dbg}")
                    if waits.get(key, (None, -1))[1] < d.tok_val:
                        waits[key] = (sem, d.tok_val)
                for key, (sem, val) in sorted(waits.items(), key=lambda kv: kv[1][1]):
                    if waited.get(key, -1) < val:
                        eng_obj.wait_ge(sem, val)
                        waited[key] = val
                ins = op.fn(eng_obj)
                if op.is_dma:
                    ins.then_inc(dma_sem[op.chan], 16)
                elif op.need_tok:
                    ins.then_inc(tok_sem[eng_name], 1)
        return emit_engine


def build_nc(debug=False, phases=(1, 2, 3, 4), p2_heads=NHEAD, p2_stage=9):
    nc = bass.Bass("TRN2", target_bir_lowering=False, debug=False)

    x_in = nc.dram_tensor("x", [T, D], F32, kind="ExternalInput").ap()
    w_q = nc.dram_tensor("w_q", [D, D], F32, kind="ExternalInput").ap()
    w_k = nc.dram_tensor("w_k", [D, D], F32, kind="ExternalInput").ap()
    w_v = nc.dram_tensor("w_v", [D, D], F32, kind="ExternalInput").ap()
    w_o = nc.dram_tensor("w_o", [D, D], F32, kind="ExternalInput").ap()
    mem_in = nc.dram_tensor("memories", [NMEM, NHEAD, HDIM, HDIM], F32, kind="ExternalInput").ap()
    mn_in = nc.dram_tensor("memory_norms", [NMEM, NHEAD, HDIM], F32, kind="ExternalInput").ap()
    gate_in = nc.dram_tensor("gate", [1, NHEAD], F32, kind="ExternalInput").ap()
    gw_in = nc.dram_tensor("gate_w", [D, NEXP], F32, kind="ExternalInput").ap()
    bias_in = nc.dram_tensor("expert_bias", [1, NEXP], F32, kind="ExternalInput").ap()
    w1_in = nc.dram_tensor("w1", [NEXP, D, HID], F32, kind="ExternalInput").ap()
    w2_in = nc.dram_tensor("w2", [NEXP, HID, D], F32, kind="ExternalInput").ap()
    w3_in = nc.dram_tensor("w3", [NEXP, D, HID], F32, kind="ExternalInput").ap()
    sw1_in = nc.dram_tensor("sw1", [D, HID], F32, kind="ExternalInput").ap()
    sw2_in = nc.dram_tensor("sw2", [HID, D], F32, kind="ExternalInput").ap()
    sw3_in = nc.dram_tensor("sw3", [D, HID], F32, kind="ExternalInput").ap()

    y_out = nc.dram_tensor("y", [T, D], F32, kind="ExternalOutput").ap()

    scratch_kind = "ExternalOutput" if debug else "Internal"
    xn_tm = nc.dram_tensor("xn_tm", [T, D], BF16, kind=scratch_kind).ap()
    xn_fm = nc.dram_tensor("xn_fm", [NCH, 128, T], BF16, kind=scratch_kind).ap()
    attn_tm = nc.dram_tensor("attn_tm", [T, D], BF16, kind=scratch_kind).ap()
    gT_dram = nc.dram_tensor("gT_dram", [NEXP, T], F32, kind=scratch_kind).ap()
    wq_bf = nc.dram_tensor("wq_bf", [D, D], BF16, kind="Internal").ap()
    wk_bf = nc.dram_tensor("wk_bf", [D, D], BF16, kind="Internal").ap()
    wv_bf = nc.dram_tensor("wv_bf", [D, D], BF16, kind="Internal").ap()
    wo_bf = nc.dram_tensor("wo_bf", [D, D], BF16, kind="Internal").ap()
    w1bf9 = nc.dram_tensor("w1bf9", [NEXP + 1, D, HID], BF16, kind="Internal").ap()
    w3bf9 = nc.dram_tensor("w3bf9", [NEXP + 1, D, HID], BF16, kind="Internal").ap()
    w2bf9 = nc.dram_tensor("w2bf9", [NEXP + 1, HID, D], BF16, kind="Internal").ap()

    SBUF = lambda name, shape, dt=F32: nc.alloc_sbuf_tensor(name, shape, dt).ap()
    ident_bf = SBUF("ident_bf", [128, 128], BF16)
    ident_f = SBUF("ident_f", [128, 128], F32)
    tri_bf = SBUF("tri_bf", [128, 128], BF16)
    ones_bf = SBUF("ones_bf", [1, 128], BF16)
    ones_f = SBUF("ones_f", [1, 128], F32)
    g_res = SBUF("g_res", [128, NT128 * NEXP], F32)
    iota8 = SBUF("iota8", [128, NEXP], F32)
    bias_b = SBUF("bias_b", [128, NEXP], F32)
    gwf_sb = SBUF("gwf_sb", [128, NCH * NEXP], F32)
    ncsb = SBUF("ncsb", [128, NEXP], F32)
    xsum_bf = SBUF("xsum_bf", [128, NB * NCH], BF16)
    xsum_f = SBUF("xsum_f", [128, NB * NCH], F32)

    x_tile = [SBUF(f"x_tile{i}", [128, D]) for i in range(2)]
    sq_scr = SBUF("sq_scr", [128, D])
    xn_bf = [SBUF(f"xn_bf{i}", [128, D], BF16) for i in range(2)]
    stats = {k: [SBUF(f"st_{k}{i}", [128, 1]) for i in range(2)]
             for k in ("rsum", "ss", "mean", "ex2", "vneg", "va", "sq", "rq",
                       "nt", "s1", "rr", "racc", "nmr")}

    rt = {k: [SBUF(f"rt_{k}{i}", [128, NEXP], U32 if k == "ix" else F32) for i in range(2)]
          for k in ("ex", "sc", "b", "mx", "ix", "ixf", "m1", "m2",
                    "lgc", "u", "kf", "f", "pp", "q")}
    rt_ki = [SBUF(f"rt_ki{i}", [128, NEXP], mybir.dt.int32) for i in range(2)]
    rt_nmax = [SBUF(f"rt_nmax{i}", [128, 1]) for i in range(2)]
    rt_rsum = [SBUF(f"rt_rsum{i}", [128, 1]) for i in range(2)]
    rt_rden = [SBUF(f"rt_rden{i}", [128, 1]) for i in range(2)]
    gt_stage = [SBUF(f"gt_stage{i}", [NEXP, 128], F32) for i in range(2)]

    xn_sb = SBUF("xn_sb", [128, NCH * T], BF16)

    def OVERLAY(name, shape, dt, base_ap, byte_off=0):
        addr = nc.lookup_mloc(base_ap.tensor).addr + byte_off
        return nc.alloc_sbuf_tensor_at(name, shape, dt, offset=addr).ap()

    wT = {m: [SBUF(f"w{m}T{i}", [128, D], BF16) for i in range(2)] for m in "qkv"}
    allones_f = OVERLAY("allones_f", [128, 128], F32, wT["q"][0], 0)
    xt_sb = [OVERLAY(f"xt_sb{i}", [128, 128], F32, wT["q"][0], 512 + i * 512)
             for i in range(2)]
    mem_rhs = [SBUF(f"mem_rhs{i}", [128, NMEM * HDIM], BF16) for i in range(2)]
    mn_raw = [SBUF(f"mn_raw{i}", [NMEM, HDIM], BF16) for i in range(2)]
    mn_t = [SBUF(f"mn_t{i}", [128, NMEM], BF16) for i in range(2)]
    qp_sb = OVERLAY("qp_sb", [128, S], BF16, x_tile[0])
    kp_sb = OVERLAY("kp_sb", [128, S], BF16, x_tile[1])
    kpT_sb = OVERLAY("kpT_sb", [128, S], BF16, sq_scr)
    vfm_sb = OVERLAY("vfm_sb", [128, S], BF16, sq_scr, 4096)
    vc_sb = SBUF("vc_sb", [128, NCHK * (HDIM + 1)], BF16)
    elu_t1 = SBUF("elu_t1", [128, 512], F32)
    elu_t2 = SBUF("elu_t2", [128, 512], F32)
    elu_t3 = SBUF("elu_t3", [128, 512], F32)
    R_f = [SBUF(f"R_f{i}", [128, HDIM + 1], F32) for i in range(2)]
    R_bf = [SBUF(f"R_bf{i}", [128, HDIM + 1], BF16) for i in range(2)]
    sT_sb = SBUF("sT_sb", [128, 128], BF16)
    hb_sb = [OVERLAY(f"hb_sb{i}", [128, S], BF16, xn_bf[i]) for i in range(2)]
    mn_rows = OVERLAY("mn_rows", [NMEM, NHEAD * HDIM], BF16, qp_sb)
    macc = [SBUF(f"macc{i}", [128, 128], F32) for i in range(2)]
    cm64 = SBUF("cm64", [128, NCHK * NMEM], F32)
    dclip = SBUF("dclip", [128, 1])
    rden_sb = SBUF("rden_sb", [128, 1])
    rdeno = SBUF("rdeno", [128, 1])
    ho_f = SBUF("ho_f", [128, 128], F32)
    wgb_sb = SBUF("wgb_sb", [128, NMEM])
    omgb_sb = SBUF("omgb_sb", [128, 1])
    qbarT_bf = SBUF("qbarT_bf", [128, 1], BF16)
    mnsum = SBUF("mnsum", [NMEM, 1], F32)
    mnsum_t = SBUF("mnsum_t", [1, NMEM], F32)
    memmask = SBUF("memmask", [1, NMEM], F32)
    gate_t = SBUF("gate_t", [1, NHEAD], F32)
    gate_e = SBUF("gate_e", [1, NHEAD], F32)
    gate_g = SBUF("gate_g", [1, NHEAD], F32)
    gate_omg = SBUF("gate_omg", [1, NHEAD], F32)
    rel_s = SBUF("rel_s", [1, NMEM], F32)
    rel_nm = SBUF("rel_nm", [1, 1], F32)
    rel_e = SBUF("rel_e", [1, NMEM], F32)
    rel_den = SBUF("rel_den", [1, 1], F32)
    rel_rd = SBUF("rel_rd", [1, 1], F32)
    rel_w = SBUF("rel_w", [1, NMEM], F32)
    wg_row = SBUF("wg_row", [1, NMEM], F32)
    wg_bf = SBUF("wg_bf", [1, NMEM], BF16)
    omg_bf = SBUF("omg_bf", [1, 1], BF16)

    PS = [nc.alloc_psum_tensor(f"psum{i}", [128, 512], F32).ap() for i in range(8)]
    PS5B = PS[5].bitcast(BF16)

    P = Prog()

    t0 = P.pool(lambda g: g.memset(tri_bf[:], 1.0), dbg="tri0")
    tok_tri = P.pool(lambda g: g.affine_select(
        out=tri_bf[:], in_=tri_bf[:], compare_op=ALU.is_ge, fill=0.0,
        base=0, pattern=[[1, 128]], channel_multiplier=-1), deps=(t0,), dbg="tri")
    t0 = P.pool(lambda g: g.memset(ident_bf[:], 1.0), dbg="idb0")
    tok_id_bf = P.pool(lambda g: g.affine_select(
        out=ident_bf[:], in_=ident_bf[:], compare_op=ALU.is_equal, fill=0.0,
        base=0, pattern=[[1, 128]], channel_multiplier=-1), deps=(t0,), dbg="idb")
    t0 = P.pool(lambda g: g.memset(ident_f[:], 1.0), dbg="idf0")
    tok_id = P.pool(lambda g: g.affine_select(
        out=ident_f[:], in_=ident_f[:], compare_op=ALU.is_equal, fill=0.0,
        base=0, pattern=[[1, 128]], channel_multiplier=-1), deps=(t0,), dbg="idf")
    tok_ones_bf = P.pool(lambda g: g.memset(ones_bf[:], 1.0), dbg="ones_bf")
    tok_ones_f = P.pool(lambda g: g.memset(ones_f[:], 1.0), dbg="ones_f")
    tok_misc = P.pool(lambda g: g.iota(iota8[:], pattern=[[1, NEXP]], base=0,
                                       channel_multiplier=0,
                                       allow_small_or_imprecise_dtypes=True), dbg="iota")

    tok_bias = P.dma("SP", lambda s: s.dma_start(
        out=bias_b[:], in_=bias_in.to_broadcast([128, NEXP])), chan="bias", dbg="bias")

    tok_gwf = P.dma("POOL", lambda g: g.dma_start(
        out=gwf_sb.rearrange("p (c e) -> p c e", e=NEXP),
        in_=gw_in.rearrange("(c p) e -> p c e", p=128)), chan="gw", dbg="gwf")
    tok_ones128 = P.pool(lambda g: g.memset(allones_f[:], 1.0), dbg="ones128")

    def mk_csum():
        def f(t):
            ins = None
            for c in range(NCH):
                ins = t.matmul(PS[7][:, 0:NEXP], allones_f[:, :],
                               gwf_sb[:, c * NEXP:(c + 1) * NEXP],
                               start=(c == 0), stop=(c == NCH - 1))
            return ins
        return f
    tok_csmm = P.pe(mk_csum(), deps=(tok_gwf, tok_ones128), dbg="csmm")
    tok_ncsb = P.v(lambda v: v.tensor_scalar_mul(ncsb[:], PS[7][:, 0:NEXP], -1.0),
                   deps=(tok_csmm,), dbg="ncsb")

    LOG2E = 1.4426950408889634
    MAGIC = 12582912.0
    LN2_HI = 0.693359375
    LN2_LO = -2.12194440054690581e-4
    EXPC = [1.0 / 720, 1.0 / 120, 1.0 / 24, 1.0 / 6, 0.5, 1.0, 1.0]

    tok_xn_store = [None] * NT128
    tok_xnbf = [None] * NT128
    tok_ztr = [None] * NT128
    tok_lgs = [None] * NT128
    tok_g = [None] * NT128
    tok_g_store = [None] * NT128
    rt_last_read = [None, tok_ncsb]
    gt_last_evac = [None, None]
    zbank_free = [None, None]
    xt_free = [None, None]
    for i in range(NT128):
        p = i % 2
        ld_deps = (tok_xnbf[i - 2], tok_ztr[i - 2], tok_lgs[i - 2]) if i >= 2 else ()
        t_ld = P.dma("SP", (lambda i=i, p=p: (lambda s: s.dma_start(
            out=x_tile[p][:], in_=x_in[i * 128:(i + 1) * 128, :])))(),
            deps=ld_deps, chan=f"xl{p}", dbg=f"xld{i}")
        t_rsum = P.v((lambda p=p: (lambda v: v.reduce_sum(stats["rsum"][p][:], x_tile[p][:], axis=AX)))(),
                     deps=(t_ld,), dbg=f"rsum{i}")
        t_sq = P.a((lambda p=p: (lambda a: a.activation(sq_scr[:], x_tile[p][:], AF.Square,
                                                        accum_out=stats["ss"][p][:])))(),
                   deps=(t_ld,), dbg=f"sq{i}")
        t_mean = P.a((lambda p=p: (lambda a: a.activation(stats["mean"][p][:], stats["rsum"][p][:],
                                                          AF.Copy, scale=1.0 / D)))(),
                     deps=(t_rsum,), dbg=f"mean{i}")
        t_ex2 = P.a((lambda p=p: (lambda a: a.activation(stats["ex2"][p][:], stats["ss"][p][:],
                                                         AF.Copy, scale=1.0 / D)))(),
                    deps=(t_sq,), dbg=f"ex2{i}")
        t_vneg = P.v((lambda p=p: (lambda v: v.scalar_tensor_tensor(
            out=stats["vneg"][p][:], in0=stats["mean"][p][:], scalar=stats["mean"][p][:, 0:1],
            in1=stats["ex2"][p][:], op0=ALU.mult, op1=ALU.subtract)))(),
            deps=(t_mean, t_ex2), dbg=f"vneg{i}")
        t_va = P.v((lambda p=p: (lambda v: v.tensor_scalar(
            out=stats["va"][p][:], in0=stats["vneg"][p][:], scalar1=-1.0,
            scalar2=EPS_LN, op0=ALU.mult, op1=ALU.add)))(),
            deps=(t_vneg,), dbg=f"va{i}")
        t_sqn = P.a((lambda p=p: (lambda a: a.activation(stats["sq"][p][:],
            stats["va"][p][:], AF.Sqrt)))(), deps=(t_va,), dbg=f"sqn{i}")
        t_rq = P.v((lambda p=p: (lambda v: v.reciprocal(stats["rq"][p][:],
            stats["sq"][p][:])))(), deps=(t_sqn,), dbg=f"rq{i}")
        t_nt = P.v((lambda p=p: (lambda v: v.tensor_mul(stats["nt"][p][:],
            stats["va"][p][:], stats["rq"][p][:])))(), deps=(t_rq,), dbg=f"nt{i}")
        t_s1 = P.v((lambda p=p: (lambda v: v.tensor_add(stats["s1"][p][:],
            stats["sq"][p][:], stats["nt"][p][:])))(), deps=(t_nt,), dbg=f"s1{i}")
        t_rr = P.v((lambda p=p: (lambda v: v.reciprocal(stats["rr"][p][:],
            stats["s1"][p][:])))(), deps=(t_s1,), dbg=f"rr{i}")
        t_racc = P.v((lambda p=p: (lambda v: v.tensor_scalar_mul(stats["racc"][p][:],
            stats["rr"][p][:], 2.0)))(), deps=(t_rr,), dbg=f"racc{i}")
        t_nmr = P.v((lambda p=p: (lambda v: v.scalar_tensor_tensor(
            out=stats["nmr"][p][:], in0=stats["mean"][p][:], scalar=-1.0,
            in1=stats["racc"][p][:], op0=ALU.mult, op1=ALU.mult)))(),
            deps=(t_racc,), dbg=f"nmr{i}")
        t_xn = P.a((lambda p=p: (lambda a: a.activation(xn_bf[p][:], x_tile[p][:], AF.Identity,
                                                        scale=stats["racc"][p][:, 0:1],
                                                        bias=stats["nmr"][p][:, 0:1])))(),
                   deps=(t_nmr,) + ((tok_xn_store[i - 2],) if i >= 2 else ()), dbg=f"xn{i}")
        tok_xnbf[i] = t_xn
        tok_xn_store[i] = P.dma("ACT", (lambda i=i, p=p: (lambda a: a.dma_start(
            out=xn_tm[i * 128:(i + 1) * 128, :], in_=xn_bf[p][:])))(),
            deps=(t_xn,), chan=f"xnst{p}", dbg=f"xnst{i}")

        lg = PS[6 + p]
        tT = [None] * NCH
        tE = [None] * NCH
        tM = [None] * NCH

        def rec_T(c, i=i, p=p, t_ld=t_ld):
            b = c % 2
            deps = (t_ld, tok_id) + ((zbank_free[b],) if zbank_free[b] else ())
            tT[c] = P.pe((lambda c=c, p=p, b=b: (lambda t: t.transpose(
                PS[b][:, 0:128], x_tile[p][:, c * 128:(c + 1) * 128], ident_f[:])))(),
                deps=deps, dbg=f"zT{i}_{c}")

        def rec_E(c, i=i):
            b = c % 2
            deps = (tT[c],) + ((xt_free[b],) if xt_free[b] else ())
            tE[c] = P.v((lambda b=b: (lambda v: v.tensor_copy(
                xt_sb[b][:], PS[b][:, 0:128])))(), deps=deps, dbg=f"zE{i}_{c}")
            zbank_free[b] = tE[c]

        def rec_M(c, i=i, p=p, lg=lg):
            b = c % 2
            deps = (tE[c], tok_gwf)
            if c == 0 and rt_last_read[p] is not None:
                deps = deps + (rt_last_read[p],)
            tM[c] = P.pe((lambda c=c, b=b, lg=lg: (lambda t: t.matmul(
                lg[:, 0:NEXP], xt_sb[b][:], gwf_sb[:, c * NEXP:(c + 1) * NEXP],
                start=(c == 0), stop=(c == NCH - 1))))(), deps=deps, dbg=f"zM{i}_{c}")
            xt_free[b] = tM[c]

        rec_T(0); rec_E(0); rec_T(1); rec_E(1)
        for c in range(NCH):
            rec_M(c)
            if c + 2 < NCH:
                rec_T(c + 2)
                rec_E(c + 2)
        tok_ztr[i] = tT[NCH - 1]

        t_lgc = P.v((lambda p=p, lg=lg: (lambda v: v.scalar_tensor_tensor(
            out=rt["lgc"][p][:], in0=ncsb[:], scalar=stats["mean"][p][:, 0:1],
            in1=lg[:, 0:NEXP], op0=ALU.mult, op1=ALU.add)))(),
            deps=(tM[NCH - 1], t_mean, tok_ncsb), dbg=f"lgc{i}")
        t_lgs = P.v((lambda p=p: (lambda v: v.tensor_single_scalar(
            out=rt["lgc"][p][:], in_=rt["lgc"][p][:],
            scalar=stats["racc"][p][:, 0:1], op=ALU.mult)))(),
            deps=(t_lgc, t_racc), dbg=f"lgs{i}")
        tok_lgs[i] = t_lgs
        rt_last_read[p] = t_lgc

        t_nmax = P.v((lambda p=p: (lambda v: v.reduce_max(
            rt_nmax[p][:], rt["lgc"][p][:], axis=AX, negate=True)))(),
            deps=(t_lgs,), dbg=f"rtnmax{i}")
        t_u = P.v((lambda p=p: (lambda v: v.tensor_single_scalar(
            out=rt["u"][p][:], in_=rt["lgc"][p][:], scalar=rt_nmax[p][:, 0:1],
            op=ALU.add)))(), deps=(t_nmax,), dbg=f"rtu{i}")
        t_e1 = P.v((lambda p=p: (lambda v: v.tensor_scalar_mul(
            rt["kf"][p][:], rt["u"][p][:], LOG2E)))(), deps=(t_u,), dbg=f"e1_{i}")
        t_e2 = P.v((lambda p=p: (lambda v: v.tensor_scalar(
            out=rt["kf"][p][:], in0=rt["kf"][p][:], scalar1=MAGIC, scalar2=-MAGIC,
            op0=ALU.add, op1=ALU.add)))(), deps=(t_e1,), dbg=f"e2_{i}")
        t_e3 = P.v((lambda p=p: (lambda v: v.scalar_tensor_tensor(
            out=rt["f"][p][:], in0=rt["kf"][p][:], scalar=-LN2_HI,
            in1=rt["u"][p][:], op0=ALU.mult, op1=ALU.add)))(), deps=(t_e2,), dbg=f"e3_{i}")
        t_e4 = P.v((lambda p=p: (lambda v: v.scalar_tensor_tensor(
            out=rt["f"][p][:], in0=rt["kf"][p][:], scalar=-LN2_LO,
            in1=rt["f"][p][:], op0=ALU.mult, op1=ALU.add)))(), deps=(t_e3,), dbg=f"e4_{i}")
        t_prev = P.v((lambda p=p: (lambda v: v.tensor_scalar(
            out=rt["pp"][p][:], in0=rt["f"][p][:], scalar1=EXPC[0], scalar2=EXPC[1],
            op0=ALU.mult, op1=ALU.add)))(), deps=(t_e4,), dbg=f"ep0_{i}")
        for hi, cc in enumerate(EXPC[2:]):
            t_qm = P.v((lambda p=p: (lambda v: v.tensor_mul(
                rt["q"][p][:], rt["pp"][p][:], rt["f"][p][:])))(),
                deps=(t_prev,), dbg=f"eq{hi}_{i}")
            t_prev = P.v((lambda p=p, cc=cc: (lambda v: v.tensor_scalar_add(
                rt["pp"][p][:], rt["q"][p][:], cc)))(),
                deps=(t_qm,), dbg=f"ep{hi + 1}_{i}")
        t_ki = P.v((lambda p=p: (lambda v: v.tensor_copy(rt_ki[p][:], rt["kf"][p][:])))(),
                   deps=(t_e4,), dbg=f"eki{i}")
        t_k2 = P.v((lambda p=p: (lambda v: v.tensor_scalar_add(rt_ki[p][:], rt_ki[p][:], 127)))(),
                   deps=(t_ki,), dbg=f"ek2_{i}")
        t_k3 = P.v((lambda p=p: (lambda v: v.tensor_single_scalar(
            out=rt_ki[p][:], in_=rt_ki[p][:], scalar=23, op=ALU.logical_shift_left)))(),
            deps=(t_k2,), dbg=f"ek3_{i}")
        t_ex = P.v((lambda p=p: (lambda v: v.tensor_mul(
            rt["ex"][p][:], rt["pp"][p][:], rt_ki[p].bitcast(F32)[:])))(),
            deps=(t_prev, t_k3), dbg=f"rtex{i}")
        t_rs = P.v((lambda p=p: (lambda v: v.reduce_sum(rt_rsum[p][:], rt["ex"][p][:], axis=AX)))(),
                   deps=(t_ex,), dbg=f"rtrs{i}")
        t_rd = P.v((lambda p=p: (lambda v: v.reciprocal(rt_rden[p][:], rt_rsum[p][:])))(),
                   deps=(t_rs,), dbg=f"rtrd{i}")
        t_sc = P.v((lambda p=p: (lambda v: v.tensor_single_scalar(
            out=rt["sc"][p][:], in_=rt["ex"][p][:], scalar=rt_rden[p][:, 0:1], op=ALU.mult)))(),
            deps=(t_rd,), dbg=f"rtsc{i}")
        t_b = P.v((lambda p=p: (lambda v: v.tensor_add(rt["b"][p][:], rt["sc"][p][:], bias_b[:])))(),
                  deps=(t_sc, tok_bias), dbg=f"rtb{i}")
        t_mx = P.v((lambda p=p: (lambda v: v.max(out=rt["mx"][p][:], in_=rt["b"][p][:])))(),
                   deps=(t_b,), dbg=f"rtmx{i}")
        t_ix = P.v((lambda p=p: (lambda v: v.max_index(
            out=rt["ix"][p][:], in_max=rt["mx"][p][:], in_values=rt["b"][p][:])))(),
            deps=(t_mx,), dbg=f"rtix{i}")
        t_ixf = P.v((lambda p=p: (lambda v: v.tensor_copy(rt["ixf"][p][:], rt["ix"][p][:])))(),
                    deps=(t_ix,), dbg=f"rtixf{i}")
        t_m1 = P.v((lambda p=p: (lambda v: v.tensor_single_scalar(
            out=rt["m1"][p][:], in_=iota8[:], scalar=rt["ixf"][p][:, 0:1], op=ALU.is_equal)))(),
            deps=(t_ixf, tok_misc), dbg=f"rtm1_{i}")
        t_m2 = P.v((lambda p=p: (lambda v: v.tensor_single_scalar(
            out=rt["m2"][p][:], in_=iota8[:], scalar=rt["ixf"][p][:, 1:2], op=ALU.is_equal)))(),
            deps=(t_ixf,), dbg=f"rtm2_{i}")
        t_m12 = P.v((lambda p=p: (lambda v: v.tensor_add(rt["m1"][p][:], rt["m1"][p][:], rt["m2"][p][:])))(),
                    deps=(t_m1, t_m2), dbg=f"rtm12_{i}")
        tok_g[i] = P.v((lambda i=i, p=p: (lambda v: v.tensor_mul(
            g_res[:, i * NEXP:(i + 1) * NEXP], rt["m1"][p][:], rt["sc"][p][:])))(),
            deps=(t_m12,), dbg=f"rtg{i}")
        gt_ps = PS[6 + p][0:NEXP, 8:136]
        t_gt = P.pe((lambda i=i, gt_ps=gt_ps: (lambda t: t.transpose(
            gt_ps, g_res[:, i * NEXP:(i + 1) * NEXP], ident_f[:])))(),
            deps=(tok_g[i], tok_id) + ((gt_last_evac[p],) if gt_last_evac[p] is not None else ()),
            dbg=f"gtT{i}")
        t_gtev = P.v((lambda p=p, gt_ps=gt_ps: (lambda v: v.tensor_copy(gt_stage[p][:], gt_ps)))(),
                     deps=(t_gt,) + ((tok_g_store[i - 2],) if i >= 2 else ()), dbg=f"gtev{i}")
        gt_last_evac[p] = t_gtev
        rt_last_read[p] = t_gtev
        tok_g_store[i] = P.dma("ACT", (lambda i=i, p=p: (lambda a: a.dma_start(
            out=gT_dram[:, i * 128:(i + 1) * 128], in_=gt_stage[p][:])))(),
            deps=(t_gtev,), chan=f"gtst{p}", dbg=f"gtst{i}")

    tok_tr_b = [[None] * NCH for _ in range(NB)]
    tok_fmst = [None] * NCH
    for b in range(NB):
        half_dep = tuple(tok_xn_store[b * (NT128 // NB):(b + 1) * (NT128 // NB)])
        for c in range(NCH):
            tok_tr_b[b][c] = P.dma("SP", (lambda b=b, c=c: (lambda s: s.dma_start(
                out=xn_sb[:, c * T + b * S: c * T + (b + 1) * S],
                in_=xn_tm[b * S:(b + 1) * S, c * 128:(c + 1) * 128],
                transpose=True)))(),
                deps=half_dep, chan=f"tr{b}", dbg=f"tr{b}_{c}")
    tr_half = [tuple(tok_tr_b[b]) for b in range(NB)]
    tr_all = tr_half[0] + tr_half[1]
    for c in range(NCH):
        tok_fmst[c] = P.dma("ACT", (lambda c=c: (lambda a: a.dma_start(
            out=xn_fm[c, :, :], in_=xn_sb[:, c * T:(c + 1) * T])))(),
            deps=tr_all, chan="fmst", dbg=f"fmst{c}")

    tok_xsum = []
    for b in range(NB):
        for c in range(NCH):
            tok_xsum.append(P.v((lambda b=b, c=c: (lambda v: v.reduce_sum(
                xsum_f[:, (b * NCH + c):(b * NCH + c + 1)],
                xn_sb[:, c * T + b * S: c * T + (b + 1) * S], axis=AX)))(),
                deps=tr_half[b], dbg=f"xsum{b}_{c}"))
    tok_xsum_b = [None] * NB
    for b in range(NB):
        tok_xsum_b[b] = P.v((lambda b=b: (lambda v: v.tensor_scalar_mul(
            xsum_bf[:, b * NCH:(b + 1) * NCH], xsum_f[:, b * NCH:(b + 1) * NCH],
            1.0 / S)))(),
            deps=tuple(tok_xsum[b * NCH:(b + 1) * NCH]), dbg=f"xsum_bf{b}")
    tok_xsum_bf = tok_xsum_b[-1]

    tok_hb_store = []
    if 2 in phases:
        tok_wcast = {}
        for nm, src, dst in (("q", w_q, wq_bf), ("k", w_k, wk_bf), ("v", w_v, wv_bf),
                             ("o", w_o, wo_bf)):
            tok_wcast[nm] = P.dma("POOL", (lambda src=src, dst=dst: (
                lambda g: g.dma_start(out=dst[:, :], in_=src[:, :])))(),
                chan=f"wc{nm}", dbg=f"wcast{nm}")
        t_mnr = P.dma("POOL", lambda g: g.dma_start(
            out=mn_rows[:], in_=mn_in.rearrange("m h d -> m (h d)")),
            deps=(tok_ztr[NT128 - 2], tok_xnbf[NT128 - 2], tok_lgs[NT128 - 2]),
            chan="mnr", dbg="mnrows")
        t_mns = P.v(lambda v: v.reduce_sum(mnsum[:], mn_rows[:], axis=AX),
                    deps=(t_mnr,), dbg="mnsum")
        t_mnT = P.pe(lambda t: t.transpose(PS[7][0:1, 8:12], mnsum[:], ident_f[0:NMEM, 0:NMEM]),
                     deps=(t_mns, tok_id, rt_last_read[1]), dbg="mnT")
        t_mnev = P.v(lambda v: v.tensor_copy(mnsum_t[:], PS[7][0:1, 8:12]),
                     deps=(t_mnT,), dbg="mnev")
        t_mask = P.v(lambda v: v.tensor_single_scalar(out=memmask[:], in_=mnsum_t[:],
                                                      scalar=1e-6, op=ALU.is_ge),
                     deps=(t_mnev,), dbg="memmask")
        t_gld = P.dma("POOL", lambda g: g.dma_start(out=gate_t[:], in_=gate_in[:, :]),
                      chan="gld", dbg="gateld")
        t_ge = P.a(lambda a: a.activation(gate_e[:], gate_t[:], AF.Exp, scale=-1.0),
                   deps=(t_gld,), dbg="gate_e")
        t_g1 = P.v(lambda v: v.tensor_scalar_add(gate_e[:], gate_e[:], 1.0),
                   deps=(t_ge,), dbg="gate_1p")
        t_gg = P.v(lambda v: v.reciprocal(gate_g[:], gate_e[:]), deps=(t_g1,), dbg="gate_g")
        t_gomg = P.v(lambda v: v.tensor_scalar(out=gate_omg[:], in0=gate_g[:], scalar1=-1.0,
                                               scalar2=1.0, op0=ALU.mult, op1=ALU.add),
                     deps=(t_gg,), dbg="gate_omg")

        tok_ffncast = {}
        wT_free = [None, None]
        mem_free = [None, None]
        hb_free = [None, None]
        tok_mem = [None, None]
        tok_mnt = [None, None]
        tok_wT = [None, None]
        prep_chain = None
        bank67_free_init_mnev = True
        prev_pe_last = None
        bank01_free = [zbank_free[0], zbank_free[1]]
        bank23_free = [None, None]
        bank4_free = None
        bank5_free = None
        bank67_free = [rt_last_read[0], t_mnev]
        amv_sl = [elu_t1, elu_t2]
        amv_slot_free = [None, None]
        amv_free = None
        last_eluadd = None
        tok_vones = P.v(lambda v: v.memset(
            vc_sb.rearrange("p (n e) -> p n e", e=HDIM + 1)[:, :, HDIM:HDIM + 1], 1.0),
            dbg="vones")

        for h in range(p2_heads):
            hp = h % 2
            wdep = (wT_free[hp],)
            if h == 0:
                wdep = wdep + (xt_free[0], xt_free[1], tok_csmm)
            toks = []
            for nm in "qkv":
                for c in range(NCH):
                    toks.append(P.dma("SP", (lambda nm=nm, c=c, h=h, hp=hp: (lambda s: s.dma_start(
                        out=wT[nm][hp][:, c * 128:(c + 1) * 128],
                        in_=(wq_bf if nm == "q" else wk_bf if nm == "k" else wv_bf)[
                            h * 128:(h + 1) * 128, c * 128:(c + 1) * 128],
                        transpose=True)))(),
                        deps=wdep + (tok_wcast[nm],), chan=f"wT{hp}", dbg=f"wT{nm}{h}_{c}"))
            tok_wT[hp] = toks[-1]
            tok_mem[hp] = P.dma("POOL", (lambda h=h, hp=hp: (lambda g: g.dma_start(
                out=mem_rhs[hp].rearrange("p (m e) -> p m e", m=NMEM),
                in_=mem_in[:, h, :, :].rearrange("m d e -> d m e"))))(),
                deps=(mem_free[hp],), chan=f"mem{hp}", dbg=f"memld{h}")
            t_mnld = P.dma("POOL", (lambda h=h, hp=hp: (lambda g: g.dma_start(
                out=mn_raw[hp][:, :], in_=mn_in[:, h, :])))(),
                deps=(mem_free[hp],), chan=f"mnld{hp}", dbg=f"mnld{h}")
            stage0 = PS[0].bitcast(BF16)
            t_mntT = P.pe((lambda hp=hp, stage0=stage0: (lambda t: t.transpose(
                stage0[:, 896:896 + NMEM],
                mn_raw[hp][:, :], ident_bf[0:NMEM, 0:NMEM])))(),
                deps=(t_mnld, tok_id_bf, bank01_free[0]), dbg=f"mnT{h}")
            tok_mnt[hp] = P.a((lambda hp=hp, stage0=stage0: (lambda a: a.copy(
                mn_t[hp][:], stage0[:, 896:896 + NMEM])))(),
                deps=(t_mntT,), dbg=f"mnTev{h}")
            bank01_free[0] = tok_mnt[hp]

            if p2_stage < 2:
                continue
            for b in range(NB):
                hbp = (h * NB + b) % 2
                u_idx = h * NB + b
                if u_idx < 27:
                    kind, ei = divmod(u_idx, NEXP + 1)
                    csrc = ([w1_in[ei] if ei < NEXP else sw1_in,
                             w3_in[ei] if ei < NEXP else sw3_in,
                             w2_in[ei] if ei < NEXP else sw2_in][kind])
                    cdst = [w1bf9, w3bf9, w2bf9][kind]
                    tok_ffncast[(kind, ei)] = P.dma("POOL", (lambda csrc=csrc, cdst=cdst, ei=ei:
                        (lambda g: g.dma_start(out=cdst[ei], in_=csrc[:, :])))(),
                        chan="ffnc", dbg=f"ffnc{u_idx}")
                t_qbar = P.pe((lambda hp=hp, b=b: (lambda t: [
                    t.matmul(PS[5][:, 0:1], wT["q"][hp][:, c * 128:(c + 1) * 128],
                             xsum_bf[:, (b * NCH + c):(b * NCH + c + 1)],
                             start=(c == 0), stop=(c == NCH - 1))
                    for c in range(NCH)][-1]))(),
                    deps=tuple(d for d in (tok_wT[hp], tok_xsum_b[b], bank5_free) if d),
                    dbg=f"qbar{h}_{b}")
                t_qbev = P.v(lambda v: v.tensor_copy(qbarT_bf[:], PS[5][:, 0:1]),
                             deps=(t_qbar,), dbg=f"qbev{h}_{b}")
                t_rel = P.pe((lambda hp=hp: (lambda t: t.matmul(
                    PS[5][0:1, 1:5], qbarT_bf[:], mn_t[hp][:], start=True, stop=True)))(),
                    deps=(t_qbev, tok_mnt[hp]), dbg=f"rel{h}_{b}")
                t_rsc = P.v(lambda v: v.tensor_scalar_mul(rel_s[:], PS[5][0:1, 1:5], ATT_SCALE),
                            deps=(t_rel,), dbg=f"relsc{h}_{b}")
                t_rnm = P.v(lambda v: v.reduce_max(rel_nm[:], rel_s[:], axis=AX, negate=True),
                            deps=(t_rsc,), dbg=f"relnm{h}_{b}")
                t_rex = P.a(lambda a: a.activation(rel_e[:], rel_s[:], AF.Exp,
                                                   bias=rel_nm[0:1, 0:1]),
                            deps=(t_rnm,), dbg=f"relex{h}_{b}")
                t_rem = P.v(lambda v: v.tensor_mul(rel_e[:], rel_e[:], memmask[:]),
                            deps=(t_rex, t_mask), dbg=f"relem{h}_{b}")
                t_rds = P.v(lambda v: v.reduce_sum(rel_den[:], rel_e[:], axis=AX),
                            deps=(t_rem,), dbg=f"relds{h}_{b}")
                t_rdc = P.v(lambda v: v.tensor_scalar_max(rel_den[:], rel_den[:], 1e-30),
                            deps=(t_rds,), dbg=f"reldc{h}_{b}")
                t_rrd = P.v(lambda v: v.reciprocal(rel_rd[:], rel_den[:]),
                            deps=(t_rdc,), dbg=f"relrd{h}_{b}")
                t_rw = P.v(lambda v: v.tensor_single_scalar(out=rel_w[:], in_=rel_e[:],
                                                            scalar=rel_rd[0:1, 0:1], op=ALU.mult),
                           deps=(t_rrd,), dbg=f"relw{h}_{b}")
                t_wg = P.v((lambda h=h: (lambda v: v.tensor_single_scalar(
                    out=wg_row[:], in_=rel_w[:], scalar=gate_g[0:1, h:h + 1], op=ALU.mult)))(),
                    deps=(t_rw, t_gg), dbg=f"wgrow{h}_{b}")
                t_wgbf = P.a(lambda a: a.copy(wg_bf[:], wg_row[:]),
                             deps=(t_wg, prep_chain), dbg=f"wgbf{h}_{b}")
                t_ogbf = P.a((lambda h=h: (lambda a: a.copy(omg_bf[:], gate_omg[0:1, h:h + 1])))(),
                             deps=(t_gomg, prep_chain), dbg=f"ogbf{h}_{b}")
                t_wgmm = P.pe(lambda t: t.matmul(PS[5][:, 5:9], ones_bf[:], wg_bf[:],
                                                 start=True, stop=True),
                              deps=(t_wgbf, tok_ones_bf, t_rsc), dbg=f"wgmm{h}_{b}")
                t_wgbev = P.v(lambda v: v.tensor_copy(wgb_sb[:], PS[5][:, 5:9]),
                              deps=(t_wgmm,), dbg=f"wgbev{h}_{b}")
                t_ogmm = P.pe(lambda t: t.matmul(PS[5][:, 9:10], ones_bf[:], omg_bf[:],
                                                 start=True, stop=True),
                              deps=(t_ogbf, t_wgbev), dbg=f"ogmm{h}_{b}")
                t_ogbev = P.v(lambda v: v.tensor_copy(omgb_sb[:], PS[5][:, 9:10]),
                              deps=(t_ogmm,), dbg=f"ogbev{h}_{b}")
                bank5_free = t_ogbev
                if p2_stage < 3:
                    prep_chain = t_ogbev
                    continue

                tok_q = [None] * 4
                tok_k = [None] * 4
                tok_v = [None] * 4
                gi = 0
                for nm in "qkv":
                    for tt in range(4):
                        bank = PS[gi % 2]
                        deps_p = [tok_wT[hp], prev_pe_last, bank01_free[gi % 2]] + list(tr_half[b])
                        if h == 0 and b == 0:
                            deps_p.append(t_mns)
                        t_mm = P.pe((lambda nm=nm, hp=hp, b=b, tt=tt, bank=bank: (lambda t: [
                            t.matmul(bank[:, :],
                                     wT[nm][hp][:, c * 128:(c + 1) * 128],
                                     xn_sb[:, c * T + b * S + tt * 512: c * T + b * S + (tt + 1) * 512],
                                     start=(c == 0), stop=(c == NCH - 1))
                            for c in range(NCH)][-1]))(),
                            deps=tuple(d for d in deps_p if d is not None),
                            dbg=f"proj{nm}{h}_{b}_{tt}")
                        xn_last_reader = t_mm
                        if nm in "qk":
                            dst = qp_sb if nm == "q" else kp_sb
                            t_1 = P.a((lambda bank=bank: (lambda a: a.activation(
                                elu_t1[:], bank[:, :], AF.Relu, scale=-1.0)))(),
                                deps=(t_mm,), dbg=f"elu1{nm}{h}_{b}_{tt}")
                            t_2 = P.a(lambda a: a.activation(elu_t2[:], elu_t1[:], AF.Exp, scale=-1.0),
                                      deps=(t_1, last_eluadd), dbg=f"elu2{nm}{h}_{b}_{tt}")
                            t_3 = P.a((lambda bank=bank: (lambda a: a.activation(
                                elu_t3[:], bank[:, :], AF.Relu)))(),
                                deps=(t_2,), dbg=f"elu3{nm}{h}_{b}_{tt}")
                            t_add = P.v((lambda dst=dst, tt=tt: (lambda v: v.tensor_add(
                                dst[:, tt * 512:(tt + 1) * 512], elu_t2[:], elu_t3[:])))(),
                                deps=(t_3,), dbg=f"eluadd{nm}{h}_{b}_{tt}")
                            bank01_free[gi % 2] = t_3
                            last_eluadd = t_add
                            (tok_q if nm == "q" else tok_k)[tt] = t_add
                        else:
                            t_cp = P.a((lambda bank=bank, tt=tt: (lambda a: a.copy(
                                vfm_sb[:, tt * 512:(tt + 1) * 512], bank[:, :])))(),
                                deps=(t_mm,), dbg=f"vcp{h}_{b}_{tt}")
                            bank01_free[gi % 2] = t_cp
                            tok_v[tt] = t_cp
                        gi += 1

                if p2_stage < 4:
                    prev_pe_last = t_mm
                    wT_free[hp] = t_mm
                    continue
                tok_kpT = [None] * NCHK
                tok_vc = [None] * NCHK
                for n in range(NCHK):
                    sl = n % 2
                    st_ps = PS[sl].bitcast(BF16)[:, 896:1024]
                    t_T = P.pe((lambda n=n, st_ps=st_ps: (lambda t: t.transpose(
                        st_ps, kp_sb[:, n * 128:(n + 1) * 128], ident_bf[:])))(),
                        deps=(tok_k[n // 4], bank01_free[sl], tok_id_bf), dbg=f"kpT{h}_{b}_{n}")
                    t_ev = P.a((lambda n=n, st_ps=st_ps: (lambda a: a.copy(
                        kpT_sb[:, n * 128:(n + 1) * 128], st_ps)))(),
                        deps=(t_T, prev_pe_last), dbg=f"kpTev{h}_{b}_{n}")
                    bank01_free[sl] = t_ev
                    tok_kpT[n] = t_ev
                for n in range(NCHK):
                    sl = n % 2
                    st_ps = PS[sl].bitcast(BF16)[:, 896:1024]
                    t_T = P.pe((lambda n=n, st_ps=st_ps: (lambda t: t.transpose(
                        st_ps, vfm_sb[:, n * 128:(n + 1) * 128], ident_bf[:])))(),
                        deps=(tok_v[n // 4], bank01_free[sl], tok_id_bf), dbg=f"vT{h}_{b}_{n}")
                    t_ev = P.a((lambda n=n, st_ps=st_ps: (lambda a: a.copy(
                        vc_sb[:, n * 129:n * 129 + 128], st_ps)))(),
                        deps=(t_T, prev_pe_last), dbg=f"vTev{h}_{b}_{n}")
                    bank01_free[sl] = t_ev
                    tok_vc[n] = t_ev

                if p2_stage < 5:
                    prev_pe_last = t_T
                    wT_free[hp] = t_T
                    continue
                nm_toks = []
                for n in range(NCHK):
                    nm_toks.append(P.pe((lambda n=n, hp=hp: (lambda t: t.matmul(
                        PS[3][:, n * NMEM:(n + 1) * NMEM],
                        qp_sb[:, n * 128:(n + 1) * 128], mn_t[hp][:],
                        start=True, stop=True)))(),
                        deps=(tok_q[n // 4], tok_mnt[hp], bank23_free[1]),
                        dbg=f"nmb{h}_{b}_{n}"))
                t_ncl = P.v(lambda v: v.tensor_scalar_max(
                    cm64[:], PS[3][:, 0:NCHK * NMEM], EPS_ATT),
                    deps=tuple(nm_toks), dbg=f"nclipb{h}_{b}")
                bank23_free[1] = t_ncl
                t_cm = P.v(lambda v: v.reciprocal(cm64[:], cm64[:]),
                           deps=(t_ncl,), dbg=f"rnormb{h}_{b}")
                cm64v = cm64.rearrange("p (n m) -> p n m", m=NMEM)
                for m in range(NMEM):
                    t_cm = P.v((lambda m=m: (lambda v: v.tensor_single_scalar(
                        out=cm64v[:, :, m:m + 1], in_=cm64v[:, :, m:m + 1],
                        scalar=wgb_sb[:, m:m + 1], op=ALU.mult)))(),
                        deps=(t_cm, t_wgbev), dbg=f"cmb{h}_{b}_{m}")
                tok_cm64 = t_cm

                t_Rz = P.v(lambda v: v.memset(R_f[0][:], 0.0),
                           deps=(prev_pe_last,) if prev_pe_last else (), dbg=f"Rz{h}_{b}")
                t_Rbf = P.a(lambda a: a.copy(R_bf[0][:], R_f[0][:]),
                            deps=(t_Rz,) + ((prev_pe_last,) if prev_pe_last else ()),
                            dbg=f"Rbf{h}_{b}")
                tok_Rf = t_Rz
                tok_Rbf = t_Rbf

                last_ho_cast = None
                tok_ho = [None] * NCHK
                tail = [None] * NCHK
                tail_prev = None

                def rec_tail(n, h=h, b=b, hbp=hbp):
                    nonlocal last_ho_cast, tail_prev
                    t_nd, t_am, t_Ru = tail[n]
                    ndb = PS[6 + n % 2]
                    amb = PS[2 + n % 2]
                    t_dc = P.v((lambda ndb=ndb: (lambda v: v.tensor_scalar_max(
                        dclip[:], ndb[:, HDIM:HDIM + 1], EPS_ATT)))(),
                        deps=(t_nd, tail_prev), dbg=f"dclip{h}_{b}_{n}")
                    t_rd2 = P.v(lambda v: v.reciprocal(rden_sb[:], dclip[:]), deps=(t_dc,),
                                dbg=f"rden{h}_{b}_{n}")
                    t_rdo = P.v(lambda v: v.tensor_mul(rdeno[:], rden_sb[:], omgb_sb[:]),
                                deps=(t_rd2, t_ogbev), dbg=f"rdeno{h}_{b}_{n}")
                    t_mc = P.v((lambda n=n, amb=amb: (lambda v: v.tensor_single_scalar(
                        out=macc[0][:], in_=amb[:, 0:128],
                        scalar=cm64[:, n * NMEM:n * NMEM + 1], op=ALU.mult)))(),
                        deps=(t_am, tok_cm64, tail_prev), dbg=f"mc0_{h}_{b}_{n}")
                    for m in range(1, NMEM):
                        t_mc = P.v((lambda m=m, n=n, amb=amb: (lambda v: v.scalar_tensor_tensor(
                            out=macc[m % 2][:], in0=amb[:, m * 128:(m + 1) * 128],
                            scalar=cm64[:, n * NMEM + m:n * NMEM + m + 1], in1=macc[(m - 1) % 2][:],
                            op0=ALU.mult, op1=ALU.add)))(),
                            deps=(t_mc,), dbg=f"mc{m}_{h}_{b}_{n}")
                    bank23_free[n % 2] = t_mc
                    t_ho = P.v((lambda ndb=ndb: (lambda v: v.scalar_tensor_tensor(
                        out=ho_f[:], in0=ndb[:, 0:HDIM], scalar=rdeno[:, 0:1],
                        in1=macc[(NMEM - 1) % 2][:], op0=ALU.mult, op1=ALU.add)))(),
                        deps=(t_mc, t_rdo, last_ho_cast), dbg=f"ho{h}_{b}_{n}")
                    bank67_free[n % 2] = t_ho
                    t_hc = P.a((lambda n=n, hbp=hbp: (lambda a: a.copy(
                        hb_sb[hbp][:, n * 128:(n + 1) * 128], ho_f[:])))(),
                        deps=(t_ho, hb_free[hbp], last_ho_cast), dbg=f"hocast{h}_{b}_{n}")
                    last_ho_cast = t_hc
                    tail_prev = t_ho
                    tok_ho[n] = t_hc

                for n in range(NCHK):
                    rp = n % 2
                    t_kv = P.pe((lambda n=n: (lambda t: t.matmul(
                        PS[4][:, 0:HDIM + 1], kpT_sb[:, n * 128:(n + 1) * 128],
                        vc_sb[:, n * 129:(n + 1) * 129], start=True, stop=True)))(),
                        deps=(tok_kpT[n], tok_vc[n], bank4_free, tok_vones),
                        dbg=f"kv{h}_{b}_{n}")
                    t_sc = P.pe((lambda n=n: (lambda t: t.matmul(
                        PS[5][:, 0:128], kp_sb[:, n * 128:(n + 1) * 128],
                        qp_sb[:, n * 128:(n + 1) * 128], start=True, stop=True)))(),
                        deps=(tok_k[n // 4], tok_q[n // 4], bank5_free), dbg=f"sc{h}_{b}_{n}")
                    t_sT = P.v((lambda: (lambda v: v.tensor_mul(sT_sb[:], PS[5][:, 0:128], tri_bf[:])))(),
                               deps=(t_sc, tok_tri), dbg=f"sT{h}_{b}_{n}")
                    bank5_free = t_sT
                    t_Ru = P.v((lambda rp=rp: (lambda v: v.tensor_add(
                        R_f[1 - rp][:], R_f[rp][:], PS[4][:, 0:HDIM + 1])))(),
                        deps=(t_kv, tok_Rf), dbg=f"Ru{h}_{b}_{n}")
                    bank4_free = t_Ru
                    tok_Rf = t_Ru
                    t_nd = P.pe((lambda n=n, rp=rp, tok_Rbf=tok_Rbf: (lambda t: (
                        t.matmul(PS[6 + n % 2][:, 0:HDIM + 1], sT_sb[:],
                                 vc_sb[:, n * 129:(n + 1) * 129], start=True, stop=False),
                        t.matmul(PS[6 + n % 2][:, 0:HDIM + 1], qp_sb[:, n * 128:(n + 1) * 128],
                                 R_bf[rp][:], start=False, stop=True))[-1]))(),
                        deps=(t_sT, tok_Rbf, bank67_free[n % 2]), dbg=f"nd{h}_{b}_{n}")
                    t_am = P.pe((lambda n=n, hp=hp: (lambda t: t.matmul(
                        PS[2 + n % 2][:, :], qp_sb[:, n * 128:(n + 1) * 128], mem_rhs[hp][:],
                        start=True, stop=True)))(),
                        deps=(tok_q[n // 4], tok_mem[hp], bank23_free[n % 2]), dbg=f"am{h}_{b}_{n}")
                    pe_chunk_last = t_am
                    if n < NCHK - 1:
                        tok_Rbf = P.a((lambda rp=rp: (lambda a: a.copy(
                            R_bf[1 - rp][:], R_f[1 - rp][:])))(),
                            deps=(t_Ru, t_nd), dbg=f"Rbfc{h}_{b}_{n}")
                    tail[n] = (t_nd, t_am, t_Ru)
                    if n >= 1:
                        rec_tail(n - 1)
                rec_tail(NCHK - 1)
                prep_chain = tail_prev
                prev_pe_last = pe_chunk_last
                wT_free[hp] = pe_chunk_last
                mem_free[hp] = pe_chunk_last
                hb_free[hbp] = P.dma("POOL", (lambda h=h, b=b, hbp=hbp: (lambda g: g.dma_start(
                    out=attn_tm[b * S:(b + 1) * S, h * 128:(h + 1) * 128].rearrange(
                        "(n c) e -> c n e", c=128),
                    in_=hb_sb[hbp].rearrange("c (n e) -> c n e", e=128))))(),
                    deps=tuple(tok_ho), chan=f"hbst{hbp}", dbg=f"hbst{h}_{b}")
                tok_hb_store.append(hb_free[hbp])

    tok_y_store = []
    if 3 in phases:
        woT_sb = OVERLAY("woT_sb", [128, NCH * D], BF16, xn_sb)
        afm_sb = [OVERLAY(f"afm_sb{i}", [128, NCH * 512], BF16, xn_sb, 65536 + i * 16384)
                  for i in range(2)]
        x_res = [OVERLAY(f"x_res{i}", [128, D], F32, xn_sb, 98304 + i * 8192)
                 for i in range(2)]
        y_stage = [OVERLAY(f"y_stage{i}", [128, 512], F32, xn_sb, 114688 + i * 2048)
                   for i in range(2)]
        p3dep = (xn_last_reader,)

        tok_woT = None
        for c in range(NCH):
            for eb in range(NCH):
                tok_woT = P.dma("SP", (lambda c=c, eb=eb: (lambda sy: sy.dma_start(
                    out=woT_sb[:, c * D + eb * 128: c * D + (eb + 1) * 128],
                    in_=wo_bf[eb * 128:(eb + 1) * 128, c * 128:(c + 1) * 128],
                    transpose=True)))(),
                    deps=p3dep + (tok_wcast["o"],), chan="woT", dbg=f"woT{c}_{eb}")

        afm_free = [None, None]
        xres_free = [None, None]
        yps_free = [bank23_free[0], bank23_free[1]]
        yst_free = [None, None]
        tok_afm = [None, None]
        tok_xres = [None, None]
        for i5 in range(NT512):
            par = i5 % 2
            b = (i5 * 512) // S
            tok_afm[par] = None
            afm_toks = []
            for c in range(NCH):
                afm_toks.append(P.dma("SP", (lambda i5=i5, c=c, par=par: (lambda sy: sy.dma_start(
                    out=afm_sb[par][:, c * 512:(c + 1) * 512],
                    in_=attn_tm[i5 * 512:(i5 + 1) * 512, c * 128:(c + 1) * 128],
                    transpose=True)))(),
                    deps=(tok_hb_store[c * NB + b], afm_free[par]), chan=f"afm{par}",
                    dbg=f"afm{i5}_{c}"))
            tok_afm[par] = afm_toks[-1]
            for it in range(4):
                i = i5 * 4 + it
                xp = i % 2
                tok_xres[xp] = P.dma("ACT", (lambda i=i, xp=xp: (lambda a: a.dma_start(
                    out=x_res[xp][:], in_=x_in[i * 128:(i + 1) * 128, :])))(),
                    deps=(xres_free[xp],) + p3dep, chan=f"xres{xp}", dbg=f"xres{i}")
                for eb in range(4):
                    u = (i * 4 + eb) % 2
                    bank = PS[2 + u]
                    t_mm = P.pe((lambda it=it, eb=eb, par=par, bank=bank: (lambda t: [
                        t.matmul(bank[:, :],
                                 afm_sb[par][:, c * 512 + it * 128: c * 512 + it * 128 + 128],
                                 woT_sb[:, c * D + eb * 512:(c) * D + (eb + 1) * 512],
                                 start=(c == 0), stop=(c == NCH - 1))
                        for c in range(NCH)][-1]))(),
                        deps=(tok_afm[par], tok_woT, yps_free[u]), dbg=f"ymm{i}_{eb}")
                    afm_free[par] = t_mm
                    sp2 = (i * 4 + eb) % 2
                    t_ev = P.v((lambda eb=eb, xp=xp, sp2=sp2, bank=bank: (lambda v: v.tensor_add(
                        y_stage[sp2][:], bank[:, :], x_res[xp][:, eb * 512:(eb + 1) * 512])))(),
                        deps=(t_mm, tok_xres[xp], yst_free[sp2]), dbg=f"yev{i}_{eb}")
                    yps_free[u] = t_ev
                    xres_free[xp] = t_ev
                    yst_free[sp2] = P.dma("ACT", (lambda i=i, eb=eb, sp2=sp2: (lambda a: a.dma_start(
                        out=y_out[i * 128:(i + 1) * 128, eb * 512:(eb + 1) * 512],
                        in_=y_stage[sp2][:])))(),
                        deps=(t_ev,), chan=f"yst{sp2}", dbg=f"yst{i}_{eb}")
                    tok_y_store.append(yst_free[sp2])

    tok_y_final = []
    if 4 in phases:
        w1b = OVERLAY("w1b", [128, NCH * HID], BF16, xn_sb, 0)
        w3b = OVERLAY("w3b", [128, NCH * HID], BF16, xn_sb, NCH * HID * 2)
        xq_sb = OVERLAY("xq_sb", [128, NCH * 512], BF16, xn_sb, 2 * NCH * HID * 2)
        H_sb = OVERLAY("H_sb", [128, NJ * 512], BF16, wT["q"][0], 0)
        w2str = [OVERLAY(f"w2str{i}", [128, NJ * 512], BF16, x_tile[0], i * NJ * 512 * 2)
                 for i in range(2)]
        gb_sb = OVERLAY("gb_sb", [128, 512], F32, elu_t1, 0)
        a_sb = OVERLAY("a_sb", [128, 512], F32, elu_t2, 0)
        s_sb = OVERLAY("s_sb", [128, 512], F32, elu_t3, 0)
        yin_sb = [OVERLAY(f"yin_sb{i}", [128, 512], F32, vc_sb, i * 2048) for i in range(2)] + \
                 [OVERLAY(f"yin_sb{2 + i}", [128, 512], F32, wT["v"][0], i * 2048) for i in range(2)]
        yst4_sb = [OVERLAY(f"yst4_sb{i}", [128, 512], F32, hb_sb[1], i * 2048) for i in range(2)] + \
                  [OVERLAY(f"yst4_sb{2 + i}", [128, 512], F32, wT["v"][1], i * 2048) for i in range(2)]
        grow_sb = OVERLAY("grow_sb", [1, 512], F32, mem_rhs[0], 0)

        p4gate = tuple(d for d in (
            prev_pe_last, last_eluadd, tok_hb_store[-1], tok_hb_store[-2],
            afm_free[0], afm_free[1], xres_free[0], xres_free[1],
            yst_free[0], yst_free[1], yps_free[0], yps_free[1]) if d is not None)

        ytile = [[tok_y_store[i * 4 + eb] for eb in range(4)] for i in range(NT128)]

        u1free = [bank01_free[0], bank01_free[1]]
        u3free = [yps_free[0], yps_free[1]]
        ybfree = [bank4_free, bank5_free, yps_free[0], yps_free[1]]
        gb_free = bank67_free[0]
        yin_free = [None, None, None, None]
        yst4_free = [None, None, None, None]
        YB4 = [PS[4], PS[5], PS[2], PS[3]]
        w2s_free = [None, None]
        w2c_free = [None, None]
        u_last = None
        w2_last = None
        gbmm_last = None
        prev_s = None
        prev_a = None
        prev_H = None

        for e in range(NEXP + 1):
            routed = e < NEXP
            w1src = w1_in[e] if routed else sw1_in
            w3src = w3_in[e] if routed else sw3_in
            w2src = w2_in[e] if routed else sw2_in
            cast_dep = (u_last,) if u_last is not None else p4gate
            t_w1c = P.dma("ACT", (lambda e=e: (lambda a: a.dma_start(
                out=w1b.rearrange("p (c j) -> p c j", j=HID),
                in_=w1bf9[e].rearrange("(c p) j -> p c j", p=128))))(),
                deps=cast_dep + (tok_ffncast[(0, e)],), chan="w1b", dbg=f"w1c{e}")
            t_w3c = P.dma("ACT", (lambda e=e: (lambda a: a.dma_start(
                out=w3b.rearrange("p (c j) -> p c j", j=HID),
                in_=w3bf9[e].rearrange("(c p) j -> p c j", p=128))))(),
                deps=cast_dep + (tok_ffncast[(1, e)],), chan="w3b", dbg=f"w3c{e}")
            t_w2c = tok_ffncast[(2, e)]
            for t5 in range(NT512):
                t_xq = P.dma("SP", (lambda t5=t5: (lambda sy: sy.dma_start(
                    out=xq_sb.rearrange("p (c w) -> p c w", w=512),
                    in_=xn_fm[:, :, t5 * 512:(t5 + 1) * 512].rearrange("c p w -> p c w"))))(),
                    deps=((u_last,) if u_last is not None else p4gate) + (tok_fmst[-1],),
                    chan="xq", dbg=f"xq{e}_{t5}")
                if routed:
                    t_gr = P.dma("SP", (lambda e=e, t5=t5: (lambda sy: sy.dma_start(
                        out=grow_sb[:], in_=gT_dram[e:e + 1, t5 * 512:(t5 + 1) * 512])))(),
                        deps=((gbmm_last,) if gbmm_last is not None else p4gate) +
                             (tok_g_store[-1], tok_g_store[-2]),
                        chan="grow", dbg=f"grow{e}_{t5}")
                    t_gbmm = P.pe((lambda: (lambda t: t.matmul(
                        PS[6][:, :], ones_f[:], grow_sb[:], start=True, stop=True)))(),
                        deps=(t_gr, gb_free, tok_ones_f), dbg=f"gbmm{e}_{t5}")
                    gbmm_last = t_gbmm
                    t_gb = P.v(lambda v: v.tensor_copy(gb_sb[:], PS[6][:, :]),
                               deps=(t_gbmm, prev_a), dbg=f"gbev{e}_{t5}")
                    gb_free = t_gb
                else:
                    t_gb = None
                u_toks = []
                for jc in range(NJ):
                    b1 = PS[jc % 2]
                    b3 = PS[2 + jc % 2]
                    t_u1 = P.pe((lambda jc=jc, b1=b1: (lambda t: [
                        t.matmul(b1[:, :],
                                 w1b[:, c * HID + jc * 128: c * HID + (jc + 1) * 128],
                                 xq_sb[:, c * 512:(c + 1) * 512],
                                 start=(c == 0), stop=(c == NCH - 1))
                        for c in range(NCH)][-1]))(),
                        deps=(t_w1c, t_xq, u1free[jc % 2]), dbg=f"u1_{e}_{t5}_{jc}")
                    t_u3 = P.pe((lambda jc=jc, b3=b3: (lambda t: [
                        t.matmul(b3[:, :],
                                 w3b[:, c * HID + jc * 128: c * HID + (jc + 1) * 128],
                                 xq_sb[:, c * 512:(c + 1) * 512],
                                 start=(c == 0), stop=(c == NCH - 1))
                        for c in range(NCH)][-1]))(),
                        deps=(t_w3c, t_xq, u3free[jc % 2]), dbg=f"u3_{e}_{t5}_{jc}")
                    u_last = t_u3
                    if routed:
                        t_a = P.v((lambda b1=b1: (lambda v: v.tensor_mul(a_sb[:], b1[:, :], gb_sb[:])))(),
                                  deps=(t_u1, t_gb, prev_s), dbg=f"a{e}_{t5}_{jc}")
                        prev_a = t_a
                        t_s = P.a(lambda a: a.activation(s_sb[:], a_sb[:], AF.Silu),
                                  deps=(t_a, prev_H), dbg=f"s{e}_{t5}_{jc}")
                        u1free[jc % 2] = t_a
                    else:
                        t_s = P.a((lambda b1=b1: (lambda a: a.activation(s_sb[:], b1[:, :], AF.Silu)))(),
                                  deps=(t_u1, prev_H), dbg=f"s{e}_{t5}_{jc}")
                        u1free[jc % 2] = t_s
                    prev_s = t_s
                    t_H = P.v((lambda jc=jc, b3=b3: (lambda v: v.tensor_mul(
                        H_sb[:, jc * 512:(jc + 1) * 512], s_sb[:], b3[:, :])))(),
                        deps=(t_s, t_u3) + ((w2_last,) if jc == 0 and w2_last is not None else ()),
                        dbg=f"H{e}_{t5}_{jc}")
                    prev_H = t_H
                    u3free[jc % 2] = t_H
                    u_toks.append(t_H)
                for eb in range(4):
                    sp = eb % 2
                    t_w2s = P.dma("SP", (lambda e=e, eb=eb, sp=sp: (lambda sy: sy.dma_start(
                        out=w2str[sp].rearrange("p (c w) -> p c w", w=512),
                        in_=w2bf9[e][:, eb * 512:(eb + 1) * 512].rearrange("(c p) w -> p c w", p=128))))(),
                        deps=(t_w2c,) + ((w2s_free[sp],) if w2s_free[sp] is not None else p4gate),
                        chan=f"w2s{sp}", dbg=f"w2s{e}_{t5}_{eb}")
                    for ts in range(4):
                        i = t5 * 4 + ts
                        u = (eb * 4 + ts) % 4
                        ybank = YB4[u]
                        t_ymm = P.pe((lambda ts=ts, sp=sp, ybank=ybank: (lambda t: [
                            t.matmul(ybank[:, :],
                                     H_sb[:, jc2 * 512 + ts * 128: jc2 * 512 + ts * 128 + 128],
                                     w2str[sp][:, jc2 * 512:(jc2 + 1) * 512],
                                     start=(jc2 == 0), stop=(jc2 == NJ - 1))
                            for jc2 in range(NJ)][-1]))(),
                            deps=(u_toks[-1], u_toks[-2], t_w2s, ybfree[u]),
                            dbg=f"ymm{e}_{t5}_{eb}_{ts}")
                        w2_last = t_ymm
                        w2s_free[sp] = t_ymm
                        yp = (eb * 4 + ts) % 4
                        t_yin = P.dma("POOL", (lambda i=i, eb=eb, yp=yp: (lambda a: a.dma_start(
                            out=yin_sb[yp][:], in_=y_out[i * 128:(i + 1) * 128,
                                                        eb * 512:(eb + 1) * 512])))(),
                            deps=(ytile[i][eb],) + ((yin_free[yp],) if yin_free[yp] is not None else ()),
                            chan=f"yin{yp}", dbg=f"yin{e}_{i}_{eb}")
                        sc_ap = (lambda i=i, e=e: g_res[:, i * NEXP + e:i * NEXP + e + 1])() \
                            if routed else 1.0
                        t_yev = P.v((lambda ybank=ybank, yp=yp, sc_ap=sc_ap: (lambda v: v.scalar_tensor_tensor(
                            out=yst4_sb[yp][:], in0=ybank[:, :], scalar=sc_ap,
                            in1=yin_sb[yp][:], op0=ALU.mult, op1=ALU.add)))(),
                            deps=(t_ymm, t_yin) +
                                 ((yst4_free[yp],) if yst4_free[yp] is not None else ()) +
                                 ((tok_g[i],) if routed else ()),
                            dbg=f"yev{e}_{i}_{eb}")
                        ybfree[u] = t_yev
                        yin_free[yp] = t_yev
                        t_yst = P.dma("ACT", (lambda i=i, eb=eb, yp=yp: (lambda a: a.dma_start(
                            out=y_out[i * 128:(i + 1) * 128, eb * 512:(eb + 1) * 512],
                            in_=yst4_sb[yp][:])))(),
                            deps=(t_yev,), chan=f"yo{yp}", dbg=f"yst{e}_{i}_{eb}")
                        yst4_free[yp] = t_yst
                        ytile[i][eb] = t_yst
                u3free = [ybfree[2], ybfree[3]]
        for row in ytile:
            tok_y_final.extend(row)

    final_deps = [tok_fmst[c] for c in range(NCH)] + [tok_g_store[i] for i in range(NT128)]
    final_deps += list(tok_xsum_b) + [tok_tri] + tok_hb_store + tok_y_store + tok_y_final
    P.sp(lambda s: s.nop(), deps=tuple(final_deps), dbg="final")

    from contextlib import ExitStack
    with ExitStack() as stack:
        emit = P.finalize(nc, stack)
        with nc.Block() as block:
            @block.tensor
            def _(t):
                emit("PE", t)

            @block.vector
            def _(v):
                emit("DVE", v)

            @block.scalar
            def _(a):
                emit("ACT", a)

            @block.gpsimd
            def _(g):
                emit("POOL", g)

            @block.sync
            def _(s):
                emit("SP", s)
    return nc



def kernel(**inputs):
    ncores = 8
    x = np.ascontiguousarray(inputs["L0_attn_hidden_states"])
    nc = build_nc()
    in_maps = []
    for i in range(ncores):
        shard = np.ascontiguousarray(x[2 * i:2 * i + 2].reshape(T, D))
        in_maps.append({
            "x": shard,
            "w_q": inputs["L0_attn_w_q"], "w_k": inputs["L0_attn_w_k"],
            "w_v": inputs["L0_attn_w_v"], "w_o": inputs["L0_attn_w_o"],
            "memories": inputs["L0_attn_memories"],
            "memory_norms": inputs["L0_attn_memory_norms"],
            "gate": inputs["L0_attn_gate"].reshape(1, NHEAD),
            "gate_w": inputs["L0_ffn_gate_w"],
            "expert_bias": inputs["L0_ffn_expert_bias"].reshape(1, NEXP),
            "w1": inputs["L0_ffn_w1"], "w2": inputs["L0_ffn_w2"], "w3": inputs["L0_ffn_w3"],
            "sw1": inputs["L0_ffn_sw1"], "sw2": inputs["L0_ffn_sw2"], "sw3": inputs["L0_ffn_sw3"],
        })
    res = run_bass_kernel_spmd(nc, in_maps, core_ids=list(range(ncores)))
    out = np.concatenate([r["y"].reshape(2, S, D) for r in res.results], axis=0)
    return out


# version 06
# speedup vs baseline: 1.3615x; 1.0856x over previous
import numpy as np
import concourse.bass as bass
import concourse.mybir as mybir
from concourse.bass_utils import run_bass_kernel_spmd

F32 = mybir.dt.float32
BF16 = mybir.dt.bfloat16
U32 = mybir.dt.uint32
AF = mybir.ActivationFunctionType
ALU = mybir.AluOpType
AX = mybir.AxisListType.X

NB = 2
S = 2048
T = NB * S
D = 2048
NCH = D // 128
NHEAD = 16
HDIM = 128
NMEM = 4
NCHK = S // 128
NEXP = 8
HID = 1792
NJ = HID // 128
NT128 = T // 128
NT512 = T // 512
EPS_ATT = 1e-6
EPS_LN = 1e-5
ATT_SCALE = HDIM ** -0.5

ENGINES = ("PE", "DVE", "ACT", "POOL", "SP")


class Op:
    __slots__ = ("eng", "fn", "deps", "is_dma", "chan", "idx", "tok_val", "need_tok", "dbg")

    def __init__(self, eng, fn, deps, is_dma, chan, idx, dbg):
        self.eng = eng
        self.fn = fn
        self.deps = deps
        self.is_dma = is_dma
        self.chan = chan
        self.idx = idx
        self.tok_val = None
        self.need_tok = False
        self.dbg = dbg


class Prog:

    def __init__(self):
        self.ops = {e: [] for e in ENGINES}
        self.chan_count = {}

    def add(self, eng, fn, deps=(), dma=False, chan=None, dbg=""):
        deps = tuple(d for d in deps if d is not None)
        for d in deps:
            assert isinstance(d, Op)
        op = Op(eng, fn, deps, dma, chan, len(self.ops[eng]), dbg)
        self.ops[eng].append(op)
        if dma:
            assert chan is not None, f"dma needs a channel: {dbg}"
            self.chan_count[chan] = self.chan_count.get(chan, 0) + 1
            op.tok_val = self.chan_count[chan] * 16
        return op

    def pe(self, fn, deps=(), dbg=""):
        return self.add("PE", fn, deps, dbg=dbg)

    def v(self, fn, deps=(), dbg=""):
        return self.add("DVE", fn, deps, dbg=dbg)

    def a(self, fn, deps=(), dbg=""):
        return self.add("ACT", fn, deps, dbg=dbg)

    def pool(self, fn, deps=(), dbg=""):
        return self.add("POOL", fn, deps, dbg=dbg)

    def sp(self, fn, deps=(), dbg=""):
        return self.add("SP", fn, deps, dbg=dbg)

    def dma(self, queue, fn, deps=(), chan=None, dbg=""):
        return self.add(queue, fn, deps, dma=True, chan=chan or dbg, dbg=dbg)

    def finalize(self, nc, stack):
        for e in ENGINES:
            for op in self.ops[e]:
                for d in op.deps:
                    if not d.is_dma:
                        d.need_tok = True
        for e in ENGINES:
            cnt = 0
            for op in self.ops[e]:
                if not op.is_dma and op.need_tok:
                    cnt += 1
                    op.tok_val = cnt
        tok_sem = {e: stack.enter_context(nc.semaphore(f"tok_{e}")) for e in ENGINES}
        dma_sem = {c: stack.enter_context(nc.semaphore(f"dc_{c}"))
                   for c in self.chan_count}

        def emit_engine(eng_name, eng_obj):
            waited = {}
            for op in self.ops[eng_name]:
                waits = {}
                for d in op.deps:
                    sem = dma_sem[d.chan] if d.is_dma else tok_sem[d.eng]
                    key = id(sem)
                    if d.tok_val is None:
                        raise AssertionError(f"dep without token: {d.eng}#{d.idx} {d.dbg}")
                    if waits.get(key, (None, -1))[1] < d.tok_val:
                        waits[key] = (sem, d.tok_val)
                for key, (sem, val) in sorted(waits.items(), key=lambda kv: kv[1][1]):
                    if waited.get(key, -1) < val:
                        eng_obj.wait_ge(sem, val)
                        waited[key] = val
                ins = op.fn(eng_obj)
                if op.is_dma:
                    ins.then_inc(dma_sem[op.chan], 16)
                elif op.need_tok:
                    ins.then_inc(tok_sem[eng_name], 1)
        return emit_engine


def build_nc(debug=False, phases=(1, 2, 3, 4), p2_heads=NHEAD, p2_stage=9):
    nc = bass.Bass("TRN2", target_bir_lowering=False, debug=False)

    x_in = nc.dram_tensor("x", [T, D], F32, kind="ExternalInput").ap()
    w_q = nc.dram_tensor("w_q", [D, D], F32, kind="ExternalInput").ap()
    w_k = nc.dram_tensor("w_k", [D, D], F32, kind="ExternalInput").ap()
    w_v = nc.dram_tensor("w_v", [D, D], F32, kind="ExternalInput").ap()
    w_o = nc.dram_tensor("w_o", [D, D], F32, kind="ExternalInput").ap()
    mem_in = nc.dram_tensor("memories", [NMEM, NHEAD, HDIM, HDIM], F32, kind="ExternalInput").ap()
    mn_in = nc.dram_tensor("memory_norms", [NMEM, NHEAD, HDIM], F32, kind="ExternalInput").ap()
    gate_in = nc.dram_tensor("gate", [1, NHEAD], F32, kind="ExternalInput").ap()
    gw_in = nc.dram_tensor("gate_w", [D, NEXP], F32, kind="ExternalInput").ap()
    bias_in = nc.dram_tensor("expert_bias", [1, NEXP], F32, kind="ExternalInput").ap()
    w1_in = nc.dram_tensor("w1", [NEXP, D, HID], F32, kind="ExternalInput").ap()
    w2_in = nc.dram_tensor("w2", [NEXP, HID, D], F32, kind="ExternalInput").ap()
    w3_in = nc.dram_tensor("w3", [NEXP, D, HID], F32, kind="ExternalInput").ap()
    sw1_in = nc.dram_tensor("sw1", [D, HID], F32, kind="ExternalInput").ap()
    sw2_in = nc.dram_tensor("sw2", [HID, D], F32, kind="ExternalInput").ap()
    sw3_in = nc.dram_tensor("sw3", [D, HID], F32, kind="ExternalInput").ap()

    y_out = nc.dram_tensor("y", [T, D], F32, kind="ExternalOutput").ap()

    scratch_kind = "ExternalOutput" if debug else "Internal"
    xn_tm = nc.dram_tensor("xn_tm", [T, D], BF16, kind=scratch_kind).ap()
    xn_fm = nc.dram_tensor("xn_fm", [NCH, 128, T], BF16, kind=scratch_kind).ap()
    attn_tm = nc.dram_tensor("attn_tm", [T, D], BF16, kind=scratch_kind).ap()
    gT_dram = nc.dram_tensor("gT_dram", [NEXP, T], F32, kind=scratch_kind).ap()
    wq_bf = nc.dram_tensor("wq_bf", [D, D], BF16, kind="Internal").ap()
    wk_bf = nc.dram_tensor("wk_bf", [D, D], BF16, kind="Internal").ap()
    wv_bf = nc.dram_tensor("wv_bf", [D, D], BF16, kind="Internal").ap()
    wo_bf = nc.dram_tensor("wo_bf", [D, D], BF16, kind="Internal").ap()
    woT_dram = nc.dram_tensor("woT_dram", [NCH, 128, D], BF16, kind="Internal").ap()
    w1bf9 = nc.dram_tensor("w1bf9", [NEXP + 1, D, HID], BF16, kind="Internal").ap()
    w3bf9 = nc.dram_tensor("w3bf9", [NEXP + 1, D, HID], BF16, kind="Internal").ap()
    w2bf9 = nc.dram_tensor("w2bf9", [NEXP + 1, HID, D], BF16, kind="Internal").ap()

    SBUF = lambda name, shape, dt=F32: nc.alloc_sbuf_tensor(name, shape, dt).ap()
    ident_bf = SBUF("ident_bf", [128, 128], BF16)
    ident_f = SBUF("ident_f", [128, 128], F32)
    tri_bf = SBUF("tri_bf", [128, 128], BF16)
    ones_bf = SBUF("ones_bf", [1, 128], BF16)
    ones_f = SBUF("ones_f", [1, 128], F32)
    g_res = SBUF("g_res", [128, NT128 * NEXP], F32)
    iota8 = SBUF("iota8", [128, NEXP], F32)
    bias_b = SBUF("bias_b", [128, NEXP], F32)
    gwf_sb = SBUF("gwf_sb", [128, NCH * NEXP], F32)
    ncsb = SBUF("ncsb", [128, NEXP], F32)
    xsum_bf = SBUF("xsum_bf", [128, NB * NCH], BF16)
    xsum_f = SBUF("xsum_f", [128, NB * NCH], F32)

    x_tile = [SBUF(f"x_tile{i}", [128, D]) for i in range(2)]
    sq_scr = SBUF("sq_scr", [128, D])
    xn_bf = [SBUF(f"xn_bf{i}", [128, D], BF16) for i in range(2)]
    stats = {k: [SBUF(f"st_{k}{i}", [128, 1]) for i in range(2)]
             for k in ("rsum", "ss", "mean", "ex2", "vneg", "va", "sq", "rq",
                       "nt", "s1", "rr", "racc", "nmr")}

    rt = {k: [SBUF(f"rt_{k}{i}", [128, NEXP], U32 if k == "ix" else F32) for i in range(2)]
          for k in ("ex", "sc", "b", "mx", "ix", "ixf", "m1", "m2",
                    "lgc", "u", "kf", "f", "pp", "q")}
    rt_ki = [SBUF(f"rt_ki{i}", [128, NEXP], mybir.dt.int32) for i in range(2)]
    rt_nmax = [SBUF(f"rt_nmax{i}", [128, 1]) for i in range(2)]
    rt_rsum = [SBUF(f"rt_rsum{i}", [128, 1]) for i in range(2)]
    rt_rden = [SBUF(f"rt_rden{i}", [128, 1]) for i in range(2)]
    gt_stage = [SBUF(f"gt_stage{i}", [NEXP, 128], F32) for i in range(2)]

    xn_sb = SBUF("xn_sb", [128, NCH * T], BF16)

    def OVERLAY(name, shape, dt, base_ap, byte_off=0):
        addr = nc.lookup_mloc(base_ap.tensor).addr + byte_off
        return nc.alloc_sbuf_tensor_at(name, shape, dt, offset=addr).ap()

    wT = {m: [SBUF(f"w{m}T{i}", [128, D], BF16) for i in range(2)] for m in "qkv"}
    allones_f = OVERLAY("allones_f", [128, 128], F32, wT["q"][0], 0)
    xt_sb = [OVERLAY(f"xt_sb{i}", [128, 128], F32, wT["q"][0], 512 + i * 512)
             for i in range(2)]
    wo_stg = [OVERLAY(f"wo_stg{i}", [128, 128], BF16, gt_stage[i], 0)
              for i in range(2)]
    mem_rhs = [SBUF(f"mem_rhs{i}", [128, NMEM * HDIM], BF16) for i in range(2)]
    mn_raw = [SBUF(f"mn_raw{i}", [NMEM, HDIM], BF16) for i in range(2)]
    mn_t = [SBUF(f"mn_t{i}", [128, NMEM], BF16) for i in range(2)]
    qp_sb = OVERLAY("qp_sb", [128, S], BF16, x_tile[0])
    kp_sb = OVERLAY("kp_sb", [128, S], BF16, x_tile[1])
    kpT_sb = OVERLAY("kpT_sb", [128, S], BF16, sq_scr)
    vfm_sb = OVERLAY("vfm_sb", [128, S], BF16, sq_scr, 4096)
    vc_sb = SBUF("vc_sb", [128, NCHK * (HDIM + 1)], BF16)
    elu_t1 = SBUF("elu_t1", [128, 512], F32)
    elu_t2 = SBUF("elu_t2", [128, 512], F32)
    elu_t3 = SBUF("elu_t3", [128, 512], F32)
    R_f = [SBUF(f"R_f{i}", [128, HDIM + 1], F32) for i in range(2)]
    R_bf = [SBUF(f"R_bf{i}", [128, HDIM + 1], BF16) for i in range(2)]
    sT_sb = SBUF("sT_sb", [128, 128], BF16)
    hb_sb = [OVERLAY(f"hb_sb{i}", [128, S], BF16, xn_bf[i]) for i in range(2)]
    mn_rows = OVERLAY("mn_rows", [NMEM, NHEAD * HDIM], BF16, qp_sb)
    macc = [SBUF(f"macc{i}", [128, 128], F32) for i in range(2)]
    cm64 = SBUF("cm64", [128, NCHK * NMEM], F32)
    dclip = SBUF("dclip", [128, 1])
    rden_sb = SBUF("rden_sb", [128, 1])
    rdeno = SBUF("rdeno", [128, 1])
    ho_f = SBUF("ho_f", [128, 128], F32)
    wgb_sb = SBUF("wgb_sb", [128, NMEM])
    omgb_sb = SBUF("omgb_sb", [128, 1])
    qbarT_bf = SBUF("qbarT_bf", [128, 1], BF16)
    mnsum = SBUF("mnsum", [NMEM, 1], F32)
    mnsum_t = SBUF("mnsum_t", [1, NMEM], F32)
    memmask = SBUF("memmask", [1, NMEM], F32)
    gate_t = SBUF("gate_t", [1, NHEAD], F32)
    gate_e = SBUF("gate_e", [1, NHEAD], F32)
    gate_g = SBUF("gate_g", [1, NHEAD], F32)
    gate_omg = SBUF("gate_omg", [1, NHEAD], F32)
    rel_s = SBUF("rel_s", [1, NMEM], F32)
    rel_nm = SBUF("rel_nm", [1, 1], F32)
    rel_e = SBUF("rel_e", [1, NMEM], F32)
    rel_den = SBUF("rel_den", [1, 1], F32)
    rel_rd = SBUF("rel_rd", [1, 1], F32)
    rel_w = SBUF("rel_w", [1, NMEM], F32)
    wg_row = SBUF("wg_row", [1, NMEM], F32)
    wg_bf = SBUF("wg_bf", [1, NMEM], BF16)
    omg_bf = SBUF("omg_bf", [1, 1], BF16)

    PS = [nc.alloc_psum_tensor(f"psum{i}", [128, 512], F32).ap() for i in range(8)]
    PS5B = PS[5].bitcast(BF16)

    P = Prog()

    t0 = P.pool(lambda g: g.memset(tri_bf[:], 1.0), dbg="tri0")
    tok_tri = P.pool(lambda g: g.affine_select(
        out=tri_bf[:], in_=tri_bf[:], compare_op=ALU.is_ge, fill=0.0,
        base=0, pattern=[[1, 128]], channel_multiplier=-1), deps=(t0,), dbg="tri")
    t0 = P.pool(lambda g: g.memset(ident_bf[:], 1.0), dbg="idb0")
    tok_id_bf = P.pool(lambda g: g.affine_select(
        out=ident_bf[:], in_=ident_bf[:], compare_op=ALU.is_equal, fill=0.0,
        base=0, pattern=[[1, 128]], channel_multiplier=-1), deps=(t0,), dbg="idb")
    t0 = P.pool(lambda g: g.memset(ident_f[:], 1.0), dbg="idf0")
    tok_id = P.pool(lambda g: g.affine_select(
        out=ident_f[:], in_=ident_f[:], compare_op=ALU.is_equal, fill=0.0,
        base=0, pattern=[[1, 128]], channel_multiplier=-1), deps=(t0,), dbg="idf")
    tok_ones_bf = P.pool(lambda g: g.memset(ones_bf[:], 1.0), dbg="ones_bf")
    tok_ones_f = P.pool(lambda g: g.memset(ones_f[:], 1.0), dbg="ones_f")
    tok_misc = P.pool(lambda g: g.iota(iota8[:], pattern=[[1, NEXP]], base=0,
                                       channel_multiplier=0,
                                       allow_small_or_imprecise_dtypes=True), dbg="iota")

    tok_bias = P.dma("SP", lambda s: s.dma_start(
        out=bias_b[:], in_=bias_in.to_broadcast([128, NEXP])), chan="bias", dbg="bias")

    tok_gwf = P.dma("POOL", lambda g: g.dma_start(
        out=gwf_sb.rearrange("p (c e) -> p c e", e=NEXP),
        in_=gw_in.rearrange("(c p) e -> p c e", p=128)), chan="gw", dbg="gwf")
    tok_ones128 = P.pool(lambda g: g.memset(allones_f[:], 1.0), dbg="ones128")

    def mk_csum():
        def f(t):
            ins = None
            for c in range(NCH):
                ins = t.matmul(PS[7][:, 0:NEXP], allones_f[:, :],
                               gwf_sb[:, c * NEXP:(c + 1) * NEXP],
                               start=(c == 0), stop=(c == NCH - 1))
            return ins
        return f
    tok_csmm = P.pe(mk_csum(), deps=(tok_gwf, tok_ones128), dbg="csmm")
    tok_ncsb = P.v(lambda v: v.tensor_scalar_mul(ncsb[:], PS[7][:, 0:NEXP], -1.0),
                   deps=(tok_csmm,), dbg="ncsb")

    LOG2E = 1.4426950408889634
    MAGIC = 12582912.0
    LN2_HI = 0.693359375
    LN2_LO = -2.12194440054690581e-4
    EXPC = [1.0 / 720, 1.0 / 120, 1.0 / 24, 1.0 / 6, 0.5, 1.0, 1.0]

    tok_xn_store = [None] * NT128
    tok_xnbf = [None] * NT128
    tok_ztr = [None] * NT128
    tok_lgs = [None] * NT128
    tok_g = [None] * NT128
    tok_g_store = [None] * NT128
    rt_last_read = [None, tok_ncsb]
    gt_last_evac = [None, None]
    zbank_free = [None, None]
    xt_free = [None, None]
    def rec_gt(i):
        p = i % 2
        gt_ps = PS[6 + p][0:NEXP, 8:136]
        t_gt = P.pe((lambda i=i, gt_ps=gt_ps: (lambda t: t.transpose(
            gt_ps, g_res[:, i * NEXP:(i + 1) * NEXP], ident_f[:])))(),
            deps=(tok_g[i], tok_id) + ((gt_last_evac[p],) if gt_last_evac[p] is not None else ()),
            dbg=f"gtT{i}")
        t_gtev = P.v((lambda p=p, gt_ps=gt_ps: (lambda v: v.tensor_copy(gt_stage[p][:], gt_ps)))(),
                     deps=(t_gt,) + ((tok_g_store[i - 2],) if i >= 2 else ()), dbg=f"gtev{i}")
        gt_last_evac[p] = t_gtev
        rt_last_read[p] = t_gtev
        tok_g_store[i] = P.dma("ACT", (lambda i=i, p=p: (lambda a: a.dma_start(
            out=gT_dram[:, i * 128:(i + 1) * 128], in_=gt_stage[p][:])))(),
            deps=(t_gtev,), chan=f"gtst{p}", dbg=f"gtst{i}")

    for i in range(NT128):
        p = i % 2
        ld_deps = (tok_xnbf[i - 2], tok_ztr[i - 2], tok_lgs[i - 2]) if i >= 2 else ()
        t_ld = P.dma("SP", (lambda i=i, p=p: (lambda s: s.dma_start(
            out=x_tile[p][:], in_=x_in[i * 128:(i + 1) * 128, :])))(),
            deps=ld_deps, chan=f"xl{p}", dbg=f"xld{i}")
        t_rsum = P.v((lambda p=p: (lambda v: v.reduce_sum(stats["rsum"][p][:], x_tile[p][:], axis=AX)))(),
                     deps=(t_ld,), dbg=f"rsum{i}")
        t_sq = P.a((lambda p=p: (lambda a: a.activation(sq_scr[:], x_tile[p][:], AF.Square,
                                                        accum_out=stats["ss"][p][:])))(),
                   deps=(t_ld,), dbg=f"sq{i}")
        t_mean = P.a((lambda p=p: (lambda a: a.activation(stats["mean"][p][:], stats["rsum"][p][:],
                                                          AF.Copy, scale=1.0 / D)))(),
                     deps=(t_rsum,), dbg=f"mean{i}")
        t_ex2 = P.a((lambda p=p: (lambda a: a.activation(stats["ex2"][p][:], stats["ss"][p][:],
                                                         AF.Copy, scale=1.0 / D)))(),
                    deps=(t_sq,), dbg=f"ex2{i}")
        t_vneg = P.v((lambda p=p: (lambda v: v.scalar_tensor_tensor(
            out=stats["vneg"][p][:], in0=stats["mean"][p][:], scalar=stats["mean"][p][:, 0:1],
            in1=stats["ex2"][p][:], op0=ALU.mult, op1=ALU.subtract)))(),
            deps=(t_mean, t_ex2), dbg=f"vneg{i}")
        t_va = P.v((lambda p=p: (lambda v: v.tensor_scalar(
            out=stats["va"][p][:], in0=stats["vneg"][p][:], scalar1=-1.0,
            scalar2=EPS_LN, op0=ALU.mult, op1=ALU.add)))(),
            deps=(t_vneg,), dbg=f"va{i}")
        t_sqn = P.a((lambda p=p: (lambda a: a.activation(stats["sq"][p][:],
            stats["va"][p][:], AF.Sqrt)))(), deps=(t_va,), dbg=f"sqn{i}")
        t_rq = P.v((lambda p=p: (lambda v: v.reciprocal(stats["rq"][p][:],
            stats["sq"][p][:])))(), deps=(t_sqn,), dbg=f"rq{i}")
        t_nt = P.v((lambda p=p: (lambda v: v.tensor_mul(stats["nt"][p][:],
            stats["va"][p][:], stats["rq"][p][:])))(), deps=(t_rq,), dbg=f"nt{i}")
        t_s1 = P.v((lambda p=p: (lambda v: v.tensor_add(stats["s1"][p][:],
            stats["sq"][p][:], stats["nt"][p][:])))(), deps=(t_nt,), dbg=f"s1{i}")
        t_rr = P.v((lambda p=p: (lambda v: v.reciprocal(stats["rr"][p][:],
            stats["s1"][p][:])))(), deps=(t_s1,), dbg=f"rr{i}")
        t_racc = P.v((lambda p=p: (lambda v: v.tensor_scalar_mul(stats["racc"][p][:],
            stats["rr"][p][:], 2.0)))(), deps=(t_rr,), dbg=f"racc{i}")
        t_nmr = P.v((lambda p=p: (lambda v: v.scalar_tensor_tensor(
            out=stats["nmr"][p][:], in0=stats["mean"][p][:], scalar=-1.0,
            in1=stats["racc"][p][:], op0=ALU.mult, op1=ALU.mult)))(),
            deps=(t_racc,), dbg=f"nmr{i}")
        t_xn = P.a((lambda p=p: (lambda a: a.activation(xn_bf[p][:], x_tile[p][:], AF.Identity,
                                                        scale=stats["racc"][p][:, 0:1],
                                                        bias=stats["nmr"][p][:, 0:1])))(),
                   deps=(t_nmr,) + ((tok_xn_store[i - 2],) if i >= 2 else ()), dbg=f"xn{i}")
        tok_xnbf[i] = t_xn
        tok_xn_store[i] = P.dma("ACT", (lambda i=i, p=p: (lambda a: a.dma_start(
            out=xn_tm[i * 128:(i + 1) * 128, :], in_=xn_bf[p][:])))(),
            deps=(t_xn,), chan=f"xnst{p}", dbg=f"xnst{i}")

        lg = PS[6 + p]
        tT = [None] * NCH
        tE = [None] * NCH
        tM = [None] * NCH

        def rec_T(c, i=i, p=p, t_ld=t_ld):
            b = c % 2
            deps = (t_ld, tok_id) + ((zbank_free[b],) if zbank_free[b] else ())
            tT[c] = P.pe((lambda c=c, p=p, b=b: (lambda t: t.transpose(
                PS[b][:, 0:128], x_tile[p][:, c * 128:(c + 1) * 128], ident_f[:])))(),
                deps=deps, dbg=f"zT{i}_{c}")

        def rec_E(c, i=i):
            b = c % 2
            deps = (tT[c],) + ((xt_free[b],) if xt_free[b] else ())
            tE[c] = P.a((lambda b=b: (lambda a: a.copy(
                xt_sb[b][:], PS[b][:, 0:128])))(), deps=deps, dbg=f"zE{i}_{c}")
            zbank_free[b] = tE[c]

        def rec_M(c, i=i, p=p, lg=lg):
            b = c % 2
            deps = (tE[c], tok_gwf)
            if c == 0 and rt_last_read[p] is not None:
                deps = deps + (rt_last_read[p],)
            tM[c] = P.pe((lambda c=c, b=b, lg=lg: (lambda t: t.matmul(
                lg[:, 0:NEXP], xt_sb[b][:], gwf_sb[:, c * NEXP:(c + 1) * NEXP],
                start=(c == 0), stop=(c == NCH - 1))))(), deps=deps, dbg=f"zM{i}_{c}")
            xt_free[b] = tM[c]

        rec_T(0); rec_E(0); rec_T(1); rec_E(1)
        for c in range(NCH):
            rec_M(c)
            if c + 2 < NCH:
                rec_T(c + 2)
                rec_E(c + 2)
        tok_ztr[i] = tT[NCH - 1]

        t_lgc = P.v((lambda p=p, lg=lg: (lambda v: v.scalar_tensor_tensor(
            out=rt["lgc"][p][:], in0=ncsb[:], scalar=stats["mean"][p][:, 0:1],
            in1=lg[:, 0:NEXP], op0=ALU.mult, op1=ALU.add)))(),
            deps=(tM[NCH - 1], t_mean, tok_ncsb), dbg=f"lgc{i}")
        t_lgs = P.v((lambda p=p: (lambda v: v.tensor_single_scalar(
            out=rt["lgc"][p][:], in_=rt["lgc"][p][:],
            scalar=stats["racc"][p][:, 0:1], op=ALU.mult)))(),
            deps=(t_lgc, t_racc), dbg=f"lgs{i}")
        tok_lgs[i] = t_lgs
        rt_last_read[p] = t_lgc

        t_nmax = P.v((lambda p=p: (lambda v: v.reduce_max(
            rt_nmax[p][:], rt["lgc"][p][:], axis=AX, negate=True)))(),
            deps=(t_lgs,), dbg=f"rtnmax{i}")
        t_u = P.v((lambda p=p: (lambda v: v.tensor_single_scalar(
            out=rt["u"][p][:], in_=rt["lgc"][p][:], scalar=rt_nmax[p][:, 0:1],
            op=ALU.add)))(), deps=(t_nmax,), dbg=f"rtu{i}")
        t_e1 = P.v((lambda p=p: (lambda v: v.tensor_scalar_mul(
            rt["kf"][p][:], rt["u"][p][:], LOG2E)))(), deps=(t_u,), dbg=f"e1_{i}")
        t_e2 = P.v((lambda p=p: (lambda v: v.tensor_scalar(
            out=rt["kf"][p][:], in0=rt["kf"][p][:], scalar1=MAGIC, scalar2=-MAGIC,
            op0=ALU.add, op1=ALU.add)))(), deps=(t_e1,), dbg=f"e2_{i}")
        t_e3 = P.v((lambda p=p: (lambda v: v.scalar_tensor_tensor(
            out=rt["f"][p][:], in0=rt["kf"][p][:], scalar=-LN2_HI,
            in1=rt["u"][p][:], op0=ALU.mult, op1=ALU.add)))(), deps=(t_e2,), dbg=f"e3_{i}")
        t_e4 = P.v((lambda p=p: (lambda v: v.scalar_tensor_tensor(
            out=rt["f"][p][:], in0=rt["kf"][p][:], scalar=-LN2_LO,
            in1=rt["f"][p][:], op0=ALU.mult, op1=ALU.add)))(), deps=(t_e3,), dbg=f"e4_{i}")
        t_prev = P.v((lambda p=p: (lambda v: v.tensor_scalar(
            out=rt["pp"][p][:], in0=rt["f"][p][:], scalar1=EXPC[0], scalar2=EXPC[1],
            op0=ALU.mult, op1=ALU.add)))(), deps=(t_e4,), dbg=f"ep0_{i}")
        for hi, cc in enumerate(EXPC[2:]):
            t_qm = P.v((lambda p=p: (lambda v: v.tensor_mul(
                rt["q"][p][:], rt["pp"][p][:], rt["f"][p][:])))(),
                deps=(t_prev,), dbg=f"eq{hi}_{i}")
            t_prev = P.v((lambda p=p, cc=cc: (lambda v: v.tensor_scalar_add(
                rt["pp"][p][:], rt["q"][p][:], cc)))(),
                deps=(t_qm,), dbg=f"ep{hi + 1}_{i}")
        t_ki = P.v((lambda p=p: (lambda v: v.tensor_copy(rt_ki[p][:], rt["kf"][p][:])))(),
                   deps=(t_e4,), dbg=f"eki{i}")
        t_k2 = P.v((lambda p=p: (lambda v: v.tensor_scalar_add(rt_ki[p][:], rt_ki[p][:], 127)))(),
                   deps=(t_ki,), dbg=f"ek2_{i}")
        t_k3 = P.v((lambda p=p: (lambda v: v.tensor_single_scalar(
            out=rt_ki[p][:], in_=rt_ki[p][:], scalar=23, op=ALU.logical_shift_left)))(),
            deps=(t_k2,), dbg=f"ek3_{i}")
        t_ex = P.v((lambda p=p: (lambda v: v.tensor_mul(
            rt["ex"][p][:], rt["pp"][p][:], rt_ki[p].bitcast(F32)[:])))(),
            deps=(t_prev, t_k3), dbg=f"rtex{i}")
        t_rs = P.v((lambda p=p: (lambda v: v.reduce_sum(rt_rsum[p][:], rt["ex"][p][:], axis=AX)))(),
                   deps=(t_ex,), dbg=f"rtrs{i}")
        t_rd = P.v((lambda p=p: (lambda v: v.reciprocal(rt_rden[p][:], rt_rsum[p][:])))(),
                   deps=(t_rs,), dbg=f"rtrd{i}")
        t_sc = P.v((lambda p=p: (lambda v: v.tensor_single_scalar(
            out=rt["sc"][p][:], in_=rt["ex"][p][:], scalar=rt_rden[p][:, 0:1], op=ALU.mult)))(),
            deps=(t_rd,), dbg=f"rtsc{i}")
        t_b = P.v((lambda p=p: (lambda v: v.tensor_add(rt["b"][p][:], rt["sc"][p][:], bias_b[:])))(),
                  deps=(t_sc, tok_bias), dbg=f"rtb{i}")
        t_mx = P.v((lambda p=p: (lambda v: v.max(out=rt["mx"][p][:], in_=rt["b"][p][:])))(),
                   deps=(t_b,), dbg=f"rtmx{i}")
        t_ix = P.v((lambda p=p: (lambda v: v.max_index(
            out=rt["ix"][p][:], in_max=rt["mx"][p][:], in_values=rt["b"][p][:])))(),
            deps=(t_mx,), dbg=f"rtix{i}")
        t_ixf = P.v((lambda p=p: (lambda v: v.tensor_copy(rt["ixf"][p][:], rt["ix"][p][:])))(),
                    deps=(t_ix,), dbg=f"rtixf{i}")
        t_m1 = P.v((lambda p=p: (lambda v: v.tensor_single_scalar(
            out=rt["m1"][p][:], in_=iota8[:], scalar=rt["ixf"][p][:, 0:1], op=ALU.is_equal)))(),
            deps=(t_ixf, tok_misc), dbg=f"rtm1_{i}")
        t_m2 = P.v((lambda p=p: (lambda v: v.tensor_single_scalar(
            out=rt["m2"][p][:], in_=iota8[:], scalar=rt["ixf"][p][:, 1:2], op=ALU.is_equal)))(),
            deps=(t_ixf,), dbg=f"rtm2_{i}")
        t_m12 = P.v((lambda p=p: (lambda v: v.tensor_add(rt["m1"][p][:], rt["m1"][p][:], rt["m2"][p][:])))(),
                    deps=(t_m1, t_m2), dbg=f"rtm12_{i}")
        tok_g[i] = P.v((lambda i=i, p=p: (lambda v: v.tensor_mul(
            g_res[:, i * NEXP:(i + 1) * NEXP], rt["m1"][p][:], rt["sc"][p][:])))(),
            deps=(t_m12,), dbg=f"rtg{i}")
        if i >= 1:
            rec_gt(i - 1)
    rec_gt(NT128 - 1)

    tok_tr_b = [[None] * NCH for _ in range(NB)]
    tok_fmst = [None] * NCH
    for b in range(NB):
        half_dep = tuple(tok_xn_store[b * (NT128 // NB):(b + 1) * (NT128 // NB)])
        for c in range(NCH):
            tok_tr_b[b][c] = P.dma("SP", (lambda b=b, c=c: (lambda s: s.dma_start(
                out=xn_sb[:, c * T + b * S: c * T + (b + 1) * S],
                in_=xn_tm[b * S:(b + 1) * S, c * 128:(c + 1) * 128],
                transpose=True)))(),
                deps=half_dep, chan=f"tr{b}", dbg=f"tr{b}_{c}")
    tr_half = [tuple(tok_tr_b[b]) for b in range(NB)]
    tr_all = tr_half[0] + tr_half[1]
    for c in range(NCH):
        tok_fmst[c] = P.dma("ACT", (lambda c=c: (lambda a: a.dma_start(
            out=xn_fm[c, :, :], in_=xn_sb[:, c * T:(c + 1) * T])))(),
            deps=tr_all, chan="fmst", dbg=f"fmst{c}")

    tok_xsum = []
    for b in range(NB):
        for c in range(NCH):
            tok_xsum.append(P.v((lambda b=b, c=c: (lambda v: v.reduce_sum(
                xsum_f[:, (b * NCH + c):(b * NCH + c + 1)],
                xn_sb[:, c * T + b * S: c * T + (b + 1) * S], axis=AX)))(),
                deps=tr_half[b], dbg=f"xsum{b}_{c}"))
    tok_xsum_b = [None] * NB
    for b in range(NB):
        tok_xsum_b[b] = P.v((lambda b=b: (lambda v: v.tensor_scalar_mul(
            xsum_bf[:, b * NCH:(b + 1) * NCH], xsum_f[:, b * NCH:(b + 1) * NCH],
            1.0 / S)))(),
            deps=tuple(tok_xsum[b * NCH:(b + 1) * NCH]), dbg=f"xsum_bf{b}")
    tok_xsum_bf = tok_xsum_b[-1]

    tok_hb_store = []
    if 2 in phases:
        tok_wcast = {}
        for nm, src, dst in (("q", w_q, wq_bf), ("k", w_k, wk_bf), ("v", w_v, wv_bf),
                             ("o", w_o, wo_bf)):
            tok_wcast[nm] = P.dma("POOL", (lambda src=src, dst=dst: (
                lambda g: g.dma_start(out=dst[:, :], in_=src[:, :])))(),
                chan=f"wc{nm}", dbg=f"wcast{nm}")
        t_mnr = P.dma("POOL", lambda g: g.dma_start(
            out=mn_rows[:], in_=mn_in.rearrange("m h d -> m (h d)")),
            deps=(tok_ztr[NT128 - 2], tok_xnbf[NT128 - 2], tok_lgs[NT128 - 2]),
            chan="mnr", dbg="mnrows")
        t_mns = P.v(lambda v: v.reduce_sum(mnsum[:], mn_rows[:], axis=AX),
                    deps=(t_mnr,), dbg="mnsum")
        t_mnT = P.pe(lambda t: t.transpose(PS[7][0:1, 8:12], mnsum[:], ident_f[0:NMEM, 0:NMEM]),
                     deps=(t_mns, tok_id, rt_last_read[1]), dbg="mnT")
        t_mnev = P.v(lambda v: v.tensor_copy(mnsum_t[:], PS[7][0:1, 8:12]),
                     deps=(t_mnT,), dbg="mnev")
        t_mask = P.v(lambda v: v.tensor_single_scalar(out=memmask[:], in_=mnsum_t[:],
                                                      scalar=1e-6, op=ALU.is_ge),
                     deps=(t_mnev,), dbg="memmask")
        t_gld = P.dma("POOL", lambda g: g.dma_start(out=gate_t[:], in_=gate_in[:, :]),
                      chan="gld", dbg="gateld")
        t_ge = P.a(lambda a: a.activation(gate_e[:], gate_t[:], AF.Exp, scale=-1.0),
                   deps=(t_gld,), dbg="gate_e")
        t_g1 = P.v(lambda v: v.tensor_scalar_add(gate_e[:], gate_e[:], 1.0),
                   deps=(t_ge,), dbg="gate_1p")
        t_gg = P.v(lambda v: v.reciprocal(gate_g[:], gate_e[:]), deps=(t_g1,), dbg="gate_g")
        t_gomg = P.v(lambda v: v.tensor_scalar(out=gate_omg[:], in0=gate_g[:], scalar1=-1.0,
                                               scalar2=1.0, op0=ALU.mult, op1=ALU.add),
                     deps=(t_gg,), dbg="gate_omg")

        tok_ffncast = {}
        wot_s = [None] * 256
        wT_free = [None, None]
        mem_free = [None, None]
        hb_free = [None, None]
        tok_mem = [None, None]
        tok_mnt = [None, None]
        tok_wT = [None, None]
        prep_chain = None
        bank67_free_init_mnev = True
        prev_pe_last = None
        bank01_free = [zbank_free[0], zbank_free[1]]
        bank23_free = [None, None]
        bank4_free = None
        bank5_free = None
        bank67_free = [rt_last_read[0], t_mnev]
        amv_sl = [elu_t1, elu_t2]
        amv_slot_free = [None, None]
        amv_free = None
        last_eluadd = None
        tok_vones = P.v(lambda v: v.memset(
            vc_sb.rearrange("p (n e) -> p n e", e=HDIM + 1)[:, :, HDIM:HDIM + 1], 1.0),
            dbg="vones")

        for h in range(p2_heads):
            hp = h % 2
            wdep = (wT_free[hp],)
            if h == 0:
                wdep = wdep + (xt_free[0], xt_free[1], tok_csmm)
            toks = []
            for nm in "qkv":
                for c in range(NCH):
                    toks.append(P.dma("SP", (lambda nm=nm, c=c, h=h, hp=hp: (lambda s: s.dma_start(
                        out=wT[nm][hp][:, c * 128:(c + 1) * 128],
                        in_=(wq_bf if nm == "q" else wk_bf if nm == "k" else wv_bf)[
                            h * 128:(h + 1) * 128, c * 128:(c + 1) * 128],
                        transpose=True)))(),
                        deps=wdep + (tok_wcast[nm],), chan=f"wT{hp}", dbg=f"wT{nm}{h}_{c}"))
            tok_wT[hp] = toks[-1]
            tok_mem[hp] = P.dma("POOL", (lambda h=h, hp=hp: (lambda g: g.dma_start(
                out=mem_rhs[hp].rearrange("p (m e) -> p m e", m=NMEM),
                in_=mem_in[:, h, :, :].rearrange("m d e -> d m e"))))(),
                deps=(mem_free[hp],), chan=f"mem{hp}", dbg=f"memld{h}")
            t_mnld = P.dma("POOL", (lambda h=h, hp=hp: (lambda g: g.dma_start(
                out=mn_raw[hp][:, :], in_=mn_in[:, h, :])))(),
                deps=(mem_free[hp],), chan=f"mnld{hp}", dbg=f"mnld{h}")
            stage0 = PS[0].bitcast(BF16)
            t_mntT = P.pe((lambda hp=hp, stage0=stage0: (lambda t: t.transpose(
                stage0[:, 896:896 + NMEM],
                mn_raw[hp][:, :], ident_bf[0:NMEM, 0:NMEM])))(),
                deps=(t_mnld, tok_id_bf, bank01_free[0]), dbg=f"mnT{h}")
            tok_mnt[hp] = P.a((lambda hp=hp, stage0=stage0: (lambda a: a.copy(
                mn_t[hp][:], stage0[:, 896:896 + NMEM])))(),
                deps=(t_mntT,), dbg=f"mnTev{h}")
            bank01_free[0] = tok_mnt[hp]

            if p2_stage < 2:
                continue
            for b in range(NB):
                hbp = (h * NB + b) % 2
                u_idx = h * NB + b
                if u_idx < 27:
                    kind, ei = divmod(u_idx, NEXP + 1)
                    csrc = ([w1_in[ei] if ei < NEXP else sw1_in,
                             w3_in[ei] if ei < NEXP else sw3_in,
                             w2_in[ei] if ei < NEXP else sw2_in][kind])
                    cdst = [w1bf9, w3bf9, w2bf9][kind]
                    tok_ffncast[(kind, ei)] = P.dma("POOL", (lambda csrc=csrc, cdst=cdst, ei=ei:
                        (lambda g: g.dma_start(out=cdst[ei], in_=csrc[:, :])))(),
                        chan="ffnc", dbg=f"ffnc{u_idx}")
                t_qbar = P.pe((lambda hp=hp, b=b: (lambda t: [
                    t.matmul(PS[5][:, 0:1], wT["q"][hp][:, c * 128:(c + 1) * 128],
                             xsum_bf[:, (b * NCH + c):(b * NCH + c + 1)],
                             start=(c == 0), stop=(c == NCH - 1))
                    for c in range(NCH)][-1]))(),
                    deps=tuple(d for d in (tok_wT[hp], tok_xsum_b[b], bank5_free) if d),
                    dbg=f"qbar{h}_{b}")
                t_qbev = P.v(lambda v: v.tensor_copy(qbarT_bf[:], PS[5][:, 0:1]),
                             deps=(t_qbar,), dbg=f"qbev{h}_{b}")
                t_rel = P.pe((lambda hp=hp: (lambda t: t.matmul(
                    PS[5][0:1, 1:5], qbarT_bf[:], mn_t[hp][:], start=True, stop=True)))(),
                    deps=(t_qbev, tok_mnt[hp]), dbg=f"rel{h}_{b}")
                t_rsc = P.v(lambda v: v.tensor_scalar_mul(rel_s[:], PS[5][0:1, 1:5], ATT_SCALE),
                            deps=(t_rel,), dbg=f"relsc{h}_{b}")
                t_rnm = P.v(lambda v: v.reduce_max(rel_nm[:], rel_s[:], axis=AX, negate=True),
                            deps=(t_rsc,), dbg=f"relnm{h}_{b}")
                t_rex = P.a(lambda a: a.activation(rel_e[:], rel_s[:], AF.Exp,
                                                   bias=rel_nm[0:1, 0:1]),
                            deps=(t_rnm,), dbg=f"relex{h}_{b}")
                t_rem = P.v(lambda v: v.tensor_mul(rel_e[:], rel_e[:], memmask[:]),
                            deps=(t_rex, t_mask), dbg=f"relem{h}_{b}")
                t_rds = P.v(lambda v: v.reduce_sum(rel_den[:], rel_e[:], axis=AX),
                            deps=(t_rem,), dbg=f"relds{h}_{b}")
                t_rdc = P.v(lambda v: v.tensor_scalar_max(rel_den[:], rel_den[:], 1e-30),
                            deps=(t_rds,), dbg=f"reldc{h}_{b}")
                t_rrd = P.v(lambda v: v.reciprocal(rel_rd[:], rel_den[:]),
                            deps=(t_rdc,), dbg=f"relrd{h}_{b}")
                t_rw = P.v(lambda v: v.tensor_single_scalar(out=rel_w[:], in_=rel_e[:],
                                                            scalar=rel_rd[0:1, 0:1], op=ALU.mult),
                           deps=(t_rrd,), dbg=f"relw{h}_{b}")
                t_wg = P.v((lambda h=h: (lambda v: v.tensor_single_scalar(
                    out=wg_row[:], in_=rel_w[:], scalar=gate_g[0:1, h:h + 1], op=ALU.mult)))(),
                    deps=(t_rw, t_gg), dbg=f"wgrow{h}_{b}")
                t_wgbf = P.a(lambda a: a.copy(wg_bf[:], wg_row[:]),
                             deps=(t_wg, prep_chain), dbg=f"wgbf{h}_{b}")
                t_ogbf = P.a((lambda h=h: (lambda a: a.copy(omg_bf[:], gate_omg[0:1, h:h + 1])))(),
                             deps=(t_gomg, prep_chain), dbg=f"ogbf{h}_{b}")
                t_wgmm = P.pe(lambda t: t.matmul(PS[5][:, 5:9], ones_bf[:], wg_bf[:],
                                                 start=True, stop=True),
                              deps=(t_wgbf, tok_ones_bf, t_rsc), dbg=f"wgmm{h}_{b}")
                t_wgbev = P.v(lambda v: v.tensor_copy(wgb_sb[:], PS[5][:, 5:9]),
                              deps=(t_wgmm,), dbg=f"wgbev{h}_{b}")
                t_ogmm = P.pe(lambda t: t.matmul(PS[5][:, 9:10], ones_bf[:], omg_bf[:],
                                                 start=True, stop=True),
                              deps=(t_ogbf, t_wgbev), dbg=f"ogmm{h}_{b}")
                t_ogbev = P.v(lambda v: v.tensor_copy(omgb_sb[:], PS[5][:, 9:10]),
                              deps=(t_ogmm,), dbg=f"ogbev{h}_{b}")
                bank5_free = t_ogbev
                if p2_stage < 3:
                    prep_chain = t_ogbev
                    continue

                tok_q = [None] * 4
                tok_k = [None] * 4
                tok_v = [None] * 4
                gi = 0
                for nm in "qkv":
                    for tt in range(4):
                        bank = PS[gi % 2]
                        deps_p = [tok_wT[hp], prev_pe_last, bank01_free[gi % 2]] + list(tr_half[b])
                        if h == 0 and b == 0:
                            deps_p.append(t_mns)
                        t_mm = P.pe((lambda nm=nm, hp=hp, b=b, tt=tt, bank=bank: (lambda t: [
                            t.matmul(bank[:, :],
                                     wT[nm][hp][:, c * 128:(c + 1) * 128],
                                     xn_sb[:, c * T + b * S + tt * 512: c * T + b * S + (tt + 1) * 512],
                                     start=(c == 0), stop=(c == NCH - 1))
                            for c in range(NCH)][-1]))(),
                            deps=tuple(d for d in deps_p if d is not None),
                            dbg=f"proj{nm}{h}_{b}_{tt}")
                        xn_last_reader = t_mm
                        if nm in "qk":
                            dst = qp_sb if nm == "q" else kp_sb
                            t_1 = P.a((lambda bank=bank: (lambda a: a.activation(
                                elu_t1[:], bank[:, :], AF.Relu, scale=-1.0)))(),
                                deps=(t_mm,), dbg=f"elu1{nm}{h}_{b}_{tt}")
                            t_2 = P.a(lambda a: a.activation(elu_t2[:], elu_t1[:], AF.Exp, scale=-1.0),
                                      deps=(t_1, last_eluadd), dbg=f"elu2{nm}{h}_{b}_{tt}")
                            t_3 = P.a((lambda bank=bank: (lambda a: a.activation(
                                elu_t3[:], bank[:, :], AF.Relu)))(),
                                deps=(t_2,), dbg=f"elu3{nm}{h}_{b}_{tt}")
                            t_add = P.v((lambda dst=dst, tt=tt: (lambda v: v.tensor_add(
                                dst[:, tt * 512:(tt + 1) * 512], elu_t2[:], elu_t3[:])))(),
                                deps=(t_3,), dbg=f"eluadd{nm}{h}_{b}_{tt}")
                            bank01_free[gi % 2] = t_3
                            last_eluadd = t_add
                            (tok_q if nm == "q" else tok_k)[tt] = t_add
                        else:
                            t_cp = P.a((lambda bank=bank, tt=tt: (lambda a: a.copy(
                                vfm_sb[:, tt * 512:(tt + 1) * 512], bank[:, :])))(),
                                deps=(t_mm,), dbg=f"vcp{h}_{b}_{tt}")
                            bank01_free[gi % 2] = t_cp
                            tok_v[tt] = t_cp
                        gi += 1

                if p2_stage < 4:
                    prev_pe_last = t_mm
                    wT_free[hp] = t_mm
                    continue
                tok_kpT = [None] * NCHK
                tok_vc = [None] * NCHK
                for n in range(NCHK):
                    sl = n % 2
                    st_ps = PS[sl].bitcast(BF16)[:, 896:1024]
                    t_T = P.pe((lambda n=n, st_ps=st_ps: (lambda t: t.transpose(
                        st_ps, kp_sb[:, n * 128:(n + 1) * 128], ident_bf[:])))(),
                        deps=(tok_k[n // 4], bank01_free[sl], tok_id_bf), dbg=f"kpT{h}_{b}_{n}")
                    t_ev = P.a((lambda n=n, st_ps=st_ps: (lambda a: a.copy(
                        kpT_sb[:, n * 128:(n + 1) * 128], st_ps)))(),
                        deps=(t_T, prev_pe_last), dbg=f"kpTev{h}_{b}_{n}")
                    bank01_free[sl] = t_ev
                    tok_kpT[n] = t_ev
                for n in range(NCHK):
                    sl = n % 2
                    st_ps = PS[sl].bitcast(BF16)[:, 896:1024]
                    t_T = P.pe((lambda n=n, st_ps=st_ps: (lambda t: t.transpose(
                        st_ps, vfm_sb[:, n * 128:(n + 1) * 128], ident_bf[:])))(),
                        deps=(tok_v[n // 4], bank01_free[sl], tok_id_bf), dbg=f"vT{h}_{b}_{n}")
                    t_ev = P.a((lambda n=n, st_ps=st_ps: (lambda a: a.copy(
                        vc_sb[:, n * 129:n * 129 + 128], st_ps)))(),
                        deps=(t_T, prev_pe_last), dbg=f"vTev{h}_{b}_{n}")
                    bank01_free[sl] = t_ev
                    tok_vc[n] = t_ev

                if p2_stage < 5:
                    prev_pe_last = t_T
                    wT_free[hp] = t_T
                    continue
                nm_toks = []
                for n in range(NCHK):
                    nm_toks.append(P.pe((lambda n=n, hp=hp: (lambda t: t.matmul(
                        PS[3][:, n * NMEM:(n + 1) * NMEM],
                        qp_sb[:, n * 128:(n + 1) * 128], mn_t[hp][:],
                        start=True, stop=True)))(),
                        deps=(tok_q[n // 4], tok_mnt[hp], bank23_free[1]),
                        dbg=f"nmb{h}_{b}_{n}"))
                t_ncl = P.v(lambda v: v.tensor_scalar_max(
                    cm64[:], PS[3][:, 0:NCHK * NMEM], EPS_ATT),
                    deps=tuple(nm_toks), dbg=f"nclipb{h}_{b}")
                bank23_free[1] = t_ncl
                t_cm = P.v(lambda v: v.reciprocal(cm64[:], cm64[:]),
                           deps=(t_ncl,), dbg=f"rnormb{h}_{b}")
                cm64v = cm64.rearrange("p (n m) -> p n m", m=NMEM)
                for m in range(NMEM):
                    t_cm = P.v((lambda m=m: (lambda v: v.tensor_single_scalar(
                        out=cm64v[:, :, m:m + 1], in_=cm64v[:, :, m:m + 1],
                        scalar=wgb_sb[:, m:m + 1], op=ALU.mult)))(),
                        deps=(t_cm, t_wgbev), dbg=f"cmb{h}_{b}_{m}")
                tok_cm64 = t_cm

                t_Rz = P.v(lambda v: v.memset(R_f[0][:], 0.0),
                           deps=(prev_pe_last,) if prev_pe_last else (), dbg=f"Rz{h}_{b}")
                t_Rbf = P.a(lambda a: a.copy(R_bf[0][:], R_f[0][:]),
                            deps=(t_Rz,) + ((prev_pe_last,) if prev_pe_last else ()),
                            dbg=f"Rbf{h}_{b}")
                tok_Rf = t_Rz
                tok_Rbf = t_Rbf

                last_ho_cast = None
                tok_ho = [None] * NCHK
                tail = [None] * NCHK
                tail_prev = None

                def rec_tail(n, h=h, b=b, hbp=hbp):
                    nonlocal last_ho_cast, tail_prev
                    t_nd, t_am, t_Ru = tail[n]
                    ndb = PS[6 + n % 2]
                    amb = PS[2 + n % 2]
                    t_dc = P.v((lambda ndb=ndb: (lambda v: v.tensor_scalar_max(
                        dclip[:], ndb[:, HDIM:HDIM + 1], EPS_ATT)))(),
                        deps=(t_nd, tail_prev), dbg=f"dclip{h}_{b}_{n}")
                    t_rd2 = P.v(lambda v: v.reciprocal(rden_sb[:], dclip[:]), deps=(t_dc,),
                                dbg=f"rden{h}_{b}_{n}")
                    t_rdo = P.v(lambda v: v.tensor_mul(rdeno[:], rden_sb[:], omgb_sb[:]),
                                deps=(t_rd2, t_ogbev), dbg=f"rdeno{h}_{b}_{n}")
                    t_mc = P.v((lambda n=n, amb=amb: (lambda v: v.tensor_single_scalar(
                        out=macc[0][:], in_=amb[:, 0:128],
                        scalar=cm64[:, n * NMEM:n * NMEM + 1], op=ALU.mult)))(),
                        deps=(t_am, tok_cm64, tail_prev), dbg=f"mc0_{h}_{b}_{n}")
                    for m in range(1, NMEM):
                        t_mc = P.v((lambda m=m, n=n, amb=amb: (lambda v: v.scalar_tensor_tensor(
                            out=macc[m % 2][:], in0=amb[:, m * 128:(m + 1) * 128],
                            scalar=cm64[:, n * NMEM + m:n * NMEM + m + 1], in1=macc[(m - 1) % 2][:],
                            op0=ALU.mult, op1=ALU.add)))(),
                            deps=(t_mc,), dbg=f"mc{m}_{h}_{b}_{n}")
                    bank23_free[n % 2] = t_mc
                    t_ho = P.v((lambda ndb=ndb: (lambda v: v.scalar_tensor_tensor(
                        out=ho_f[:], in0=ndb[:, 0:HDIM], scalar=rdeno[:, 0:1],
                        in1=macc[(NMEM - 1) % 2][:], op0=ALU.mult, op1=ALU.add)))(),
                        deps=(t_mc, t_rdo, last_ho_cast), dbg=f"ho{h}_{b}_{n}")
                    bank67_free[n % 2] = t_ho
                    t_hc = P.a((lambda n=n, hbp=hbp: (lambda a: a.copy(
                        hb_sb[hbp][:, n * 128:(n + 1) * 128], ho_f[:])))(),
                        deps=(t_ho, hb_free[hbp], last_ho_cast), dbg=f"hocast{h}_{b}_{n}")
                    last_ho_cast = t_hc
                    tail_prev = t_ho
                    tok_ho[n] = t_hc

                for n in range(NCHK):
                    rp = n % 2
                    t_kv = P.pe((lambda n=n: (lambda t: t.matmul(
                        PS[4][:, 0:HDIM + 1], kpT_sb[:, n * 128:(n + 1) * 128],
                        vc_sb[:, n * 129:(n + 1) * 129], start=True, stop=True)))(),
                        deps=(tok_kpT[n], tok_vc[n], bank4_free, tok_vones),
                        dbg=f"kv{h}_{b}_{n}")
                    t_sc = P.pe((lambda n=n: (lambda t: t.matmul(
                        PS[5][:, 0:128], kp_sb[:, n * 128:(n + 1) * 128],
                        qp_sb[:, n * 128:(n + 1) * 128], start=True, stop=True)))(),
                        deps=(tok_k[n // 4], tok_q[n // 4], bank5_free), dbg=f"sc{h}_{b}_{n}")
                    t_sT = P.v((lambda: (lambda v: v.tensor_mul(sT_sb[:], PS[5][:, 0:128], tri_bf[:])))(),
                               deps=(t_sc, tok_tri), dbg=f"sT{h}_{b}_{n}")
                    bank5_free = t_sT
                    t_Ru = P.v((lambda rp=rp: (lambda v: v.tensor_add(
                        R_f[1 - rp][:], R_f[rp][:], PS[4][:, 0:HDIM + 1])))(),
                        deps=(t_kv, tok_Rf), dbg=f"Ru{h}_{b}_{n}")
                    bank4_free = t_Ru
                    tok_Rf = t_Ru
                    t_nd = P.pe((lambda n=n, rp=rp, tok_Rbf=tok_Rbf: (lambda t: (
                        t.matmul(PS[6 + n % 2][:, 0:HDIM + 1], sT_sb[:],
                                 vc_sb[:, n * 129:(n + 1) * 129], start=True, stop=False),
                        t.matmul(PS[6 + n % 2][:, 0:HDIM + 1], qp_sb[:, n * 128:(n + 1) * 128],
                                 R_bf[rp][:], start=False, stop=True))[-1]))(),
                        deps=(t_sT, tok_Rbf, bank67_free[n % 2]), dbg=f"nd{h}_{b}_{n}")
                    t_am = P.pe((lambda n=n, hp=hp: (lambda t: t.matmul(
                        PS[2 + n % 2][:, :], qp_sb[:, n * 128:(n + 1) * 128], mem_rhs[hp][:],
                        start=True, stop=True)))(),
                        deps=(tok_q[n // 4], tok_mem[hp], bank23_free[n % 2]), dbg=f"am{h}_{b}_{n}")
                    pe_chunk_last = t_am
                    if n < NCHK - 1:
                        tok_Rbf = P.a((lambda rp=rp: (lambda a: a.copy(
                            R_bf[1 - rp][:], R_f[1 - rp][:])))(),
                            deps=(t_Ru, t_nd), dbg=f"Rbfc{h}_{b}_{n}")
                    tail[n] = (t_nd, t_am, t_Ru)
                    if n >= 1:
                        rec_tail(n - 1)
                rec_tail(NCHK - 1)
                prep_chain = tail_prev
                prev_pe_last = pe_chunk_last
                wT_free[hp] = pe_chunk_last
                mem_free[hp] = pe_chunk_last
                hb_free[hbp] = P.dma("POOL", (lambda h=h, b=b, hbp=hbp: (lambda g: g.dma_start(
                    out=attn_tm[b * S:(b + 1) * S, h * 128:(h + 1) * 128].rearrange(
                        "(n c) e -> c n e", c=128),
                    in_=hb_sb[hbp].rearrange("c (n e) -> c n e", e=128))))(),
                    deps=tuple(tok_ho), chan=f"hbst{hbp}", dbg=f"hbst{h}_{b}")
                tok_hb_store.append(hb_free[hbp])

    tok_y_store = []
    if 3 in phases:
        woT_sb = OVERLAY("woT_sb", [128, NCH * D], BF16, xn_sb)
        afm_sb = [OVERLAY(f"afm_sb{i}", [128, NCH * 512], BF16, xn_sb, 65536 + i * 16384)
                  for i in range(2)]
        x_res = [OVERLAY(f"x_res{i}", [128, D], F32, xn_sb, 98304 + i * 8192)
                 for i in range(2)]
        y_stage = [OVERLAY(f"y_stage{i}", [128, 512], F32, xn_sb, 114688 + i * 2048)
                   for i in range(2)]
        p3dep = (xn_last_reader,)

        tok_woT = None
        for c in range(NCH):
            for eb in range(NCH):
                tok_woT = P.dma("SP", (lambda c=c, eb=eb: (lambda sy: sy.dma_start(
                    out=woT_sb[:, c * D + eb * 128: c * D + (eb + 1) * 128],
                    in_=wo_bf[eb * 128:(eb + 1) * 128, c * 128:(c + 1) * 128],
                    transpose=True)))(),
                    deps=p3dep + (tok_wcast["o"],), chan="woT", dbg=f"woT{c}_{eb}")

        afm_free = [None, None]
        xres_free = [None, None]
        yps_free = [bank23_free[0], bank23_free[1]]
        yst_free = [None, None]
        tok_afm = [None, None]
        tok_xres = [None, None]
        for i5 in range(NT512):
            par = i5 % 2
            b = (i5 * 512) // S
            tok_afm[par] = None
            afm_toks = []
            for c in range(NCH):
                afm_toks.append(P.dma("SP", (lambda i5=i5, c=c, par=par: (lambda sy: sy.dma_start(
                    out=afm_sb[par][:, c * 512:(c + 1) * 512],
                    in_=attn_tm[i5 * 512:(i5 + 1) * 512, c * 128:(c + 1) * 128],
                    transpose=True)))(),
                    deps=(tok_hb_store[c * NB + b], afm_free[par]), chan=f"afm{par}",
                    dbg=f"afm{i5}_{c}"))
            tok_afm[par] = afm_toks[-1]
            for it in range(4):
                i = i5 * 4 + it
                xp = i % 2
                tok_xres[xp] = P.dma("ACT", (lambda i=i, xp=xp: (lambda a: a.dma_start(
                    out=x_res[xp][:], in_=x_in[i * 128:(i + 1) * 128, :])))(),
                    deps=(xres_free[xp],) + p3dep, chan=f"xres{xp}", dbg=f"xres{i}")
                for eb in range(4):
                    u = (i * 4 + eb) % 2
                    bank = PS[2 + u]
                    t_mm = P.pe((lambda it=it, eb=eb, par=par, bank=bank: (lambda t: [
                        t.matmul(bank[:, :],
                                 afm_sb[par][:, c * 512 + it * 128: c * 512 + it * 128 + 128],
                                 woT_sb[:, c * D + eb * 512:(c) * D + (eb + 1) * 512],
                                 start=(c == 0), stop=(c == NCH - 1))
                        for c in range(NCH)][-1]))(),
                        deps=(tok_afm[par], tok_woT, yps_free[u]), dbg=f"ymm{i}_{eb}")
                    afm_free[par] = t_mm
                    sp2 = (i * 4 + eb) % 2
                    t_ev = P.v((lambda eb=eb, xp=xp, sp2=sp2, bank=bank: (lambda v: v.tensor_add(
                        y_stage[sp2][:], bank[:, :], x_res[xp][:, eb * 512:(eb + 1) * 512])))(),
                        deps=(t_mm, tok_xres[xp], yst_free[sp2]), dbg=f"yev{i}_{eb}")
                    yps_free[u] = t_ev
                    xres_free[xp] = t_ev
                    yst_free[sp2] = P.dma("ACT", (lambda i=i, eb=eb, sp2=sp2: (lambda a: a.dma_start(
                        out=y_out[i * 128:(i + 1) * 128, eb * 512:(eb + 1) * 512],
                        in_=y_stage[sp2][:])))(),
                        deps=(t_ev,), chan=f"yst{sp2}", dbg=f"yst{i}_{eb}")
                    tok_y_store.append(yst_free[sp2])

    tok_y_final = []
    if 4 in phases:
        w1b = OVERLAY("w1b", [128, NCH * HID], BF16, xn_sb, 0)
        w3b = OVERLAY("w3b", [128, NCH * HID], BF16, xn_sb, NCH * HID * 2)
        xq_sb = OVERLAY("xq_sb", [128, NCH * 512], BF16, xn_sb, 2 * NCH * HID * 2)
        H_sb = OVERLAY("H_sb", [128, NJ * 512], BF16, wT["q"][0], 0)
        w2str = [OVERLAY(f"w2str{i}", [128, NJ * 512], BF16, x_tile[0], i * NJ * 512 * 2)
                 for i in range(2)]
        gb_sb = OVERLAY("gb_sb", [128, 512], F32, elu_t1, 0)
        a_sb = OVERLAY("a_sb", [128, 512], F32, elu_t2, 0)
        s_sb = OVERLAY("s_sb", [128, 512], F32, elu_t3, 0)
        yin_sb = [OVERLAY(f"yin_sb{i}", [128, 512], F32, vc_sb, i * 2048) for i in range(2)] + \
                 [OVERLAY(f"yin_sb{2 + i}", [128, 512], F32, wT["v"][0], i * 2048) for i in range(2)]
        yst4_sb = [OVERLAY(f"yst4_sb{i}", [128, 512], F32, hb_sb[1], i * 2048) for i in range(2)] + \
                  [OVERLAY(f"yst4_sb{2 + i}", [128, 512], F32, wT["v"][1], i * 2048) for i in range(2)]
        grow_sb = OVERLAY("grow_sb", [1, 512], F32, mem_rhs[0], 0)

        p4gate = tuple(d for d in (
            prev_pe_last, last_eluadd, tok_hb_store[-1], tok_hb_store[-2],
            afm_free[0], afm_free[1], xres_free[0], xres_free[1],
            yst_free[0], yst_free[1], yps_free[0], yps_free[1]) if d is not None)

        ytile = [[tok_y_store[i * 4 + eb] for eb in range(4)] for i in range(NT128)]

        u1free = [bank01_free[0], bank01_free[1]]
        u3free = [yps_free[0], yps_free[1]]
        ybfree = [bank4_free, bank5_free, yps_free[0], yps_free[1]]
        gb_free = bank67_free[0]
        yin_free = [None, None, None, None]
        yst4_free = [None, None, None, None]
        YB4 = [PS[4], PS[5], PS[2], PS[3]]
        w2s_free = [None, None]
        w2c_free = [None, None]
        u_last = None
        w2_last = None
        gbmm_last = None
        prev_s = None
        prev_a = None
        prev_H = None

        for e in range(NEXP + 1):
            routed = e < NEXP
            w1src = w1_in[e] if routed else sw1_in
            w3src = w3_in[e] if routed else sw3_in
            w2src = w2_in[e] if routed else sw2_in
            cast_dep = (u_last,) if u_last is not None else p4gate
            t_w1c = P.dma("ACT", (lambda e=e: (lambda a: a.dma_start(
                out=w1b.rearrange("p (c j) -> p c j", j=HID),
                in_=w1bf9[e].rearrange("(c p) j -> p c j", p=128))))(),
                deps=cast_dep + (tok_ffncast[(0, e)],), chan="w1b", dbg=f"w1c{e}")
            t_w3c = P.dma("ACT", (lambda e=e: (lambda a: a.dma_start(
                out=w3b.rearrange("p (c j) -> p c j", j=HID),
                in_=w3bf9[e].rearrange("(c p) j -> p c j", p=128))))(),
                deps=cast_dep + (tok_ffncast[(1, e)],), chan="w3b", dbg=f"w3c{e}")
            t_w2c = tok_ffncast[(2, e)]
            for t5 in range(NT512):
                t_xq = P.dma("SP", (lambda t5=t5: (lambda sy: sy.dma_start(
                    out=xq_sb.rearrange("p (c w) -> p c w", w=512),
                    in_=xn_fm[:, :, t5 * 512:(t5 + 1) * 512].rearrange("c p w -> p c w"))))(),
                    deps=((u_last,) if u_last is not None else p4gate) + (tok_fmst[-1],),
                    chan="xq", dbg=f"xq{e}_{t5}")
                if routed:
                    t_gr = P.dma("SP", (lambda e=e, t5=t5: (lambda sy: sy.dma_start(
                        out=grow_sb[:], in_=gT_dram[e:e + 1, t5 * 512:(t5 + 1) * 512])))(),
                        deps=((gbmm_last,) if gbmm_last is not None else p4gate) +
                             (tok_g_store[-1], tok_g_store[-2]),
                        chan="grow", dbg=f"grow{e}_{t5}")
                    t_gbmm = P.pe((lambda: (lambda t: t.matmul(
                        PS[6][:, :], ones_f[:], grow_sb[:], start=True, stop=True)))(),
                        deps=(t_gr, gb_free, tok_ones_f), dbg=f"gbmm{e}_{t5}")
                    gbmm_last = t_gbmm
                    t_gb = P.v(lambda v: v.tensor_copy(gb_sb[:], PS[6][:, :]),
                               deps=(t_gbmm, prev_a), dbg=f"gbev{e}_{t5}")
                    gb_free = t_gb
                else:
                    t_gb = None
                u_toks = []
                for jc in range(NJ):
                    b1 = PS[jc % 2]
                    b3 = PS[2 + jc % 2]
                    t_u1 = P.pe((lambda jc=jc, b1=b1: (lambda t: [
                        t.matmul(b1[:, :],
                                 w1b[:, c * HID + jc * 128: c * HID + (jc + 1) * 128],
                                 xq_sb[:, c * 512:(c + 1) * 512],
                                 start=(c == 0), stop=(c == NCH - 1))
                        for c in range(NCH)][-1]))(),
                        deps=(t_w1c, t_xq, u1free[jc % 2]), dbg=f"u1_{e}_{t5}_{jc}")
                    t_u3 = P.pe((lambda jc=jc, b3=b3: (lambda t: [
                        t.matmul(b3[:, :],
                                 w3b[:, c * HID + jc * 128: c * HID + (jc + 1) * 128],
                                 xq_sb[:, c * 512:(c + 1) * 512],
                                 start=(c == 0), stop=(c == NCH - 1))
                        for c in range(NCH)][-1]))(),
                        deps=(t_w3c, t_xq, u3free[jc % 2]), dbg=f"u3_{e}_{t5}_{jc}")
                    u_last = t_u3
                    if routed:
                        t_a = P.v((lambda b1=b1: (lambda v: v.tensor_mul(a_sb[:], b1[:, :], gb_sb[:])))(),
                                  deps=(t_u1, t_gb, prev_s), dbg=f"a{e}_{t5}_{jc}")
                        prev_a = t_a
                        t_s = P.a(lambda a: a.activation(s_sb[:], a_sb[:], AF.Silu),
                                  deps=(t_a, prev_H), dbg=f"s{e}_{t5}_{jc}")
                        u1free[jc % 2] = t_a
                    else:
                        t_s = P.a((lambda b1=b1: (lambda a: a.activation(s_sb[:], b1[:, :], AF.Silu)))(),
                                  deps=(t_u1, prev_H), dbg=f"s{e}_{t5}_{jc}")
                        u1free[jc % 2] = t_s
                    prev_s = t_s
                    t_H = P.v((lambda jc=jc, b3=b3: (lambda v: v.tensor_mul(
                        H_sb[:, jc * 512:(jc + 1) * 512], s_sb[:], b3[:, :])))(),
                        deps=(t_s, t_u3) + ((w2_last,) if jc == 0 and w2_last is not None else ()),
                        dbg=f"H{e}_{t5}_{jc}")
                    prev_H = t_H
                    u3free[jc % 2] = t_H
                    u_toks.append(t_H)
                for eb in range(4):
                    sp = eb % 2
                    t_w2s = P.dma("SP", (lambda e=e, eb=eb, sp=sp: (lambda sy: sy.dma_start(
                        out=w2str[sp].rearrange("p (c w) -> p c w", w=512),
                        in_=w2bf9[e][:, eb * 512:(eb + 1) * 512].rearrange("(c p) w -> p c w", p=128))))(),
                        deps=(t_w2c,) + ((w2s_free[sp],) if w2s_free[sp] is not None else p4gate),
                        chan=f"w2s{sp}", dbg=f"w2s{e}_{t5}_{eb}")
                    for ts in range(4):
                        i = t5 * 4 + ts
                        u = (eb * 4 + ts) % 4
                        ybank = YB4[u]
                        t_ymm = P.pe((lambda ts=ts, sp=sp, ybank=ybank: (lambda t: [
                            t.matmul(ybank[:, :],
                                     H_sb[:, jc2 * 512 + ts * 128: jc2 * 512 + ts * 128 + 128],
                                     w2str[sp][:, jc2 * 512:(jc2 + 1) * 512],
                                     start=(jc2 == 0), stop=(jc2 == NJ - 1))
                            for jc2 in range(NJ)][-1]))(),
                            deps=(u_toks[-1], u_toks[-2], t_w2s, ybfree[u]),
                            dbg=f"ymm{e}_{t5}_{eb}_{ts}")
                        w2_last = t_ymm
                        w2s_free[sp] = t_ymm
                        yp = (eb * 4 + ts) % 4
                        t_yin = P.dma("POOL", (lambda i=i, eb=eb, yp=yp: (lambda a: a.dma_start(
                            out=yin_sb[yp][:], in_=y_out[i * 128:(i + 1) * 128,
                                                        eb * 512:(eb + 1) * 512])))(),
                            deps=(ytile[i][eb],) + ((yin_free[yp],) if yin_free[yp] is not None else ()),
                            chan=f"yin{yp}", dbg=f"yin{e}_{i}_{eb}")
                        sc_ap = (lambda i=i, e=e: g_res[:, i * NEXP + e:i * NEXP + e + 1])() \
                            if routed else 1.0
                        t_yev = P.v((lambda ybank=ybank, yp=yp, sc_ap=sc_ap: (lambda v: v.scalar_tensor_tensor(
                            out=yst4_sb[yp][:], in0=ybank[:, :], scalar=sc_ap,
                            in1=yin_sb[yp][:], op0=ALU.mult, op1=ALU.add)))(),
                            deps=(t_ymm, t_yin) +
                                 ((yst4_free[yp],) if yst4_free[yp] is not None else ()) +
                                 ((tok_g[i],) if routed else ()),
                            dbg=f"yev{e}_{i}_{eb}")
                        ybfree[u] = t_yev
                        yin_free[yp] = t_yev
                        t_yst = P.dma("ACT", (lambda i=i, eb=eb, yp=yp: (lambda a: a.dma_start(
                            out=y_out[i * 128:(i + 1) * 128, eb * 512:(eb + 1) * 512],
                            in_=yst4_sb[yp][:])))(),
                            deps=(t_yev,), chan=f"yo{yp}", dbg=f"yst{e}_{i}_{eb}")
                        yst4_free[yp] = t_yst
                        ytile[i][eb] = t_yst
                u3free = [ybfree[2], ybfree[3]]
        for row in ytile:
            tok_y_final.extend(row)

    final_deps = [tok_fmst[c] for c in range(NCH)] + [tok_g_store[i] for i in range(NT128)]
    final_deps += list(tok_xsum_b) + [tok_tri] + tok_hb_store + tok_y_store + tok_y_final
    P.sp(lambda s: s.nop(), deps=tuple(final_deps), dbg="final")

    from contextlib import ExitStack
    with ExitStack() as stack:
        emit = P.finalize(nc, stack)
        with nc.Block() as block:
            @block.tensor
            def _(t):
                emit("PE", t)

            @block.vector
            def _(v):
                emit("DVE", v)

            @block.scalar
            def _(a):
                emit("ACT", a)

            @block.gpsimd
            def _(g):
                emit("POOL", g)

            @block.sync
            def _(s):
                emit("SP", s)
    return nc



def kernel(**inputs):
    ncores = 8
    x = np.ascontiguousarray(inputs["L0_attn_hidden_states"])
    nc = build_nc()
    in_maps = []
    for i in range(ncores):
        shard = np.ascontiguousarray(x[2 * i:2 * i + 2].reshape(T, D))
        in_maps.append({
            "x": shard,
            "w_q": inputs["L0_attn_w_q"], "w_k": inputs["L0_attn_w_k"],
            "w_v": inputs["L0_attn_w_v"], "w_o": inputs["L0_attn_w_o"],
            "memories": inputs["L0_attn_memories"],
            "memory_norms": inputs["L0_attn_memory_norms"],
            "gate": inputs["L0_attn_gate"].reshape(1, NHEAD),
            "gate_w": inputs["L0_ffn_gate_w"],
            "expert_bias": inputs["L0_ffn_expert_bias"].reshape(1, NEXP),
            "w1": inputs["L0_ffn_w1"], "w2": inputs["L0_ffn_w2"], "w3": inputs["L0_ffn_w3"],
            "sw1": inputs["L0_ffn_sw1"], "sw2": inputs["L0_ffn_sw2"], "sw3": inputs["L0_ffn_sw3"],
        })
    res = run_bass_kernel_spmd(nc, in_maps, core_ids=list(range(ncores)))
    out = np.concatenate([r["y"].reshape(2, S, D) for r in res.results], axis=0)
    return out
